# version 05
# baseline (speedup 1.0000x reference)
import math
import numpy as np
import ml_dtypes
import concourse.bass as bass
import concourse.mybir as mybir
from concourse.bass_utils import run_bass_kernel_spmd
from contextlib import ExitStack

F32 = mybir.dt.float32
BF16 = mybir.dt.bfloat16
AF = mybir.ActivationFunctionType
ALU = mybir.AluOpType
AXX = mybir.AxisListType.X

NCORES = 8
S = 8192
D = 4096
NH = 32
NKV = 8
HD = 128
R = 1024
IC = 32
QT = 8
HIDP = 11264
HC = 88
FC = 22
W2K = 2816
RPC = 2049
SCL = 1.0 / math.sqrt(HD)
LNC_BIAS = -0.5 * math.log(HD)
ENGINES = ("sync", "gpsimd", "tensor", "vector", "scalar")


class Sched:
    def __init__(self):
        self.ops = {e: [] for e in ENGINES}
        self.ctr = {}

    def val(self, sem):
        return self.ctr.get(sem, 0)

    def emit(self, eng, fn, waits=(), inc=None, inc_n=1):
        post = None
        if inc is not None:
            self.ctr[inc] = self.ctr.get(inc, 0) + inc_n
            post = self.ctr[inc]
        self.ops[eng].append((tuple(waits), fn, inc, inc_n))
        return post


class _Stop(Exception):
    pass


def build(dbg=False, upto=99):
    nc = bass.Bass("TRN2", target_bir_lowering=False)

    def din(name, shape, dt=F32):
        return nc.dram_tensor(name, shape, dt, kind="ExternalInput").ap()

    xT = din("xT", [D, R])
    wqT = din("wqT", [D, D])
    wkT = din("wkT", [D, NKV * HD])
    wvT = din("wvT", [D, NKV * HD])
    woT = din("woT", [D, D])
    w1T = din("w1T", [1024 + 128, HIDP])
    w3T = din("w3T", [1024 + 128, HIDP])
    w2T = din("w2T", [W2K + 128, D])
    a1c = din("a1c", [128, IC * 2])
    a3c = din("a3c", [128, IC * 2])
    a2c = din("a2c", [128, HC * 2])
    cosq = din("cosq", [128, R])
    sinq = din("sinq", [128, R])
    masks = din("masks", [128, 8 * 512], BF16)
    consts = din("consts", [128, 256], BF16)

    out = nc.dram_tensor("out", [D, R], F32, kind="ExternalOutput").ap()

    contrib = nc.dram_tensor("contrib", [RPC, 1024], BF16)
    gathered = nc.dram_tensor("gathered", [NCORES * RPC, 1024], BF16)
    x1d = nc.dram_tensor("x1d", [D, R], F32)

    dbg_t = {}
    if dbg:
        for nm, shp, dt in [
            ("dbg_ln1", [128, IC * 1024], BF16),
            ("dbg_q", [128, IC * 1024], BF16),
            ("dbg_contrib", [RPC, 1024], BF16),
            ("dbg_negB", [128, 8], F32),
            ("dbg_attnT", [128, IC * 1024], BF16),
            ("dbg_x1", [D, R], F32),
            ("dbg_ln2", [128, IC * 1024], BF16),
            ("dbg_gs", [128, FC * 1024], BF16),
        ]:
            dbg_t[nm] = nc.dram_tensor(nm, shp, dt, kind="ExternalOutput").ap()

    sch = Sched()
    with ExitStack() as ctx:
        def sb(name, shape, dt):
            return ctx.enter_context(nc.sbuf_tensor(name, shape, dt))

        def pst(name, shape, dt):
            return ctx.enter_context(nc.psum_tensor(name, shape, dt))

        poolA = sb("poolA", [128, 32768], BF16)
        poolB = sb("poolB", [128, 32768], BF16)
        wbufA = sb("wbufA", [128, 4096], BF16)
        wbufB = sb("wbufB", [128, 4096], BF16)
        masks_s = sb("masks_s", [128, 4096], BF16)
        consts_s = sb("consts_s", [128, 256], BF16)
        scr8 = sb("scr8", [128, 2048], F32)
        fscr = sb("fscr", [128, 1024], F32)
        bscr = sb("bscr", [128, 4096], BF16)
        upads = sb("upads", [128, 3072], BF16)
        kbuf = [sb(f"kbuf{i}", [128, 1024], BF16) for i in range(2)]
        vbuf = [sb(f"vbuf{i}", [128, 8 * 136], BF16) for i in range(2)]
        ptb = [sb(f"ptb{i}", [128, 1024], BF16) for i in range(3)]
        m_s = sb("m_s", [128, 512], F32)
        r_s = sb("r_s", [128, 512], F32)
        mb16 = sb("mb16", [128, 512], BF16)
        a1s = sb("a1s", [128, IC * 2], BF16)
        a3s = sb("a3s", [128, IC * 2], BF16)
        a2s = sb("a2s", [128, HC * 2], BF16)
        qmaxw = sb("qmaxw", [128, 512], F32)
        kmaxw = sb("kmaxw", [128, 512], F32)
        qmax = sb("qmax", [128, 1], F32)
        kmax = sb("kmax", [128, 1], F32)
        kred = sb("kred", [128, 1], F32)
        kmb = sb("kmb", [128, 1], F32)
        negB = sb("negB", [128, 1], F32)
        lnc_t = sb("lnc_t", [128, 1], F32)
        rl = sb("rl", [128, 4], F32)
        stat_s = sb("stat_s", [1, 16], BF16)
        stat2 = sb("stat2", [1, 4], F32)
        epst = sb("epst", [128, 1], F32)
        statg = sb("statg", [1, 64], BF16)
        kst = sb("kst", [128, 1024], BF16)
        statf = sb("statf", [1, 64], F32)

        stp0 = pst("stp0", [128, 1024], F32)
        stp1 = pst("stp1", [128, 1024], F32)
        ob01 = pst("ob01", [128, 1024], F32)
        ob23 = pst("ob23", [128, 1024], F32)
        pb = [stp0[:, 0:512], stp0[:, 512:1024],
              ob01[:, 0:512], ob01[:, 512:1024],
              ob23[:, 0:512], ob23[:, 512:1024]]
        stq = [stp0[:], stp1[:], ob23[:]]
        nrm = stp1[:, 0:512]
        bslot = stp1[:, 512:1024]

        eye = consts_s[:, 0:128]
        ones = consts_s[:, 128:256]
        onesrow = consts_s[0:1, 128:256]

        Ar = poolA[:].rearrange("p (c t) -> p c t", c=IC)
        Br = poolB[:].rearrange("p (c t) -> p c t", c=IC)

        E = sch.emit

        def dma(eng, sem_, o, i, waits=()):
            return E(eng, (lambda e, o=o, i=i: e.dma_start(out=o, in_=i)),
                     waits=waits, inc=sem_, inc_n=16)

        def _sched_all():
            xT_r = xT.rearrange("(c p) t -> p c t", c=IC)
            for q4 in range(4):
                dma("gpsimd", f"XB{q4}", Ar[:, 8 * q4:8 * q4 + 8, :],
                    xT_r[:, 8 * q4:8 * q4 + 8, :])
            dma("gpsimd", "W0", consts_s[:], consts)
            dma("gpsimd", "W0", masks_s[:], masks)
            dma("gpsimd", "W0", a1s[:], a1c)
            dma("gpsimd", "W0", a3s[:], a3c)
            dma("gpsimd", "W0", a2s[:], a2c)
            W0F = sch.val("W0")
            dma("sync", "K0", scr8[:, 0:1024], cosq)
            dma("sync", "K0", scr8[:, 1024:2048], sinq)
            K0F = sch.val("K0")
            cos = scr8[:, 0:1024]
            sin = scr8[:, 1024:2048]

            E("vector", lambda e: e.memset(epst[:], 1e-5))
            E("vector", lambda e: e.memset(lnc_t[:], LNC_BIAS))
            E("vector", lambda e: e.memset(qmaxw[:], 0.0))
            E("vector", lambda e: e.memset(kmaxw[:], 0.0))
            E("vector", lambda e: e.memset(upads[:], 0.0))
            E("vector", lambda e: e.memset(stat_s[:], 0.0))
            E("vector", lambda e: e.memset(vbuf[0][:], 1.0))
            v_memset = E("vector", lambda e: e.memset(vbuf[1][:], 1.0),
                         inc="VE")

            xsq = [bscr[:, 0:512], bscr[:, 512:1024]]
            for tt in range(2):
                for ic in range(IC):
                    w = [(f"XB{ic // 8}", 16)]
                    if ic == 0 and tt == 0:
                        w.append(("W0", W0F))
                    E("tensor", (lambda e, ic=ic, tt=tt: e.matmul(
                        pb[tt][:], ones, Ar[:, ic, 512 * tt:512 * tt + 512],
                        start=(ic == 0), stop=(ic == IC - 1),
                        skip_group_check=True)), waits=w)
            sq_pe = []
            for tt in range(2):
                for ic in range(IC):
                    g = tt * IC + ic
                    par = g % 2
                    w = [(f"XB{ic // 8}", 16)]
                    if g >= 2:
                        w.append(("PE", sq_pe[g - 2]))
                    vv = E("vector", (lambda e, ic=ic, tt=tt, par=par:
                                      e.tensor_tensor(
                                          xsq[par],
                                          Ar[:, ic, 512 * tt:512 * tt + 512],
                                          Ar[:, ic, 512 * tt:512 * tt + 512],
                                          ALU.mult)), waits=w, inc="VE")
                    pv = E("tensor", (lambda e, ic=ic, tt=tt, par=par: e.matmul(
                        pb[2 + tt][:], ones, xsq[par],
                        start=(ic == 0), stop=(ic == IC - 1),
                        skip_group_check=True)), waits=[("VE", vv)], inc="PE")
                    sq_pe.append(pv)
            v_stats_pe = sq_pe[-1]

            def ln_finalize_and_apply(tt, bank_m, bank_sq, src_r, dst_r,
                                      pe_wait):
                E("vector", (lambda e: e.tensor_scalar(
                    m_s[:], pb[bank_m][:], 1.0 / D, None, ALU.mult)),
                    waits=[("PE", pe_wait)])
                E("vector", (lambda e: e.tensor_scalar(
                    fscr[:, 0:512], pb[bank_sq][:], 1.0 / D, None, ALU.mult)))
                E("vector", (lambda e: e.tensor_tensor(
                    r_s[:], m_s[:], m_s[:], ALU.mult)))
                E("vector", (lambda e: e.tensor_tensor(
                    fscr[:, 0:512], fscr[:, 0:512], r_s[:], ALU.subtract)))
                vw = E("vector", lambda e: e.tensor_copy(mb16[:], m_s[:]),
                       inc="VE")
                E("scalar", (lambda e: e.activation(
                    fscr[:, 512:1024], fscr[:, 0:512], AF.Ln,
                    bias=epst[:, 0:1])), waits=[("VE", vw)])
                sw = E("scalar", (lambda e: e.activation(
                    r_s[:], fscr[:, 512:1024], AF.Exp, scale=-0.5)), inc="SE")
                last = 0
                chunk_incs = []
                for ic in range(IC):
                    E("vector", (lambda e, ic=ic: e.tensor_tensor(
                        fscr[:, 0:512], src_r[:, ic, 512 * tt:512 * tt + 512],
                        mb16[:], ALU.subtract)), waits=[("SE", sw)])
                    last = E("vector", (lambda e, ic=ic: e.tensor_tensor(
                        dst_r[:, ic, 512 * tt:512 * tt + 512], fscr[:, 0:512],
                        r_s[:], ALU.mult)), inc="VE")
                    chunk_incs.append(last)
                return last, chunk_incs

            v_ln1 = 0
            for tt in range(2):
                v_ln1, _ = ln_finalize_and_apply(tt, tt, 2 + tt, Ar, Br,
                                                 v_stats_pe)

            if dbg:
                dma("sync", "DBG", dbg_t["dbg_ln1"], poolB[:],
                    waits=[("VE", v_ln1)])
            if upto < 2:
                raise _Stop

            wsem = ["WA", "WB"]
            wkT_r = wkT.rearrange("(c p) d -> p c d", c=IC)
            wvT_r = wvT.rearrange("(c p) d -> p c d", c=IC)
            wb = [wbufA, wbufB]
            wbr = [w[:].rearrange("p (c o) -> p c o", c=IC) for w in wb]
            rope_ve = []
            norm_pe = []
            norm_ve = []
            wpar_last_use = {0: 0, 1: 0}

            def proj_rope_block(tt, par, dst_ap, v_w, wpar, is_q,
                                extra_mm_waits=()):
                gidx = len(rope_ve)
                w = [(wsem[wpar], v_w), ("K0", K0F), ("VE", v_ln1)]
                w += list(extra_mm_waits)
                if gidx >= 2:
                    w.append(("VE", rope_ve[gidx - 2]))
                for ic in range(IC):
                    pk = E("tensor", (lambda e, ic=ic, tt=tt, par=par,
                                      wpar=wpar: e.matmul(
                                          pb[4 + par][:], wbr[wpar][:, ic, :],
                                          Br[:, ic, 512 * tt:512 * tt + 512],
                                          start=(ic == 0), stop=(ic == IC - 1),
                                          skip_group_check=True)),
                           waits=w if ic == 0 else (),
                           inc="PE" if ic == IC - 1 else None)
                sqw = [("PE", pk)]
                if norm_pe:
                    sqw.append(("PE", norm_pe[-1]))
                se = E("scalar", (lambda e, par=par: e.activation(
                    bscr[:, 0:512], pb[4 + par][:], AF.Square)),
                    waits=sqw, inc="SE")
                nmw = [("SE", se)]
                if norm_ve:
                    nmw.append(("VE", norm_ve[-1]))
                pn = E("tensor", (lambda e: e.matmul(
                    nrm, ones, bscr[:, 0:512], start=True, stop=True,
                    skip_group_check=True)), waits=nmw, inc="PE")
                norm_pe.append(pn)
                tgtw = qmaxw if is_q else kmaxw
                nv = E("vector", (lambda e, tgtw=tgtw: e.tensor_tensor(
                    tgtw[:], tgtw[:], nrm, ALU.max)),
                    waits=[("PE", pn)], inc="VE")
                norm_ve.append(nv)
                E("vector", (lambda e, par=par, tt=tt: e.tensor_tensor(
                    fscr[:, 0:512], pb[4 + par][:],
                    cos[:, 512 * tt:512 * tt + 512], ALU.mult)),
                    waits=[("SE", se)])
                E("vector", (lambda e, par=par, tt=tt: e.tensor_tensor(
                    fscr[0:64, 512:1024], pb[4 + par][64:128, :],
                    sin[0:64, 512 * tt:512 * tt + 512], ALU.mult)))
                E("vector", (lambda e, par=par, tt=tt: e.tensor_tensor(
                    fscr[64:128, 512:1024], pb[4 + par][0:64, :],
                    sin[64:128, 512 * tt:512 * tt + 512], ALU.mult)))
                rv = E("vector", (lambda e, dst_ap=dst_ap: e.tensor_tensor(
                    dst_ap, fscr[:, 0:512], fscr[:, 512:1024], ALU.add)),
                    inc="VE")
                rope_ve.append(rv)
                return rv

            kwv = {}
            for kc in range(2):
                kwv[kc] = dma("gpsimd", wsem[kc % 2], wbr[kc % 2],
                              wkT_r[:, :, 128 * kc:128 * kc + 128])
            wst = [poolA[:, 0:16384].rearrange("p (c d) -> p c d", d=512),
                   poolA[:, 16384:32768].rearrange("p (c d) -> p c d", d=512)]
            v_wv = [dma("gpsimd", f"WV{vh}", wst[vh],
                        wvT_r[:, :, 512 * vh:512 * vh + 512],
                        waits=[("VE", v_ln1)]) for vh in range(2)]
            kst_dma = 0
            for kc in range(NKV):
                wpar = kc % 2
                if kc in kwv:
                    v_w = kwv[kc]
                else:
                    v_w = dma("gpsimd", wsem[wpar], wbr[wpar],
                              wkT_r[:, :, 128 * kc:128 * kc + 128],
                              waits=[("PE", wpar_last_use[wpar])])
                for tt in range(2):
                    extra = [("WC", kst_dma)] if kst_dma else []
                    rv = proj_rope_block(tt, tt,
                                         kst[:, 512 * tt:512 * tt + 512],
                                         v_w, wpar, is_q=False,
                                         extra_mm_waits=extra)
                wpar_last_use[wpar] = sch.val("PE")
                kst_dma = dma("gpsimd", "WC",
                              contrib[128 * kc:128 * kc + 128, :],
                              kst[:], waits=[("VE", rv)])

            vkr = E("vector", (lambda e: e.reduce_max(
                kmax[:], kmaxw[:], axis=AXX)), inc="VE")
            se_ks = E("scalar", (lambda e: e.activation(
                stat_s[0:1, 0:1], kmax[0:1, :], AF.Copy)),
                waits=[("VE", vkr)], inc="SE")
            dma("gpsimd", "WC", contrib[2048:2049, 0:16], stat_s[:],
                waits=[("SE", se_ks)])

            vstage = [ptb[1][:, 0:512], ptb[1][:, 512:1024]]
            pe_v = {}
            vev = {}
            v_contribs = []
            for vh in range(2):
                for ts in range(8):
                    g = vh * 8 + ts
                    par = g % 2
                    w = [(f"WV{vh}", v_wv[vh]), ("VE", v_ln1)]
                    if g >= 2:
                        w.append(("VE", vev[g - 2]))
                    for ic in range(IC):
                        pv_ = E("tensor", (lambda e, ic=ic, ts=ts, par=par,
                                           vh=vh: e.matmul(
                                               pb[par][:],
                                               Br[:, ic,
                                                  128 * ts:128 * ts + 128],
                                               wst[vh][:, ic, :],
                                               start=(ic == 0),
                                               stop=(ic == IC - 1),
                                               skip_group_check=True)),
                                waits=w if ic == 0 else (),
                                inc="PE" if ic == IC - 1 else None)
                    pe_v[(vh, ts)] = pv_
                    evw = [("PE", pv_)]
                    if g >= 2:
                        evw.append(("WC", v_contribs[g - 2]))
                    vev[g] = E("vector", (lambda e, par=par, g=g:
                                          e.tensor_copy(
                                              vstage[g % 2], pb[par][:])),
                               waits=evw, inc="VE")
                    v_contribs.append(dma(
                        "gpsimd", "WC",
                        contrib[1024 + 128 * ts:1024 + 128 * ts + 128,
                                512 * vh:512 * vh + 512],
                        vstage[g % 2], waits=[("VE", vev[g])]))

            if dbg:
                dma("sync", "DBG", dbg_t["dbg_contrib"], contrib[:, :],
                    waits=[("WC", sch.val("WC"))])
            if upto < 4:
                raise _Stop

            E("gpsimd", (lambda e: e.collective_compute(
                "AllGather", ALU.bypass, replica_groups=[list(range(NCORES))],
                ins=[contrib.ap().opt()], outs=[gathered.ap().opt()])),
                waits=[("WC", sch.val("WC"))], inc="CC")

            wqT_r = wqT.rearrange("(c p) o -> p c o", c=IC)
            for oc in range(NH):
                wpar = oc % 2
                v_w = dma("gpsimd", wsem[wpar], wbr[wpar],
                          wqT_r[:, :, 128 * oc:128 * oc + 128],
                          waits=[("PE", wpar_last_use[wpar])])
                for tt in range(2):
                    rv = proj_rope_block(tt, tt,
                                         Ar[:, oc, 512 * tt:512 * tt + 512],
                                         v_w, wpar, is_q=True)
                wpar_last_use[wpar] = sch.val("PE")
            v_q = rope_ve[-1]

            if dbg:
                dma("sync", "DBG", dbg_t["dbg_q"], poolA[:],
                    waits=[("VE", v_q)])

            vqr = E("vector", (lambda e: e.reduce_max(
                qmax[:], qmaxw[:], axis=AXX)), inc="VE")
            v_stg = dma("sync", "SG",
                        statg[:].rearrange("p (r h) -> p r h", r=8),
                        gathered.ap().rearrange(
                            "(r q) j -> r q j",
                            r=NCORES)[:, 2048, 0:8].unsqueeze(0),
                        waits=[("CC", 1)])
            E("vector", (lambda e: e.tensor_copy(statf[:], statg[:])),
                waits=[("SG", v_stg)])
            v_kg0 = E("vector", (lambda e: e.reduce_max(
                stat2[:, 0:1], statf[:], axis=AXX)), inc="VE")
            se_kg = E("scalar", (lambda e: e.activation(
                stat_s[0:1, 1:2], stat2[0:1, 0:1], AF.Copy)),
                waits=[("VE", v_kg0)], inc="SE")
            pbc = E("tensor", (lambda e: e.matmul(
                bslot[:, 0:1], onesrow, stat_s[:, 1:2], start=True, stop=True,
                skip_group_check=True)),
                waits=[("SE", se_kg), ("VE", norm_ve[-1])], inc="PE")
            vkmb = E("vector", (lambda e: e.tensor_copy(kmb[:],
                                                         bslot[:, 0:1])),
                     waits=[("PE", pbc)], inc="VE")
            po1 = E("gpsimd", (lambda e: e.tensor_tensor(
                negB[:], qmax[:], kmb[:], ALU.mult)),
                waits=[("VE", max(vqr, vkmb))], inc="PO")
            E("scalar", (lambda e: e.activation(
                kred[:], negB[:], AF.Ln)), waits=[("PO", po1)])
            v_b2 = E("scalar", (lambda e: e.activation(
                negB[:], kred[:], AF.Exp, scale=0.5, bias=lnc_t[:, 0:1])),
                inc="SE")
            v_negB = E("vector", (lambda e: e.tensor_scalar(
                negB[:], negB[:], 60.0, -1.0, ALU.min, ALU.mult)),
                waits=[("SE", v_b2)], inc="VE")
            if dbg:
                dbgst = sb("dbgst", [128, 8], F32)
                E("vector", lambda e: e.tensor_copy(dbgst[:, 0:1], negB[:]),
                  waits=[("VE", v_negB)])
                E("vector", lambda e: e.tensor_copy(dbgst[:, 1:2], qmax[:]))
                E("vector", lambda e: e.tensor_copy(dbgst[:, 2:3], kmax[:]))
                E("vector", lambda e: e.tensor_copy(dbgst[:, 3:4], kmb[:]))
                vdb = E("vector", lambda e: e.tensor_copy(
                    dbgst[0:1, 4:8], statf[:, 0:4]), inc="VE")
                dma("sync", "DBG", dbg_t["dbg_negB"], dbgst[:],
                    waits=[("VE", vdb)])
            if upto < 5:
                raise _Stop

            gat = gathered.ap().rearrange("(r q) j -> r q j", r=NCORES)
            Aq = poolA[:].rearrange("p (g t) -> p g t", g=NH)
            kb_r = [k[:].rearrange("p (r j) -> p r j", r=8) for k in kbuf]
            vb_r = [v[:].rearrange("p (r c) -> p r c", r=8) for v in vbuf]
            kksem = ["KK0", "KK1"]
            vvsem = ["VV0", "VV1"]
            obank = (pb[2], pb[3])

            pairs = []
            for t in range(QT):
                for kvh in range(NKV):
                    u = t * NKV + kvh
                    n = 4 * (t + 1)
                    for pi in range(n):
                        jb, h = pi // 4, pi % 4
                        pairs.append(dict(t=t, kvh=kvh, u=u, jb=jb,
                                          r0=2 * h, r1=2 * h + 1,
                                          first=(pi == 0), last=(pi == n - 1)))
            NPAIR = len(pairs)
            st_pe = [0] * NPAIR
            exp_seL = [0] * NPAIR
            av_peL = [0] * NPAIR
            kv_done = {}
            nload = [0]
            last_st_on_kpar = {0: 0, 1: 0}
            last_av_on_kpar = {0: 0, 1: 0}
            last_ev = [0]

            def kv_load(p):
                pr = pairs[p]
                key = (pr["u"], pr["jb"])
                if key in kv_done:
                    return kv_done[key]
                t, kvh, jb = pr["t"], pr["kvh"], pr["jb"]
                kpar = nload[0] % 2
                nload[0] += 1
                ksrc = gat[:, kvh * 128:kvh * 128 + 128,
                           jb * 128:jb * 128 + 128].transpose((1, 0, 2))
                vsrc = gat[:, 1024 + jb * 128:1024 + jb * 128 + 128,
                           kvh * 128:kvh * 128 + 128].transpose((1, 0, 2))
                v_k = dma("sync", kksem[kpar], kb_r[kpar], ksrc,
                          waits=[("PE", last_st_on_kpar[kpar]), ("CC", 1)])
                vw = [("PE", last_av_on_kpar[kpar])]
                if nload[0] <= 2:
                    vw.append(("VE", v_memset))
                v_v = dma("sync", vvsem[kpar], vb_r[kpar][:, :, 0:128], vsrc,
                          waits=vw)
                kv_done[key] = (kpar, v_k, v_v)
                return kv_done[key]

            def emit_stgroup(p):
                pr = pairs[p]
                sp = p % 3
                kpar, v_k, v_v = kv_load(p)
                pr["kpar"], pr["v_v"] = kpar, v_v
                stw = [(kksem[kpar], v_k), ("VE", v_q)]
                if p < 3:
                    stw.append(("VE", v_negB))
                if p >= 3:
                    stw.append(("SE", exp_seL[p - 3]))
                qrhs = Aq[:, 4 * pr["kvh"]:4 * pr["kvh"] + 4,
                          128 * pr["t"]:128 * pr["t"] + 128]
                diag = (pr["jb"] == pr["t"])
                vst = 0
                for h, r in ((0, pr["r0"]), (1, pr["r1"])):
                    lastmm = (h == 1) and not diag
                    vst = E("tensor", (lambda e, kpar=kpar, r=r, sp=sp, h=h,
                                       qrhs=qrhs, diag=diag: e.matmul(
                                           stq[sp][:, 512 * h:512 * h + 512],
                                           kb_r[kpar][:, r, :], qrhs,
                                           start=True, stop=not diag,
                                           skip_group_check=True)),
                            waits=stw if h == 0 else (),
                            inc="PE" if lastmm else None)
                    if diag:
                        vst = E("tensor", (lambda e, sp=sp, h=h, r=r: e.matmul(
                            stq[sp][:, 512 * h:512 * h + 512], eye,
                            masks_s[:, 512 * r:512 * r + 512],
                            start=False, stop=True, skip_group_check=True)),
                            inc="PE" if h == 1 else None)
                last_st_on_kpar[kpar] = vst
                st_pe[p] = vst

            def emit_exp(p):
                sp = p % 3
                ew = [("PE", st_pe[p]), ("VE", v_negB)]
                if p >= 3:
                    ew.append(("PE", av_peL[p - 3]))
                exp_seL[p] = E("scalar", (lambda e, sp=sp: e.activation(
                    ptb[sp][:], stq[sp][:], AF.Exp,
                    bias=negB[:, 0:1], scale=SCL)), waits=ew, inc="SE")

            def emit_avgroup(p):
                pr = pairs[p]
                sp = p % 3
                ob = obank
                kpar, v_v = pr["kpar"], pr["v_v"]
                avw = [("SE", exp_seL[p]), (vvsem[kpar], v_v)]
                if pr["first"]:
                    avw.append(("VE", last_ev[0]))
                vav = 0
                for h, r in ((0, pr["r0"]), (1, pr["r1"])):
                    for g in range(4):
                        st_flag = pr["first"] and h == 0 and (g % 2 == 0)
                        vav = E("tensor", (lambda e, sp=sp, h=h, r=r, g=g,
                                           ob=ob, kpar=kpar, st_flag=st_flag:
                                           e.matmul(
                                               ob[g // 2][:, 129 * (g % 2):
                                                          129 * (g % 2) + 129],
                                               ptb[sp][:, 512 * h + 128 * g:
                                                       512 * h + 128 * g + 128],
                                               vb_r[kpar][:, r, 0:129],
                                               start=st_flag, stop=False,
                                               skip_group_check=True)),
                                waits=avw if (h == 0 and g == 0) else (),
                                inc="PE" if (h == 1 and g == 3) else None)
                av_peL[p] = vav
                last_av_on_kpar[kpar] = vav
                if pr["last"]:
                    emit_evac(pr, vav)

            def emit_evac(pr, v_av_all):
                ob = obank
                t, kvh = pr["t"], pr["kvh"]
                E("scalar", (lambda e, ob=ob: e.activation(
                    rl[:, 0:2], ob[0][:, 128:259:129], AF.Ln)),
                    waits=[("PE", v_av_all)])
                E("scalar", (lambda e, ob=ob: e.activation(
                    rl[:, 2:4], ob[1][:, 128:259:129], AF.Ln)))
                v_l = E("scalar", (lambda e: e.activation(
                    rl[:], rl[:], AF.Exp, scale=-1.0)), inc="SE")
                vsc = 0
                for g in range(4):
                    gg = 4 * kvh + g
                    vsc = E("vector", (lambda e, ob=ob, g=g, gg=gg, t=t:
                                       e.tensor_scalar(
                                           Aq[:, gg, 128 * t:128 * t + 128],
                                           ob[g // 2][:, 129 * (g % 2):
                                                      129 * (g % 2) + 128],
                                           rl[:, g:g + 1], None, ALU.mult)),
                            waits=[("SE", v_l)] if g == 0 else (), inc="VE")
                last_ev[0] = vsc

            for pp in range(min(3, NPAIR)):
                emit_stgroup(pp)
            for p in range(NPAIR):
                emit_exp(p)
                if p + 3 < NPAIR:
                    emit_stgroup(p + 3)
                emit_avgroup(p)
            v_attn = last_ev[0]

            trbit = stp1[:].bitcast(BF16)
            tslots = [trbit[:, 0:128], trbit[:, 1024:1152]]
            tr_ve = []
            for t in range(QT):
                for g in range(NH):
                    idx = len(tr_ve)
                    tw = []
                    if idx == 0:
                        tw.append(("VE", v_attn))
                    if idx >= 2:
                        tw.append(("VE", tr_ve[idx - 2]))
                    ptv = E("tensor", (lambda e, g=g, t=t, idx=idx:
                                       e.transpose(
                                           tslots[idx % 2],
                                           Aq[:, g, 128 * t:128 * t + 128],
                                           eye)), waits=tw, inc="PE")
                    tv = E("vector", (lambda e, g=g, t=t, idx=idx:
                                      e.tensor_copy(
                                          Br[:, g, 128 * t:128 * t + 128],
                                          tslots[idx % 2])),
                           waits=[("PE", ptv)], inc="VE")
                    tr_ve.append(tv)
            v_attn = tr_ve[-1]

            if dbg:
                dma("sync", "DBG", dbg_t["dbg_attnT"], poolB[:],
                    waits=[("VE", v_attn)])
            if upto < 6:
                raise _Stop

            woT_r = woT.rearrange("(c p) o -> p c o", c=IC)
            xT_rr = xT.rearrange("(c p) t -> c p t", c=IC)
            xrel = [scr8[:, 0:512], scr8[:, 512:1024]]
            xrsem = ["XR0", "XR1"]
            x1f = fscr[:, 0:512]
            x1f_ve = []
            x1_st = []
            wpar_lu = {0: sch.val("PE"), 1: sch.val("PE")}
            v_ln2pe = 0
            for oc in range(IC):
                wpar = oc % 2
                v_w = dma("gpsimd", wsem[wpar], wbr[wpar],
                          woT_r[:, :, 128 * oc:128 * oc + 128],
                          waits=[("PE", wpar_lu[wpar])])
                for tt in range(2):
                    g = oc * 2 + tt
                    par = tt
                    xpar = g % 2
                    xw = []
                    if g >= 2:
                        xw.append(("VE", x1f_ve[g - 2]))
                    v_x = dma("sync", xrsem[xpar], xrel[xpar],
                              xT_rr[oc, :, 512 * tt:512 * tt + 512], waits=xw)
                    mw = [(wsem[wpar], v_w), ("VE", v_attn)]
                    if g >= 2:
                        mw.append(("VE", x1f_ve[g - 2]))
                    for ic in range(IC):
                        v_pe = E("tensor", (lambda e, ic=ic, tt=tt, par=par,
                                            wpar=wpar: e.matmul(
                                                pb[par][:],
                                                wbr[wpar][:, ic, :],
                                                Br[:, ic,
                                                   512 * tt:512 * tt + 512],
                                                start=(ic == 0),
                                                stop=(ic == IC - 1),
                                                skip_group_check=True)),
                                 waits=mw if ic == 0 else (),
                                 inc="PE" if ic == IC - 1 else None)
                    fw = [("PE", v_pe), (xrsem[xpar], v_x)]
                    if x1_st:
                        fw.append(("ST1", x1_st[-1]))
                    E("vector", (lambda e, par=par, xpar=xpar:
                                 e.tensor_tensor(
                                     x1f, pb[par][:], xrel[xpar], ALU.add)),
                        waits=fw)
                    vx1 = E("vector", lambda e: e.tensor_copy(
                        fscr[:, 512:1024], x1f), inc="VE")
                    vst = dma("sync", "ST1",
                              x1d[128 * oc:128 * oc + 128,
                                  512 * tt:512 * tt + 512],
                              fscr[:, 512:1024], waits=[("VE", vx1)])
                    x1_st.append(vst)
                    E("vector", (lambda e, oc=oc, tt=tt: e.tensor_copy(
                        Ar[:, oc, 512 * tt:512 * tt + 512], x1f)))
                    vsq = E("vector", (lambda e, oc=oc, tt=tt, xpar=xpar:
                                       e.tensor_tensor(
                                           xsq[xpar],
                                           Ar[:, oc,
                                              512 * tt:512 * tt + 512],
                                           Ar[:, oc,
                                              512 * tt:512 * tt + 512],
                                           ALU.mult)), inc="VE")
                    x1f_ve.append(vsq)
                    E("tensor", (lambda e, tt=tt, oc=oc: e.matmul(
                        pb[2 + tt][:], ones,
                        Ar[:, oc, 512 * tt:512 * tt + 512],
                        start=(oc == 0), stop=(oc == IC - 1),
                        skip_group_check=True)), waits=[("VE", vsq)])
                    v_ln2pe = E("tensor", (lambda e, tt=tt, oc=oc, xpar=xpar:
                                           e.matmul(
                                               pb[4 + tt][:], ones,
                                               xsq[xpar],
                                               start=(oc == 0),
                                               stop=(oc == IC - 1),
                                               skip_group_check=True)),
                                inc="PE")
                wpar_lu[wpar] = sch.val("PE")
            if dbg:
                dma("sync", "DBG", dbg_t["dbg_x1"], x1d[:, :],
                    waits=[("ST1", x1_st[-1])])
            if upto < 7:
                raise _Stop

            v_ln2 = 0
            ln2_chunks = {}
            for tt in range(2):
                v_ln2, ln2_chunks[tt] = ln_finalize_and_apply(
                    tt, 2 + tt, 4 + tt, Ar, Br, v_ln2pe)
            if dbg:
                dma("sync", "DBG", dbg_t["dbg_ln2"], poolB[:],
                    waits=[("VE", v_ln2)])
            if upto < 8:
                raise _Stop

            u1pad = upads[:, 0:1024]
            u3pad = upads[:, 1024:2048]
            u2pad = upads[:, 2048:3072]
            a1r = a1s[:].rearrange("p (c r) -> p c r", c=IC)
            a3r = a3s[:].rearrange("p (c r) -> p c r", c=IC)
            v_use = 0
            for tt in range(2):
                for (ar, upad) in ((a1r, u1pad), (a3r, u3pad)):
                    for ic in range(IC):
                        w = [("VE", ln2_chunks[tt][ic]), ("W0", W0F)]
                        if v_use:
                            w.append(("SE", v_use))
                        pu = E("tensor", (lambda e, ic=ic, tt=tt, ar=ar:
                                          e.matmul(
                                              nrm[0:2, :], ar[:, ic, :],
                                              Br[:, ic,
                                                 512 * tt:512 * tt + 512],
                                              start=(ic == 0),
                                              stop=(ic == IC - 1),
                                              skip_group_check=True)),
                               waits=w,
                               inc="PE" if ic == IC - 1 else None)
                    v_use = E("scalar", (lambda e, tt=tt, upad=upad:
                                         e.activation(
                                             upad[0:2,
                                                  512 * tt:512 * tt + 512],
                                             nrm[0:2, :], AF.Copy,
                                             scale=0.5)),
                              waits=[("PE", pu)], inc="SE")
            hs = poolA[:, 0:8192].rearrange("p (c t) -> p c t", c=8)
            v_hs = 0
            for j in range(8):
                for tt in range(2):
                    E("vector", (lambda e, j=j, tt=tt: e.tensor_tensor(
                        fscr[:, 0:512], Br[:, j, 512 * tt:512 * tt + 512],
                        Br[:, j + 8, 512 * tt:512 * tt + 512], ALU.add)),
                        waits=[("VE", v_ln2)])
                    E("vector", (lambda e, j=j, tt=tt: e.tensor_tensor(
                        fscr[:, 512:1024],
                        Br[:, j + 16, 512 * tt:512 * tt + 512],
                        Br[:, j + 24, 512 * tt:512 * tt + 512], ALU.add)))
                    v_hs = E("vector", (lambda e, j=j, tt=tt:
                                        e.tensor_tensor(
                                            hs[:, j,
                                               512 * tt:512 * tt + 512],
                                            fscr[:, 0:512],
                                            fscr[:, 512:1024], ALU.add)),
                             inc="VE")

            w1T_r = w1T.rearrange("(c p) h -> p c h", c=9)
            w3T_r = w3T.rearrange("(c p) h -> p c h", c=9)
            gsv = poolB[:, 0:FC * 1024].rearrange("p (c t) -> p c t", c=FC)
            gcs = [[bscr[:, 512 * (2 * f + tt):512 * (2 * f + tt) + 512]
                    for tt in range(2)] for f in range(4)]
            sg = ptb[0][:, 0:512]
            a2r = a2s[:].rearrange("p (c r) -> p c r", c=HC)
            w1b = [w[:, 0:1152].rearrange("p (c h) -> p c h", c=9)
                   for w in wb]
            w3b = [w[:, 1152:2304].rearrange("p (c h) -> p c h", c=9)
                   for w in wb]
            silu_se = []
            gc_ve = []
            u2_pe = {}
            wpar_last = dict(wpar_lu)
            u2_started = {0: False, 1: False}
            v_gs = 0
            wfst = [poolA[:, 8192 + 4608 * i:8192 + 4608 * i + 4608]
                    .bitcast(F32).rearrange("p (c h) -> p c h", c=18)
                    for i in range(2)]
            cast_ve = {}
            for p in range(FC):
                for f in range(4):
                    hc = 22 * f + p
                    wpar = (4 * p + f) % 2
                    i_wl = 4 * p + f
                    lw = [("VE", v_ln2)] if i_wl < 2 else [
                        ("VE", cast_ve[i_wl - 2])]
                    v_wf = dma("sync", f"WF{wpar}",
                               wfst[wpar][:, 0:9, :],
                               w1T_r[:, :, 128 * hc:128 * hc + 128], waits=lw)
                    v_wf = dma("sync", f"WF{wpar}",
                               wfst[wpar][:, 9:18, :],
                               w3T_r[:, :, 128 * hc:128 * hc + 128])
                    cw = [(f"WF{wpar}", v_wf), ("PE", wpar_last[wpar])]
                    E("vector", (lambda e, wpar=wpar: e.tensor_copy(
                        wb[wpar][:, 0:1152],
                        wfst[wpar][:, 0:9, :].rearrange("p c h -> p (c h)"))),
                        waits=cw)
                    v_w3 = cast_ve[i_wl] = E(
                        "vector", (lambda e, wpar=wpar: e.tensor_copy(
                            wb[wpar][:, 1152:2304],
                            wfst[wpar][:, 9:18, :].rearrange(
                                "p c h -> p (c h)"))), inc="VE")
                    for tt in range(2):
                        i_gu = len(silu_se)
                        gbank = [pb[0], nrm][i_gu % 2]
                        ubank = [pb[1], bslot][i_gu % 2]
                        sgb = [sg, ptb[0][:, 512:1024]][i_gu % 2]
                        mw = [("VE", v_w3), ("VE", v_hs)]
                        if v_use:
                            mw.append(("SE", v_use))
                        if i_gu >= 2:
                            mw.append(("SE", silu_se[i_gu - 2]))
                        for ic in range(9):
                            rhs = (hs[:, ic, 512 * tt:512 * tt + 512]
                                   if ic < 8
                                   else u1pad[:, 512 * tt:512 * tt + 512])
                            E("tensor", (lambda e, ic=ic, wpar=wpar, rhs=rhs,
                                         gbank=gbank: e.matmul(
                                             gbank, w1b[wpar][:, ic, :],
                                             rhs, start=(ic == 0),
                                             stop=(ic == 8),
                                             skip_group_check=True)),
                                waits=mw if ic == 0 else ())
                        uw = [("VE", gc_ve[i_gu - 2])] if i_gu >= 2 else []
                        for ic in range(9):
                            rhs = (hs[:, ic, 512 * tt:512 * tt + 512]
                                   if ic < 8
                                   else u3pad[:, 512 * tt:512 * tt + 512])
                            v_gu = E("tensor", (lambda e, ic=ic, wpar=wpar,
                                                rhs=rhs, ubank=ubank:
                                                e.matmul(
                                                    ubank,
                                                    w3b[wpar][:, ic, :],
                                                    rhs, start=(ic == 0),
                                                    stop=(ic == 8),
                                                    skip_group_check=True)),
                                     waits=uw if ic == 0 else (),
                                     inc="PE" if ic == 8 else None)
                        sse = E("scalar", (lambda e, gbank=gbank, sgb=sgb:
                                           e.activation(
                                               sgb, gbank, AF.Silu)),
                            waits=[("PE", v_gu)], inc="SE")
                        silu_se.append(sse)
                        gw = [("SE", sse)]
                        if (f, tt) in u2_pe:
                            gw.append(("PE", u2_pe[(f, tt)]))
                        vgc = E("vector", (lambda e, f=f, tt=tt, sgb=sgb,
                                           ubank=ubank: e.tensor_tensor(
                                               gcs[f][tt], sgb, ubank,
                                               ALU.mult)),
                                waits=gw, inc="VE")
                        gc_ve.append(vgc)
                        u2w = [("VE", vgc), ("W0", W0F)]
                        if not u2_started[tt]:
                            u2w.append(("VE", v_ln2))
                        pu2 = E("tensor", (lambda e, hc=hc, f=f, tt=tt,
                                           s=not u2_started[tt]: e.matmul(
                                               pb[2 + tt][0:2, :],
                                               a2r[:, hc, :], gcs[f][tt],
                                               start=s,
                                               stop=(hc == HC - 1),
                                               skip_group_check=True)),
                                 waits=u2w, inc="PE")
                        u2_started[tt] = True
                        u2_pe[(f, tt)] = pu2
                    wpar_last[wpar] = sch.val("PE")
                for tt in range(2):
                    E("vector", (lambda e, tt=tt: e.tensor_tensor(
                        fscr[:, 0:512], gcs[0][tt], gcs[1][tt], ALU.add)),
                        waits=[("PE", u2_pe[(3, tt)])])
                    E("vector", (lambda e, tt=tt: e.tensor_tensor(
                        fscr[:, 512:1024], gcs[2][tt], gcs[3][tt], ALU.add)))
                    v_gs = E("vector", (lambda e, p=p, tt=tt:
                                        e.tensor_tensor(
                                            gsv[:, p,
                                                512 * tt:512 * tt + 512],
                                            fscr[:, 0:512],
                                            fscr[:, 512:1024], ALU.add)),
                             inc="VE")
            E("scalar", (lambda e: e.activation(
                u2pad[0:2, 0:512], pb[2][0:2, :], AF.Copy, scale=0.5)),
                waits=[("PE", u2_pe[(3, 0)])])
            v_u2b = E("scalar", (lambda e: e.activation(
                u2pad[0:2, 512:1024], pb[3][0:2, :], AF.Copy, scale=0.5)),
                waits=[("PE", u2_pe[(3, 1)])], inc="SE")
            if dbg:
                dma("sync", "DBG", dbg_t["dbg_gs"], poolB[:, 0:FC * 1024],
                    waits=[("VE", v_gs)])

            w2T_r = w2T.rearrange("(c p) o -> p c o", c=23)
            w2b = [w[:, 0:2944].rearrange("p (c o) -> p c o", c=23)
                   for w in wb]
            w2st = [poolA[:, 8192 + 5888 * i:8192 + 5888 * i + 5888]
                    .bitcast(F32).rearrange("p (c o) -> p c o", c=23)
                    for i in range(2)]
            cast2_ve = {}
            x2f = fscr[:, 512:1024]
            y_ve = []
            y_st = []
            st1_full = x1_st[-1]
            wpar_last = {0: sch.val("PE"), 1: sch.val("PE")}
            for oc in range(IC):
                wpar = oc % 2
                lw = [("VE", v_gs)] if oc < 2 else [
                    ("VE", cast2_ve[oc - 2])]
                v_wf = dma("sync", f"WF{wpar}", w2st[wpar],
                           w2T_r[:, :, 128 * oc:128 * oc + 128], waits=lw)
                v_w = cast2_ve[oc] = E(
                    "vector", (lambda e, wpar=wpar: e.tensor_copy(
                        wb[wpar][:, 0:2944],
                        w2st[wpar].rearrange("p c o -> p (c o)"))),
                    waits=[(f"WF{wpar}", v_wf), ("PE", wpar_last[wpar])],
                    inc="VE")
                for tt in range(2):
                    g = oc * 2 + tt
                    par = tt
                    xpar = g % 2
                    xw = [("ST1", st1_full)]
                    if g >= 2:
                        xw.append(("VE", y_ve[g - 2]))
                    v_x = dma("sync", xrsem[xpar], xrel[xpar],
                              x1d[128 * oc:128 * oc + 128,
                                  512 * tt:512 * tt + 512], waits=xw)
                    mw = [("VE", v_w), ("VE", v_gs), ("SE", v_u2b)]
                    if g >= 2:
                        mw.append(("VE", y_ve[g - 2]))
                    for ic in range(23):
                        rhs = (gsv[:, ic, 512 * tt:512 * tt + 512]
                               if ic < 22
                               else u2pad[:, 512 * tt:512 * tt + 512])
                        v_pe = E("tensor", (lambda e, ic=ic, par=par,
                                            wpar=wpar, rhs=rhs: e.matmul(
                                                pb[par][:],
                                                w2b[wpar][:, ic, :], rhs,
                                                start=(ic == 0),
                                                stop=(ic == 22),
                                                skip_group_check=True)),
                                 waits=mw if ic == 0 else (),
                                 inc="PE" if ic == 22 else None)
                    fw = [("PE", v_pe), (xrsem[xpar], v_x)]
                    if y_st:
                        fw.append(("ST2", y_st[-1]))
                    vy = E("vector", (lambda e, par=par, xpar=xpar:
                                      e.tensor_tensor(
                                          x2f, pb[par][:], xrel[xpar],
                                          ALU.add)), waits=fw, inc="VE")
                    y_ve.append(vy)
                    vst = dma("sync", "ST2",
                              out[128 * oc:128 * oc + 128,
                                  512 * tt:512 * tt + 512],
                              x2f, waits=[("VE", vy)])
                    y_st.append(vst)
                wpar_last[wpar] = sch.val("PE")

        try:
            _sched_all()
        except _Stop:
            pass

        all_dma_sems = [s for s in sch.ctr
                        if s not in ("PE", "VE", "SE", "PO", "CC")]
        E("gpsimd", lambda e: None,
          waits=[(s, sch.val(s)) for s in all_dma_sems])

        semnames = sorted({s for ops in sch.ops.values()
                           for (ws, _, inc, _) in ops
                           for s in ([w[0] for w in ws] +
                                     ([inc] if inc else []))})
        sems = {s: ctx.enter_context(nc.semaphore(s)) for s in semnames}
        block = ctx.enter_context(nc.Block())

        def mk(name):
            def f(eng):
                seen = {}
                for waits, fn, inc, inc_n in sch.ops[name]:
                    for (s, v) in waits:
                        if v > 0 and seen.get(s, 0) < v:
                            eng.wait_ge(sems[s], v)
                            seen[s] = v
                    instr = fn(eng)
                    if inc is not None:
                        instr.then_inc(sems[inc], inc_n)
            return f

        block.sync(mk("sync"))
        block.gpsimd(mk("gpsimd"))
        block.tensor(mk("tensor"))
        block.vector(mk("vector"))
        block.scalar(mk("scalar"))

    return nc



def _stage(inputs):
    x = np.asarray(inputs["L0_attn_hidden_states"])[0]
    Wq = np.asarray(inputs["L0_attn_Wq"])
    Wk = np.asarray(inputs["L0_attn_Wk"])
    Wv = np.asarray(inputs["L0_attn_Wv"])
    Wo = np.asarray(inputs["L0_attn_Wo"])
    w1q = np.asarray(inputs["L0_ffn_w1q"])
    a1 = np.asarray(inputs["L0_ffn_a1"])
    b1 = np.asarray(inputs["L0_ffn_b1"])
    w3q = np.asarray(inputs["L0_ffn_w3q"])
    a3 = np.asarray(inputs["L0_ffn_a3"])
    b3 = np.asarray(inputs["L0_ffn_b3"])
    w2q = np.asarray(inputs["L0_ffn_w2q"])
    a2 = np.asarray(inputs["L0_ffn_a2"])
    b2 = np.asarray(inputs["L0_ffn_b2"])

    C = np.ascontiguousarray
    f32 = np.float32

    wqT = C(Wq.T.astype(f32))
    wkT = C(Wk.T.astype(f32))
    wvT = C(Wv.T.astype(f32))
    woT = C(Wo.T.astype(f32))

    def pad_gate(wT, bT):
        outw = np.zeros((1152, HIDP), f32)
        for f in range(4):
            outw[0:1024, 2816 * f:2816 * f + 2752] = \
                wT[:, 2752 * f:2752 * f + 2752]
            outw[1024:1026, 2816 * f:2816 * f + 2752] = \
                bT[:, 2752 * f:2752 * f + 2752]
        return outw

    w1T = pad_gate(C(w1q.T.astype(f32)), C(b1.T.astype(f32)))
    w3T = pad_gate(C(w3q.T.astype(f32)), C(b3.T.astype(f32)))

    w2T = np.zeros((W2K + 128, D), f32)
    w2T[0:2752, :] = w2q.T.astype(f32)
    w2T[W2K:W2K + 2, :] = b2.T.astype(f32)

    def a_small(a, nch):
        acx = a.T.astype(f32).reshape(nch, 128, a.shape[0])
        return C(acx.transpose(1, 0, 2).reshape(128, nch * a.shape[0]))

    a1cv = a_small(a1, IC)
    a3cv = a_small(a3, IC)
    a2p = np.zeros((2, HIDP), f32)
    for f in range(4):
        a2p[:, 2816 * f:2816 * f + 2752] = a2[:, 2752 * f:2752 * f + 2752]
    a2cv = a_small(a2p, HC)

    inv = (1.0 / (10000.0 ** (np.arange(0, HD, 2, dtype=np.float64) / HD)))
    bf = ml_dtypes.bfloat16

    eye_ones = np.zeros((128, 256), f32)
    eye_ones[:, 0:128] = np.eye(128)
    eye_ones[:, 128:256] = 1.0
    consts_v = eye_ones.astype(bf)

    in_maps = []
    for c in range(NCORES):
        xc = C(x[c::8, :].T.astype(f32))
        pos = (c + 8 * np.arange(R)).astype(np.float64)
        fr = np.outer(pos, inv)
        emb = np.concatenate([fr, fr], axis=1)
        cosv = C(np.cos(emb).T.astype(f32))
        sinv = np.sin(emb).T.astype(f32)
        sins = sinv.copy()
        sins[0:64, :] = -sinv[0:64, :]
        sins = C(sins)

        jj = np.arange(128)[:, None]
        mmi = np.arange(128)[None, :]
        allow = (jj < mmi)
        mrow = []
        for rr in range(8):
            al = allow | ((jj == mmi) & (rr <= c))
            mk = np.where(al, 0.0, -1e9).astype(f32)
            mrow.append(np.tile(mk, (1, 4)))
        maskv = np.concatenate(mrow, axis=1).astype(bf)

        in_maps.append({
            "xT": xc, "wqT": wqT, "wkT": wkT, "wvT": wvT, "woT": woT,
            "w1T": w1T, "w3T": w3T, "w2T": w2T,
            "a1c": a1cv, "a3c": a3cv, "a2c": a2cv,
            "cosq": cosv, "sinq": sins,
            "masks": maskv, "consts": consts_v,
        })
    return in_maps


def kernel(**inputs):
    in_maps = _stage(inputs)
    nc = build(dbg=False)
    res = run_bass_kernel_spmd(nc, in_maps, core_ids=list(range(NCORES)))
    full = np.empty((1, S, D), dtype=np.float32)
    for c in range(NCORES):
        full[0, c::8, :] = np.asarray(res.results[c]["out"],
                                      dtype=np.float32).T
    return full


# version 06
# speedup vs baseline: 1.0785x; 1.0785x over previous
import math
import numpy as np
import ml_dtypes
import concourse.bass as bass
import concourse.mybir as mybir
from concourse.bass_utils import run_bass_kernel_spmd
from contextlib import ExitStack

F32 = mybir.dt.float32
BF16 = mybir.dt.bfloat16
AF = mybir.ActivationFunctionType
ALU = mybir.AluOpType
AXX = mybir.AxisListType.X

NCORES = 8
S = 8192
D = 4096
NH = 32
NKV = 8
HD = 128
R = 1024
IC = 32
QT = 8
HIDP = 11264
HC = 88
FC = 22
W2K = 2816
RPC = 2049
SCL = 1.0 / math.sqrt(HD)
LNC_BIAS = -0.5 * math.log(HD)
ENGINES = ("sync", "gpsimd", "tensor", "vector", "scalar")


class Sched:
    def __init__(self):
        self.ops = {e: [] for e in ENGINES}
        self.ctr = {}

    def val(self, sem):
        return self.ctr.get(sem, 0)

    def emit(self, eng, fn, waits=(), inc=None, inc_n=1):
        post = None
        if inc is not None:
            self.ctr[inc] = self.ctr.get(inc, 0) + inc_n
            post = self.ctr[inc]
        self.ops[eng].append((tuple(waits), fn, inc, inc_n))
        return post


class _Stop(Exception):
    pass


def build(dbg=False, upto=99):
    nc = bass.Bass("TRN2", target_bir_lowering=False)

    def din(name, shape, dt=F32):
        return nc.dram_tensor(name, shape, dt, kind="ExternalInput").ap()

    xT = din("xT", [D, R])
    wqT = din("wqT", [D, D])
    wkT = din("wkT", [D, NKV * HD])
    wvT = din("wvT", [D, NKV * HD])
    woT = din("woT", [D, D])
    w1T = din("w1T", [1024 + 128, HIDP])
    w3T = din("w3T", [1024 + 128, HIDP])
    w2T = din("w2T", [W2K + 128, D])
    a1c = din("a1c", [128, IC * 2])
    a3c = din("a3c", [128, IC * 2])
    a2c = din("a2c", [128, HC * 2])
    cosq = din("cosq", [128, R])
    sinq = din("sinq", [128, R])
    masks = din("masks", [128, 8 * 512], BF16)
    consts = din("consts", [128, 256], BF16)

    out = nc.dram_tensor("out", [D, R], F32, kind="ExternalOutput").ap()

    contrib = nc.dram_tensor("contrib", [RPC, 1024], BF16)
    gathered = nc.dram_tensor("gathered", [NCORES * RPC, 1024], BF16)
    x1d = nc.dram_tensor("x1d", [D, R], F32)

    dbg_t = {}
    if dbg:
        for nm, shp, dt in [
            ("dbg_ln1", [128, IC * 1024], BF16),
            ("dbg_q", [128, IC * 1024], BF16),
            ("dbg_contrib", [RPC, 1024], BF16),
            ("dbg_negB", [128, 8], F32),
            ("dbg_attnT", [128, IC * 1024], BF16),
            ("dbg_x1", [D, R], F32),
            ("dbg_ln2", [128, IC * 1024], BF16),
            ("dbg_gs", [128, FC * 1024], BF16),
        ]:
            dbg_t[nm] = nc.dram_tensor(nm, shp, dt, kind="ExternalOutput").ap()

    sch = Sched()
    with ExitStack() as ctx:
        def sb(name, shape, dt):
            return ctx.enter_context(nc.sbuf_tensor(name, shape, dt))

        def pst(name, shape, dt):
            return ctx.enter_context(nc.psum_tensor(name, shape, dt))

        poolA = sb("poolA", [128, 32768], BF16)
        poolB = sb("poolB", [128, 32768], BF16)
        wbufA = sb("wbufA", [128, 4096], BF16)
        wbufB = sb("wbufB", [128, 4096], BF16)
        masks_s = sb("masks_s", [128, 4096], BF16)
        consts_s = sb("consts_s", [128, 256], BF16)
        scr8 = sb("scr8", [128, 2048], F32)
        fscr = sb("fscr", [128, 1024], F32)
        bscr = sb("bscr", [128, 4096], BF16)
        upads = sb("upads", [128, 3072], BF16)
        kbuf = [sb(f"kbuf{i}", [128, 1024], BF16) for i in range(2)]
        vbuf = [sb(f"vbuf{i}", [128, 8 * 136], BF16) for i in range(2)]
        ptb = [sb(f"ptb{i}", [128, 1024], BF16) for i in range(3)]
        m_s = sb("m_s", [128, 512], F32)
        r_s = sb("r_s", [128, 512], F32)
        mb16 = sb("mb16", [128, 512], BF16)
        a1s = sb("a1s", [128, IC * 2], BF16)
        a3s = sb("a3s", [128, IC * 2], BF16)
        a2s = sb("a2s", [128, HC * 2], BF16)
        qmaxw = sb("qmaxw", [128, 512], F32)
        kmaxw = sb("kmaxw", [128, 512], F32)
        qmax = sb("qmax", [128, 1], F32)
        kmax = sb("kmax", [128, 1], F32)
        kred = sb("kred", [128, 1], F32)
        kmb = sb("kmb", [128, 1], F32)
        negB = sb("negB", [128, 1], F32)
        lnc_t = sb("lnc_t", [128, 1], F32)
        rl = sb("rl", [128, 4], F32)
        stat_s = sb("stat_s", [1, 16], BF16)
        stat2 = sb("stat2", [1, 4], F32)
        epst = sb("epst", [128, 1], F32)
        statg = sb("statg", [1, 64], BF16)
        kst = sb("kst", [128, 1024], BF16)
        statf = sb("statf", [1, 64], F32)

        stp0 = pst("stp0", [128, 1024], F32)
        stp1 = pst("stp1", [128, 1024], F32)
        ob01 = pst("ob01", [128, 1024], F32)
        ob23 = pst("ob23", [128, 1024], F32)
        pb = [stp0[:, 0:512], stp0[:, 512:1024],
              ob01[:, 0:512], ob01[:, 512:1024],
              ob23[:, 0:512], ob23[:, 512:1024]]
        stq = [stp0[:], stp1[:], ob23[:]]
        nrm = stp1[:, 0:512]
        bslot = stp1[:, 512:1024]

        eye = consts_s[:, 0:128]
        ones = consts_s[:, 128:256]
        onesrow = consts_s[0:1, 128:256]

        Ar = poolA[:].rearrange("p (c t) -> p c t", c=IC)
        Br = poolB[:].rearrange("p (c t) -> p c t", c=IC)

        E = sch.emit

        def dma(eng, sem_, o, i, waits=()):
            return E(eng, (lambda e, o=o, i=i: e.dma_start(out=o, in_=i)),
                     waits=waits, inc=sem_, inc_n=16)

        def _sched_all():
            xT_r = xT.rearrange("(c p) t -> p c t", c=IC)
            for q4 in range(4):
                dma("gpsimd", f"XB{q4}", Ar[:, 8 * q4:8 * q4 + 8, :],
                    xT_r[:, 8 * q4:8 * q4 + 8, :])
            dma("gpsimd", "W0", consts_s[:], consts)
            dma("gpsimd", "W0", masks_s[:], masks)
            dma("gpsimd", "W0", a1s[:], a1c)
            dma("gpsimd", "W0", a3s[:], a3c)
            dma("gpsimd", "W0", a2s[:], a2c)
            W0F = sch.val("W0")
            dma("sync", "K0", scr8[:, 0:1024], cosq)
            dma("sync", "K0", scr8[:, 1024:2048], sinq)
            K0F = sch.val("K0")
            cos = scr8[:, 0:1024]
            sin = scr8[:, 1024:2048]

            E("vector", lambda e: e.memset(epst[:], 1e-5))
            E("vector", lambda e: e.memset(lnc_t[:], LNC_BIAS))
            E("vector", lambda e: e.memset(qmaxw[:], 0.0))
            E("vector", lambda e: e.memset(kmaxw[:], 0.0))
            E("vector", lambda e: e.memset(upads[:], 0.0))
            E("vector", lambda e: e.memset(stat_s[:], 0.0))
            E("vector", lambda e: e.memset(vbuf[0][:], 1.0))
            v_memset = E("vector", lambda e: e.memset(vbuf[1][:], 1.0),
                         inc="VE")

            xsq = [bscr[:, 0:512], bscr[:, 512:1024]]
            for tt in range(2):
                for ic in range(IC):
                    w = [(f"XB{ic // 8}", 16)]
                    if ic == 0 and tt == 0:
                        w.append(("W0", W0F))
                    E("tensor", (lambda e, ic=ic, tt=tt: e.matmul(
                        pb[tt][:], ones, Ar[:, ic, 512 * tt:512 * tt + 512],
                        start=(ic == 0), stop=(ic == IC - 1),
                        skip_group_check=True)), waits=w)
            sq_pe = []
            for tt in range(2):
                for ic in range(IC):
                    g = tt * IC + ic
                    par = g % 2
                    w = [(f"XB{ic // 8}", 16)]
                    if g >= 2:
                        w.append(("PE", sq_pe[g - 2]))
                    vv = E("vector", (lambda e, ic=ic, tt=tt, par=par:
                                      e.tensor_tensor(
                                          xsq[par],
                                          Ar[:, ic, 512 * tt:512 * tt + 512],
                                          Ar[:, ic, 512 * tt:512 * tt + 512],
                                          ALU.mult)), waits=w, inc="VE")
                    pv = E("tensor", (lambda e, ic=ic, tt=tt, par=par: e.matmul(
                        pb[2 + tt][:], ones, xsq[par],
                        start=(ic == 0), stop=(ic == IC - 1),
                        skip_group_check=True)), waits=[("VE", vv)], inc="PE")
                    sq_pe.append(pv)
            v_stats_pe = sq_pe[-1]

            def ln_finalize_and_apply(tt, bank_m, bank_sq, src_r, dst_r,
                                      pe_wait):
                E("vector", (lambda e: e.tensor_scalar(
                    m_s[:], pb[bank_m][:], 1.0 / D, None, ALU.mult)),
                    waits=[("PE", pe_wait)])
                E("vector", (lambda e: e.tensor_scalar(
                    fscr[:, 0:512], pb[bank_sq][:], 1.0 / D, None, ALU.mult)))
                E("vector", (lambda e: e.tensor_tensor(
                    r_s[:], m_s[:], m_s[:], ALU.mult)))
                E("vector", (lambda e: e.tensor_tensor(
                    fscr[:, 0:512], fscr[:, 0:512], r_s[:], ALU.subtract)))
                vw = E("vector", lambda e: e.tensor_copy(mb16[:], m_s[:]),
                       inc="VE")
                E("scalar", (lambda e: e.activation(
                    fscr[:, 512:1024], fscr[:, 0:512], AF.Ln,
                    bias=epst[:, 0:1])), waits=[("VE", vw)])
                sw = E("scalar", (lambda e: e.activation(
                    r_s[:], fscr[:, 512:1024], AF.Exp, scale=-0.5)), inc="SE")
                last = 0
                chunk_incs = []
                for ic in range(IC):
                    E("vector", (lambda e, ic=ic: e.tensor_tensor(
                        fscr[:, 0:512], src_r[:, ic, 512 * tt:512 * tt + 512],
                        mb16[:], ALU.subtract)), waits=[("SE", sw)])
                    last = E("vector", (lambda e, ic=ic: e.tensor_tensor(
                        dst_r[:, ic, 512 * tt:512 * tt + 512], fscr[:, 0:512],
                        r_s[:], ALU.mult)), inc="VE")
                    chunk_incs.append(last)
                return last, chunk_incs

            v_ln1 = 0
            for tt in range(2):
                v_ln1, _ = ln_finalize_and_apply(tt, tt, 2 + tt, Ar, Br,
                                                 v_stats_pe)

            if dbg:
                dma("sync", "DBG", dbg_t["dbg_ln1"], poolB[:],
                    waits=[("VE", v_ln1)])
            if upto < 2:
                raise _Stop

            wsem = ["WA", "WB"]
            wkT_r = wkT.rearrange("(c p) d -> p c d", c=IC)
            wvT_r = wvT.rearrange("(c p) d -> p c d", c=IC)
            wb = [wbufA, wbufB]
            wbr = [w[:].rearrange("p (c o) -> p c o", c=IC) for w in wb]
            rope_ve = []
            norm_pe = []
            norm_ve = []
            wpar_last_use = {0: 0, 1: 0}

            def proj_rope_block(tt, par, dst_ap, v_w, wpar, is_q,
                                extra_mm_waits=()):
                gidx = len(rope_ve)
                w = [(wsem[wpar], v_w), ("K0", K0F), ("VE", v_ln1)]
                w += list(extra_mm_waits)
                if gidx >= 2:
                    w.append(("VE", rope_ve[gidx - 2]))
                for ic in range(IC):
                    pk = E("tensor", (lambda e, ic=ic, tt=tt, par=par,
                                      wpar=wpar: e.matmul(
                                          pb[4 + par][:], wbr[wpar][:, ic, :],
                                          Br[:, ic, 512 * tt:512 * tt + 512],
                                          start=(ic == 0), stop=(ic == IC - 1),
                                          skip_group_check=True)),
                           waits=w if ic == 0 else (),
                           inc="PE" if ic == IC - 1 else None)
                sqw = [("PE", pk)]
                if norm_pe:
                    sqw.append(("PE", norm_pe[-1]))
                se = E("scalar", (lambda e, par=par: e.activation(
                    bscr[:, 0:512], pb[4 + par][:], AF.Square)),
                    waits=sqw, inc="SE")
                nmw = [("SE", se)]
                if norm_ve:
                    nmw.append(("VE", norm_ve[-1]))
                pn = E("tensor", (lambda e: e.matmul(
                    nrm, ones, bscr[:, 0:512], start=True, stop=True,
                    skip_group_check=True)), waits=nmw, inc="PE")
                norm_pe.append(pn)
                tgtw = qmaxw if is_q else kmaxw
                nv = E("vector", (lambda e, tgtw=tgtw: e.tensor_tensor(
                    tgtw[:], tgtw[:], nrm, ALU.max)),
                    waits=[("PE", pn)], inc="VE")
                norm_ve.append(nv)
                E("vector", (lambda e, par=par, tt=tt: e.tensor_tensor(
                    fscr[:, 0:512], pb[4 + par][:],
                    cos[:, 512 * tt:512 * tt + 512], ALU.mult)),
                    waits=[("SE", se)])
                E("vector", (lambda e, par=par, tt=tt: e.tensor_tensor(
                    fscr[0:64, 512:1024], pb[4 + par][64:128, :],
                    sin[0:64, 512 * tt:512 * tt + 512], ALU.mult)))
                E("vector", (lambda e, par=par, tt=tt: e.tensor_tensor(
                    fscr[64:128, 512:1024], pb[4 + par][0:64, :],
                    sin[64:128, 512 * tt:512 * tt + 512], ALU.mult)))
                rv = E("vector", (lambda e, dst_ap=dst_ap: e.tensor_tensor(
                    dst_ap, fscr[:, 0:512], fscr[:, 512:1024], ALU.add)),
                    inc="VE")
                rope_ve.append(rv)
                return rv

            kwv = {}
            for kc in range(2):
                kwv[kc] = dma("gpsimd", wsem[kc % 2], wbr[kc % 2],
                              wkT_r[:, :, 128 * kc:128 * kc + 128])
            wst = [poolA[:, 0:16384].rearrange("p (c d) -> p c d", d=512),
                   poolA[:, 16384:32768].rearrange("p (c d) -> p c d", d=512)]
            v_wv = [dma("gpsimd", f"WV{vh}", wst[vh],
                        wvT_r[:, :, 512 * vh:512 * vh + 512],
                        waits=[("VE", v_ln1)]) for vh in range(2)]
            kst_dma = 0
            for kc in range(NKV):
                wpar = kc % 2
                if kc in kwv:
                    v_w = kwv[kc]
                else:
                    v_w = dma("gpsimd", wsem[wpar], wbr[wpar],
                              wkT_r[:, :, 128 * kc:128 * kc + 128],
                              waits=[("PE", wpar_last_use[wpar])])
                for tt in range(2):
                    extra = [("WC", kst_dma)] if kst_dma else []
                    rv = proj_rope_block(tt, tt,
                                         kst[:, 512 * tt:512 * tt + 512],
                                         v_w, wpar, is_q=False,
                                         extra_mm_waits=extra)
                wpar_last_use[wpar] = sch.val("PE")
                kst_dma = dma("gpsimd", "WC",
                              contrib[128 * kc:128 * kc + 128, :],
                              kst[:], waits=[("VE", rv)])

            vkr = E("vector", (lambda e: e.reduce_max(
                kmax[:], kmaxw[:], axis=AXX)), inc="VE")
            se_ks = E("scalar", (lambda e: e.activation(
                stat_s[0:1, 0:1], kmax[0:1, :], AF.Copy)),
                waits=[("VE", vkr)], inc="SE")
            dma("gpsimd", "WC", contrib[2048:2049, 0:16], stat_s[:],
                waits=[("SE", se_ks)])

            vstage = [ptb[1][:, 0:512], ptb[1][:, 512:1024]]
            pe_v = {}
            vev = {}
            v_contribs = []
            for vh in range(2):
                for ts in range(8):
                    g = vh * 8 + ts
                    par = g % 2
                    w = [(f"WV{vh}", v_wv[vh]), ("VE", v_ln1)]
                    if g >= 2:
                        w.append(("VE", vev[g - 2]))
                    for ic in range(IC):
                        pv_ = E("tensor", (lambda e, ic=ic, ts=ts, par=par,
                                           vh=vh: e.matmul(
                                               pb[par][:],
                                               Br[:, ic,
                                                  128 * ts:128 * ts + 128],
                                               wst[vh][:, ic, :],
                                               start=(ic == 0),
                                               stop=(ic == IC - 1),
                                               skip_group_check=True)),
                                waits=w if ic == 0 else (),
                                inc="PE" if ic == IC - 1 else None)
                    pe_v[(vh, ts)] = pv_
                    evw = [("PE", pv_)]
                    if g >= 2:
                        evw.append(("WC", v_contribs[g - 2]))
                    vev[g] = E("vector", (lambda e, par=par, g=g:
                                          e.tensor_copy(
                                              vstage[g % 2], pb[par][:])),
                               waits=evw, inc="VE")
                    v_contribs.append(dma(
                        "gpsimd", "WC",
                        contrib[1024 + 128 * ts:1024 + 128 * ts + 128,
                                512 * vh:512 * vh + 512],
                        vstage[g % 2], waits=[("VE", vev[g])]))

            if dbg:
                dma("sync", "DBG", dbg_t["dbg_contrib"], contrib[:, :],
                    waits=[("WC", sch.val("WC"))])
            if upto < 4:
                raise _Stop

            E("gpsimd", (lambda e: e.collective_compute(
                "AllGather", ALU.bypass, replica_groups=[list(range(NCORES))],
                ins=[contrib.ap().opt()], outs=[gathered.ap().opt()])),
                waits=[("WC", sch.val("WC"))], inc="CC")

            wqT_r = wqT.rearrange("(c p) o -> p c o", c=IC)
            for oc in range(NH):
                wpar = oc % 2
                v_w = dma("gpsimd", wsem[wpar], wbr[wpar],
                          wqT_r[:, :, 128 * oc:128 * oc + 128],
                          waits=[("PE", wpar_last_use[wpar])])
                for tt in range(2):
                    rv = proj_rope_block(tt, tt,
                                         Ar[:, oc, 512 * tt:512 * tt + 512],
                                         v_w, wpar, is_q=True)
                wpar_last_use[wpar] = sch.val("PE")
            v_q = rope_ve[-1]

            if dbg:
                dma("sync", "DBG", dbg_t["dbg_q"], poolA[:],
                    waits=[("VE", v_q)])

            vqr = E("vector", (lambda e: e.reduce_max(
                qmax[:], qmaxw[:], axis=AXX)), inc="VE")
            v_stg = dma("sync", "SG",
                        statg[:].rearrange("p (r h) -> p r h", r=8),
                        gathered.ap().rearrange(
                            "(r q) j -> r q j",
                            r=NCORES)[:, 2048, 0:8].unsqueeze(0),
                        waits=[("CC", 1)])
            E("vector", (lambda e: e.tensor_copy(statf[:], statg[:])),
                waits=[("SG", v_stg)])
            v_kg0 = E("vector", (lambda e: e.reduce_max(
                stat2[:, 0:1], statf[:], axis=AXX)), inc="VE")
            se_kg = E("scalar", (lambda e: e.activation(
                stat_s[0:1, 1:2], stat2[0:1, 0:1], AF.Copy)),
                waits=[("VE", v_kg0)], inc="SE")
            pbc = E("tensor", (lambda e: e.matmul(
                bslot[:, 0:1], onesrow, stat_s[:, 1:2], start=True, stop=True,
                skip_group_check=True)),
                waits=[("SE", se_kg), ("VE", norm_ve[-1])], inc="PE")
            vkmb = E("vector", (lambda e: e.tensor_copy(kmb[:],
                                                         bslot[:, 0:1])),
                     waits=[("PE", pbc)], inc="VE")
            po1 = E("gpsimd", (lambda e: e.tensor_tensor(
                negB[:], qmax[:], kmb[:], ALU.mult)),
                waits=[("VE", max(vqr, vkmb))], inc="PO")
            E("scalar", (lambda e: e.activation(
                kred[:], negB[:], AF.Ln)), waits=[("PO", po1)])
            v_b2 = E("scalar", (lambda e: e.activation(
                negB[:], kred[:], AF.Exp, scale=0.5, bias=lnc_t[:, 0:1])),
                inc="SE")
            v_negB = E("vector", (lambda e: e.tensor_scalar(
                negB[:], negB[:], 60.0, -1.0, ALU.min, ALU.mult)),
                waits=[("SE", v_b2)], inc="VE")
            if dbg:
                dbgst = sb("dbgst", [128, 8], F32)
                E("vector", lambda e: e.tensor_copy(dbgst[:, 0:1], negB[:]),
                  waits=[("VE", v_negB)])
                E("vector", lambda e: e.tensor_copy(dbgst[:, 1:2], qmax[:]))
                E("vector", lambda e: e.tensor_copy(dbgst[:, 2:3], kmax[:]))
                E("vector", lambda e: e.tensor_copy(dbgst[:, 3:4], kmb[:]))
                vdb = E("vector", lambda e: e.tensor_copy(
                    dbgst[0:1, 4:8], statf[:, 0:4]), inc="VE")
                dma("sync", "DBG", dbg_t["dbg_negB"], dbgst[:],
                    waits=[("VE", vdb)])
            if upto < 5:
                raise _Stop

            gat = gathered.ap().rearrange("(r q) j -> r q j", r=NCORES)
            Aq = poolA[:].rearrange("p (g t) -> p g t", g=NH)
            kb_r = [k[:].rearrange("p (r j) -> p r j", r=8) for k in kbuf]
            vb_r = [v[:].rearrange("p (r c) -> p r c", r=8) for v in vbuf]
            kksem = ["KK0", "KK1"]
            vvsem = ["VV0", "VV1"]
            obank = (pb[2], pb[3])

            pairs = []
            for t in range(QT):
                for kvh in range(NKV):
                    u = t * NKV + kvh
                    n = 4 * (t + 1)
                    for pi in range(n):
                        jb, h = pi // 4, pi % 4
                        pairs.append(dict(t=t, kvh=kvh, u=u, jb=jb,
                                          r0=2 * h, r1=2 * h + 1,
                                          first=(pi == 0), last=(pi == n - 1)))
            NPAIR = len(pairs)
            st_pe = [0] * NPAIR
            exp_seL = [0] * NPAIR
            av_peL = [0] * NPAIR
            kv_done = {}
            nload = [0]
            last_st_on_kpar = {0: 0, 1: 0}
            last_av_on_kpar = {0: 0, 1: 0}
            last_ev = [0]

            def kv_load(p):
                pr = pairs[p]
                key = (pr["u"], pr["jb"])
                if key in kv_done:
                    return kv_done[key]
                t, kvh, jb = pr["t"], pr["kvh"], pr["jb"]
                kpar = nload[0] % 2
                nload[0] += 1
                ksrc = gat[:, kvh * 128:kvh * 128 + 128,
                           jb * 128:jb * 128 + 128].transpose((1, 0, 2))
                vsrc = gat[:, 1024 + jb * 128:1024 + jb * 128 + 128,
                           kvh * 128:kvh * 128 + 128].transpose((1, 0, 2))
                v_k = dma("sync", kksem[kpar], kb_r[kpar], ksrc,
                          waits=[("PE", last_st_on_kpar[kpar]), ("CC", 1)])
                vw = [("PE", last_av_on_kpar[kpar])]
                if nload[0] <= 2:
                    vw.append(("VE", v_memset))
                v_v = dma("sync", vvsem[kpar], vb_r[kpar][:, :, 0:128], vsrc,
                          waits=vw)
                kv_done[key] = (kpar, v_k, v_v)
                return kv_done[key]

            def emit_stgroup(p):
                pr = pairs[p]
                sp = p % 3
                kpar, v_k, v_v = kv_load(p)
                pr["kpar"], pr["v_v"] = kpar, v_v
                stw = [(kksem[kpar], v_k), ("VE", v_q)]
                if p < 3:
                    stw.append(("VE", v_negB))
                if p >= 3:
                    stw.append(("SE", exp_seL[p - 3]))
                qrhs = Aq[:, 4 * pr["kvh"]:4 * pr["kvh"] + 4,
                          128 * pr["t"]:128 * pr["t"] + 128]
                diag = (pr["jb"] == pr["t"])
                vst = 0
                for h, r in ((0, pr["r0"]), (1, pr["r1"])):
                    lastmm = (h == 1) and not diag
                    vst = E("tensor", (lambda e, kpar=kpar, r=r, sp=sp, h=h,
                                       qrhs=qrhs, diag=diag: e.matmul(
                                           stq[sp][:, 512 * h:512 * h + 512],
                                           kb_r[kpar][:, r, :], qrhs,
                                           start=True, stop=not diag,
                                           skip_group_check=True)),
                            waits=stw if h == 0 else (),
                            inc="PE" if lastmm else None)
                    if diag:
                        vst = E("tensor", (lambda e, sp=sp, h=h, r=r: e.matmul(
                            stq[sp][:, 512 * h:512 * h + 512], eye,
                            masks_s[:, 512 * r:512 * r + 512],
                            start=False, stop=True, skip_group_check=True)),
                            inc="PE" if h == 1 else None)
                last_st_on_kpar[kpar] = vst
                st_pe[p] = vst

            def emit_exp(p):
                sp = p % 3
                ew = [("PE", st_pe[p]), ("VE", v_negB)]
                if p >= 3:
                    ew.append(("PE", av_peL[p - 3]))
                exp_seL[p] = E("scalar", (lambda e, sp=sp: e.activation(
                    ptb[sp][:], stq[sp][:], AF.Exp,
                    bias=negB[:, 0:1], scale=SCL)), waits=ew, inc="SE")

            def emit_avgroup(p):
                pr = pairs[p]
                sp = p % 3
                ob = obank
                kpar, v_v = pr["kpar"], pr["v_v"]
                avw = [("SE", exp_seL[p]), (vvsem[kpar], v_v)]
                if pr["first"]:
                    avw.append(("VE", last_ev[0]))
                vav = 0
                for h, r in ((0, pr["r0"]), (1, pr["r1"])):
                    for g in range(4):
                        st_flag = pr["first"] and h == 0 and (g % 2 == 0)
                        vav = E("tensor", (lambda e, sp=sp, h=h, r=r, g=g,
                                           ob=ob, kpar=kpar, st_flag=st_flag:
                                           e.matmul(
                                               ob[g // 2][:, 129 * (g % 2):
                                                          129 * (g % 2) + 129],
                                               ptb[sp][:, 512 * h + 128 * g:
                                                       512 * h + 128 * g + 128],
                                               vb_r[kpar][:, r, 0:129],
                                               start=st_flag, stop=False,
                                               skip_group_check=True)),
                                waits=avw if (h == 0 and g == 0) else (),
                                inc="PE" if (h == 1 and g == 3) else None)
                av_peL[p] = vav
                last_av_on_kpar[kpar] = vav
                if pr["last"]:
                    emit_evac(pr, vav)

            def emit_evac(pr, v_av_all):
                ob = obank
                t, kvh = pr["t"], pr["kvh"]
                E("scalar", (lambda e, ob=ob: e.activation(
                    rl[:, 0:2], ob[0][:, 128:259:129], AF.Ln)),
                    waits=[("PE", v_av_all)])
                E("scalar", (lambda e, ob=ob: e.activation(
                    rl[:, 2:4], ob[1][:, 128:259:129], AF.Ln)))
                v_l = E("scalar", (lambda e: e.activation(
                    rl[:], rl[:], AF.Exp, scale=-1.0)), inc="SE")
                vsc = 0
                for g in range(4):
                    gg = 4 * kvh + g
                    vsc = E("vector", (lambda e, ob=ob, g=g, gg=gg, t=t:
                                       e.tensor_scalar(
                                           Aq[:, gg, 128 * t:128 * t + 128],
                                           ob[g // 2][:, 129 * (g % 2):
                                                      129 * (g % 2) + 128],
                                           rl[:, g:g + 1], None, ALU.mult)),
                            waits=[("SE", v_l)] if g == 0 else (), inc="VE")
                last_ev[0] = vsc

            for pp in range(min(3, NPAIR)):
                emit_stgroup(pp)
            for p in range(NPAIR):
                emit_exp(p)
                if p + 3 < NPAIR:
                    emit_stgroup(p + 3)
                emit_avgroup(p)
            v_attn = last_ev[0]

            trbit = stp1[:].bitcast(BF16)
            tslots = [trbit[:, 0:128], trbit[:, 1024:1152]]
            tr_ve = []
            for t in range(QT):
                for g in range(NH):
                    idx = len(tr_ve)
                    tw = []
                    if idx == 0:
                        tw.append(("VE", v_attn))
                    if idx >= 2:
                        tw.append(("VE", tr_ve[idx - 2]))
                    ptv = E("tensor", (lambda e, g=g, t=t, idx=idx:
                                       e.transpose(
                                           tslots[idx % 2],
                                           Aq[:, g, 128 * t:128 * t + 128],
                                           eye)), waits=tw, inc="PE")
                    tv = E("vector", (lambda e, g=g, t=t, idx=idx:
                                      e.tensor_copy(
                                          Br[:, g, 128 * t:128 * t + 128],
                                          tslots[idx % 2])),
                           waits=[("PE", ptv)], inc="VE")
                    tr_ve.append(tv)
            v_attn = tr_ve[-1]

            if dbg:
                dma("sync", "DBG", dbg_t["dbg_attnT"], poolB[:],
                    waits=[("VE", v_attn)])
            if upto < 6:
                raise _Stop

            woT_r = woT.rearrange("(c p) o -> p c o", c=IC)
            xT_rr = xT.rearrange("(c p) t -> c p t", c=IC)
            xrel = [scr8[:, 0:512], scr8[:, 512:1024]]
            xrsem = ["XR0", "XR1"]
            x1f = fscr[:, 0:512]
            x1f_ve = []
            x1_st = []
            wpar_lu = {0: sch.val("PE"), 1: sch.val("PE")}
            v_ln2pe = 0
            for oc in range(IC):
                wpar = oc % 2
                v_w = dma("gpsimd", wsem[wpar], wbr[wpar],
                          woT_r[:, :, 128 * oc:128 * oc + 128],
                          waits=[("PE", wpar_lu[wpar])])
                for tt in range(2):
                    g = oc * 2 + tt
                    par = tt
                    xpar = g % 2
                    xw = []
                    if g >= 2:
                        xw.append(("VE", x1f_ve[g - 2]))
                    v_x = dma("sync", xrsem[xpar], xrel[xpar],
                              xT_rr[oc, :, 512 * tt:512 * tt + 512], waits=xw)
                    mw = [(wsem[wpar], v_w), ("VE", v_attn)]
                    if g >= 2:
                        mw.append(("VE", x1f_ve[g - 2]))
                    for ic in range(IC):
                        v_pe = E("tensor", (lambda e, ic=ic, tt=tt, par=par,
                                            wpar=wpar: e.matmul(
                                                pb[par][:],
                                                wbr[wpar][:, ic, :],
                                                Br[:, ic,
                                                   512 * tt:512 * tt + 512],
                                                start=(ic == 0),
                                                stop=(ic == IC - 1),
                                                skip_group_check=True)),
                                 waits=mw if ic == 0 else (),
                                 inc="PE" if ic == IC - 1 else None)
                    fw = [("PE", v_pe), (xrsem[xpar], v_x)]
                    if x1_st:
                        fw.append(("ST1", x1_st[-1]))
                    E("vector", (lambda e, par=par, xpar=xpar:
                                 e.tensor_tensor(
                                     x1f, pb[par][:], xrel[xpar], ALU.add)),
                        waits=fw)
                    vx1 = E("vector", lambda e: e.tensor_copy(
                        fscr[:, 512:1024], x1f), inc="VE")
                    vst = dma("sync", "ST1",
                              x1d[128 * oc:128 * oc + 128,
                                  512 * tt:512 * tt + 512],
                              fscr[:, 512:1024], waits=[("VE", vx1)])
                    x1_st.append(vst)
                    E("vector", (lambda e, oc=oc, tt=tt: e.tensor_copy(
                        Ar[:, oc, 512 * tt:512 * tt + 512], x1f)))
                    vsq = E("vector", (lambda e, oc=oc, tt=tt, xpar=xpar:
                                       e.tensor_tensor(
                                           xsq[xpar],
                                           Ar[:, oc,
                                              512 * tt:512 * tt + 512],
                                           Ar[:, oc,
                                              512 * tt:512 * tt + 512],
                                           ALU.mult)), inc="VE")
                    x1f_ve.append(vsq)
                    E("tensor", (lambda e, tt=tt, oc=oc: e.matmul(
                        pb[2 + tt][:], ones,
                        Ar[:, oc, 512 * tt:512 * tt + 512],
                        start=(oc == 0), stop=(oc == IC - 1),
                        skip_group_check=True)), waits=[("VE", vsq)])
                    v_ln2pe = E("tensor", (lambda e, tt=tt, oc=oc, xpar=xpar:
                                           e.matmul(
                                               pb[4 + tt][:], ones,
                                               xsq[xpar],
                                               start=(oc == 0),
                                               stop=(oc == IC - 1),
                                               skip_group_check=True)),
                                inc="PE")
                wpar_lu[wpar] = sch.val("PE")
            if dbg:
                dma("sync", "DBG", dbg_t["dbg_x1"], x1d[:, :],
                    waits=[("ST1", x1_st[-1])])
            if upto < 7:
                raise _Stop

            v_ln2 = 0
            ln2_chunks = {}
            for tt in range(2):
                v_ln2, ln2_chunks[tt] = ln_finalize_and_apply(
                    tt, 2 + tt, 4 + tt, Ar, Br, v_ln2pe)
            if dbg:
                dma("sync", "DBG", dbg_t["dbg_ln2"], poolB[:],
                    waits=[("VE", v_ln2)])
            if upto < 8:
                raise _Stop

            u1pad = upads[:, 0:1024]
            u3pad = upads[:, 1024:2048]
            u2pad = upads[:, 2048:3072]
            a1r = a1s[:].rearrange("p (c r) -> p c r", c=IC)
            a3r = a3s[:].rearrange("p (c r) -> p c r", c=IC)
            v_use = 0
            for tt in range(2):
                for (ar, upad) in ((a1r, u1pad), (a3r, u3pad)):
                    for ic in range(IC):
                        w = [("VE", ln2_chunks[tt][ic]), ("W0", W0F)]
                        if v_use:
                            w.append(("SE", v_use))
                        pu = E("tensor", (lambda e, ic=ic, tt=tt, ar=ar:
                                          e.matmul(
                                              nrm[0:2, :], ar[:, ic, :],
                                              Br[:, ic,
                                                 512 * tt:512 * tt + 512],
                                              start=(ic == 0),
                                              stop=(ic == IC - 1),
                                              skip_group_check=True)),
                               waits=w,
                               inc="PE" if ic == IC - 1 else None)
                    v_use = E("scalar", (lambda e, tt=tt, upad=upad:
                                         e.activation(
                                             upad[0:2,
                                                  512 * tt:512 * tt + 512],
                                             nrm[0:2, :], AF.Copy,
                                             scale=0.5)),
                              waits=[("PE", pu)], inc="SE")
            hs = poolA[:, 0:8192].rearrange("p (c t) -> p c t", c=8)
            v_hs = 0
            for j in range(8):
                for tt in range(2):
                    E("vector", (lambda e, j=j, tt=tt: e.tensor_tensor(
                        fscr[:, 0:512], Br[:, j, 512 * tt:512 * tt + 512],
                        Br[:, j + 8, 512 * tt:512 * tt + 512], ALU.add)),
                        waits=[("VE", v_ln2)])
                    E("vector", (lambda e, j=j, tt=tt: e.tensor_tensor(
                        fscr[:, 512:1024],
                        Br[:, j + 16, 512 * tt:512 * tt + 512],
                        Br[:, j + 24, 512 * tt:512 * tt + 512], ALU.add)))
                    v_hs = E("vector", (lambda e, j=j, tt=tt:
                                        e.tensor_tensor(
                                            hs[:, j,
                                               512 * tt:512 * tt + 512],
                                            fscr[:, 0:512],
                                            fscr[:, 512:1024], ALU.add)),
                             inc="VE")

            w1T_r = w1T.rearrange("(c p) h -> p c h", c=9)
            w3T_r = w3T.rearrange("(c p) h -> p c h", c=9)
            gsv = poolB[:, 0:FC * 1024].rearrange("p (c t) -> p c t", c=FC)
            gcs = [[bscr[:, 512 * (2 * f + tt):512 * (2 * f + tt) + 512]
                    for tt in range(2)] for f in range(4)]
            sg = ptb[0][:, 0:512]
            a2r = a2s[:].rearrange("p (c r) -> p c r", c=HC)
            w1b = [w[:, 0:1152].rearrange("p (c h) -> p c h", c=9)
                   for w in wb]
            w3b = [w[:, 1152:2304].rearrange("p (c h) -> p c h", c=9)
                   for w in wb]
            silu_se = []
            gc_ve = []
            u2_pe = {}
            wpar_last = dict(wpar_lu)
            u2_started = {0: False, 1: False}
            v_gs = 0
            for p in range(FC):
                for f in range(4):
                    hc = 22 * f + p
                    wpar = (4 * p + f) % 2
                    dma("gpsimd", wsem[wpar], w1b[wpar],
                        w1T_r[:, :, 128 * hc:128 * hc + 128],
                        waits=[("PE", wpar_last[wpar])])
                    v_w3 = dma("gpsimd", wsem[wpar], w3b[wpar],
                               w3T_r[:, :, 128 * hc:128 * hc + 128])
                    for tt in range(2):
                        i_gu = len(silu_se)
                        gbank = [pb[0], nrm][i_gu % 2]
                        ubank = [pb[1], bslot][i_gu % 2]
                        sgb = [sg, ptb[0][:, 512:1024]][i_gu % 2]
                        mw = [(wsem[wpar], v_w3), ("VE", v_hs)]
                        if v_use:
                            mw.append(("SE", v_use))
                        if i_gu >= 2:
                            mw.append(("SE", silu_se[i_gu - 2]))
                        for ic in range(9):
                            rhs = (hs[:, ic, 512 * tt:512 * tt + 512]
                                   if ic < 8
                                   else u1pad[:, 512 * tt:512 * tt + 512])
                            E("tensor", (lambda e, ic=ic, wpar=wpar, rhs=rhs,
                                         gbank=gbank: e.matmul(
                                             gbank, w1b[wpar][:, ic, :],
                                             rhs, start=(ic == 0),
                                             stop=(ic == 8),
                                             skip_group_check=True)),
                                waits=mw if ic == 0 else ())
                        uw = [("VE", gc_ve[i_gu - 2])] if i_gu >= 2 else []
                        for ic in range(9):
                            rhs = (hs[:, ic, 512 * tt:512 * tt + 512]
                                   if ic < 8
                                   else u3pad[:, 512 * tt:512 * tt + 512])
                            v_gu = E("tensor", (lambda e, ic=ic, wpar=wpar,
                                                rhs=rhs, ubank=ubank:
                                                e.matmul(
                                                    ubank,
                                                    w3b[wpar][:, ic, :],
                                                    rhs, start=(ic == 0),
                                                    stop=(ic == 8),
                                                    skip_group_check=True)),
                                     waits=uw if ic == 0 else (),
                                     inc="PE" if ic == 8 else None)
                        sse = E("scalar", (lambda e, gbank=gbank, sgb=sgb:
                                           e.activation(
                                               sgb, gbank, AF.Silu)),
                            waits=[("PE", v_gu)], inc="SE")
                        silu_se.append(sse)
                        gw = [("SE", sse)]
                        if (f, tt) in u2_pe:
                            gw.append(("PE", u2_pe[(f, tt)]))
                        vgc = E("vector", (lambda e, f=f, tt=tt, sgb=sgb,
                                           ubank=ubank: e.tensor_tensor(
                                               gcs[f][tt], sgb, ubank,
                                               ALU.mult)),
                                waits=gw, inc="VE")
                        gc_ve.append(vgc)
                        u2w = [("VE", vgc), ("W0", W0F)]
                        if not u2_started[tt]:
                            u2w.append(("VE", v_ln2))
                        pu2 = E("tensor", (lambda e, hc=hc, f=f, tt=tt,
                                           s=not u2_started[tt]: e.matmul(
                                               pb[2 + tt][0:2, :],
                                               a2r[:, hc, :], gcs[f][tt],
                                               start=s,
                                               stop=(hc == HC - 1),
                                               skip_group_check=True)),
                                 waits=u2w, inc="PE")
                        u2_started[tt] = True
                        u2_pe[(f, tt)] = pu2
                    wpar_last[wpar] = sch.val("PE")
                for tt in range(2):
                    E("vector", (lambda e, tt=tt: e.tensor_tensor(
                        fscr[:, 0:512], gcs[0][tt], gcs[1][tt], ALU.add)),
                        waits=[("PE", u2_pe[(3, tt)])])
                    E("vector", (lambda e, tt=tt: e.tensor_tensor(
                        fscr[:, 512:1024], gcs[2][tt], gcs[3][tt], ALU.add)))
                    v_gs = E("vector", (lambda e, p=p, tt=tt:
                                        e.tensor_tensor(
                                            gsv[:, p,
                                                512 * tt:512 * tt + 512],
                                            fscr[:, 0:512],
                                            fscr[:, 512:1024], ALU.add)),
                             inc="VE")
            E("scalar", (lambda e: e.activation(
                u2pad[0:2, 0:512], pb[2][0:2, :], AF.Copy, scale=0.5)),
                waits=[("PE", u2_pe[(3, 0)])])
            v_u2b = E("scalar", (lambda e: e.activation(
                u2pad[0:2, 512:1024], pb[3][0:2, :], AF.Copy, scale=0.5)),
                waits=[("PE", u2_pe[(3, 1)])], inc="SE")
            if dbg:
                dma("sync", "DBG", dbg_t["dbg_gs"], poolB[:, 0:FC * 1024],
                    waits=[("VE", v_gs)])

            w2T_r = w2T.rearrange("(c p) o -> p c o", c=23)
            w2b = [w[:, 0:2944].rearrange("p (c o) -> p c o", c=23)
                   for w in wb]

            x2f = fscr[:, 512:1024]
            y_ve = []
            y_st = []
            st1_full = x1_st[-1]
            wpar_last = {0: sch.val("PE"), 1: sch.val("PE")}
            for oc in range(IC):
                wpar = oc % 2
                v_w = dma("gpsimd", wsem[wpar], w2b[wpar],
                          w2T_r[:, :, 128 * oc:128 * oc + 128],
                          waits=[("PE", wpar_last[wpar])])
                for tt in range(2):
                    g = oc * 2 + tt
                    par = tt
                    xpar = g % 2
                    xw = [("ST1", st1_full)]
                    if g >= 2:
                        xw.append(("VE", y_ve[g - 2]))
                    v_x = dma("sync", xrsem[xpar], xrel[xpar],
                              x1d[128 * oc:128 * oc + 128,
                                  512 * tt:512 * tt + 512], waits=xw)
                    mw = [(wsem[wpar], v_w), ("VE", v_gs), ("SE", v_u2b)]
                    if g >= 2:
                        mw.append(("VE", y_ve[g - 2]))
                    for ic in range(23):
                        rhs = (gsv[:, ic, 512 * tt:512 * tt + 512]
                               if ic < 22
                               else u2pad[:, 512 * tt:512 * tt + 512])
                        v_pe = E("tensor", (lambda e, ic=ic, par=par,
                                            wpar=wpar, rhs=rhs: e.matmul(
                                                pb[par][:],
                                                w2b[wpar][:, ic, :], rhs,
                                                start=(ic == 0),
                                                stop=(ic == 22),
                                                skip_group_check=True)),
                                 waits=mw if ic == 0 else (),
                                 inc="PE" if ic == 22 else None)
                    fw = [("PE", v_pe), (xrsem[xpar], v_x)]
                    if y_st:
                        fw.append(("ST2", y_st[-1]))
                    vy = E("vector", (lambda e, par=par, xpar=xpar:
                                      e.tensor_tensor(
                                          x2f, pb[par][:], xrel[xpar],
                                          ALU.add)), waits=fw, inc="VE")
                    y_ve.append(vy)
                    vst = dma("sync", "ST2",
                              out[128 * oc:128 * oc + 128,
                                  512 * tt:512 * tt + 512],
                              x2f, waits=[("VE", vy)])
                    y_st.append(vst)
                wpar_last[wpar] = sch.val("PE")

        try:
            _sched_all()
        except _Stop:
            pass

        all_dma_sems = [s for s in sch.ctr
                        if s not in ("PE", "VE", "SE", "PO", "CC")]
        E("gpsimd", lambda e: None,
          waits=[(s, sch.val(s)) for s in all_dma_sems])

        semnames = sorted({s for ops in sch.ops.values()
                           for (ws, _, inc, _) in ops
                           for s in ([w[0] for w in ws] +
                                     ([inc] if inc else []))})
        sems = {s: ctx.enter_context(nc.semaphore(s)) for s in semnames}
        block = ctx.enter_context(nc.Block())

        def mk(name):
            def f(eng):
                seen = {}
                for waits, fn, inc, inc_n in sch.ops[name]:
                    for (s, v) in waits:
                        if v > 0 and seen.get(s, 0) < v:
                            eng.wait_ge(sems[s], v)
                            seen[s] = v
                    instr = fn(eng)
                    if inc is not None:
                        instr.then_inc(sems[inc], inc_n)
            return f

        block.sync(mk("sync"))
        block.gpsimd(mk("gpsimd"))
        block.tensor(mk("tensor"))
        block.vector(mk("vector"))
        block.scalar(mk("scalar"))

    return nc



def _stage(inputs):
    x = np.asarray(inputs["L0_attn_hidden_states"])[0]
    Wq = np.asarray(inputs["L0_attn_Wq"])
    Wk = np.asarray(inputs["L0_attn_Wk"])
    Wv = np.asarray(inputs["L0_attn_Wv"])
    Wo = np.asarray(inputs["L0_attn_Wo"])
    w1q = np.asarray(inputs["L0_ffn_w1q"])
    a1 = np.asarray(inputs["L0_ffn_a1"])
    b1 = np.asarray(inputs["L0_ffn_b1"])
    w3q = np.asarray(inputs["L0_ffn_w3q"])
    a3 = np.asarray(inputs["L0_ffn_a3"])
    b3 = np.asarray(inputs["L0_ffn_b3"])
    w2q = np.asarray(inputs["L0_ffn_w2q"])
    a2 = np.asarray(inputs["L0_ffn_a2"])
    b2 = np.asarray(inputs["L0_ffn_b2"])

    C = np.ascontiguousarray
    f32 = np.float32

    wqT = C(Wq.T.astype(f32))
    wkT = C(Wk.T.astype(f32))
    wvT = C(Wv.T.astype(f32))
    woT = C(Wo.T.astype(f32))

    def pad_gate(wT, bT):
        outw = np.zeros((1152, HIDP), f32)
        for f in range(4):
            outw[0:1024, 2816 * f:2816 * f + 2752] = \
                wT[:, 2752 * f:2752 * f + 2752]
            outw[1024:1026, 2816 * f:2816 * f + 2752] = \
                bT[:, 2752 * f:2752 * f + 2752]
        return outw

    w1T = pad_gate(C(w1q.T.astype(f32)), C(b1.T.astype(f32)))
    w3T = pad_gate(C(w3q.T.astype(f32)), C(b3.T.astype(f32)))

    w2T = np.zeros((W2K + 128, D), f32)
    w2T[0:2752, :] = w2q.T.astype(f32)
    w2T[W2K:W2K + 2, :] = b2.T.astype(f32)

    def a_small(a, nch):
        acx = a.T.astype(f32).reshape(nch, 128, a.shape[0])
        return C(acx.transpose(1, 0, 2).reshape(128, nch * a.shape[0]))

    a1cv = a_small(a1, IC)
    a3cv = a_small(a3, IC)
    a2p = np.zeros((2, HIDP), f32)
    for f in range(4):
        a2p[:, 2816 * f:2816 * f + 2752] = a2[:, 2752 * f:2752 * f + 2752]
    a2cv = a_small(a2p, HC)

    inv = (1.0 / (10000.0 ** (np.arange(0, HD, 2, dtype=np.float64) / HD)))
    bf = ml_dtypes.bfloat16

    eye_ones = np.zeros((128, 256), f32)
    eye_ones[:, 0:128] = np.eye(128)
    eye_ones[:, 128:256] = 1.0
    consts_v = eye_ones.astype(bf)

    in_maps = []
    for c in range(NCORES):
        xc = C(x[c::8, :].T.astype(f32))
        pos = (c + 8 * np.arange(R)).astype(np.float64)
        fr = np.outer(pos, inv)
        emb = np.concatenate([fr, fr], axis=1)
        cosv = C(np.cos(emb).T.astype(f32))
        sinv = np.sin(emb).T.astype(f32)
        sins = sinv.copy()
        sins[0:64, :] = -sinv[0:64, :]
        sins = C(sins)

        jj = np.arange(128)[:, None]
        mmi = np.arange(128)[None, :]
        allow = (jj < mmi)
        mrow = []
        for rr in range(8):
            al = allow | ((jj == mmi) & (rr <= c))
            mk = np.where(al, 0.0, -1e9).astype(f32)
            mrow.append(np.tile(mk, (1, 4)))
        maskv = np.concatenate(mrow, axis=1).astype(bf)

        in_maps.append({
            "xT": xc, "wqT": wqT, "wkT": wkT, "wvT": wvT, "woT": woT,
            "w1T": w1T, "w3T": w3T, "w2T": w2T,
            "a1c": a1cv, "a3c": a3cv, "a2c": a2cv,
            "cosq": cosv, "sinq": sins,
            "masks": maskv, "consts": consts_v,
        })
    return in_maps


def kernel(**inputs):
    in_maps = _stage(inputs)
    nc = build(dbg=False)
    res = run_bass_kernel_spmd(nc, in_maps, core_ids=list(range(NCORES)))
    full = np.empty((1, S, D), dtype=np.float32)
    for c in range(NCORES):
        full[0, c::8, :] = np.asarray(res.results[c]["out"],
                                      dtype=np.float32).T
    return full


# version 07
# speedup vs baseline: 1.0926x; 1.0130x over previous
import math
import numpy as np
import ml_dtypes
import concourse.bass as bass
import concourse.mybir as mybir
from concourse.bass_utils import run_bass_kernel_spmd
from contextlib import ExitStack

F32 = mybir.dt.float32
BF16 = mybir.dt.bfloat16
AF = mybir.ActivationFunctionType
ALU = mybir.AluOpType
AXX = mybir.AxisListType.X

NCORES = 8
S = 8192
D = 4096
NH = 32
NKV = 8
HD = 128
R = 1024
IC = 32
QT = 8
HIDP = 11264
HC = 88
FC = 22
W2K = 2816
RPC = 2049
SCL = 1.0 / math.sqrt(HD)
LNC_BIAS = -0.5 * math.log(HD)
ENGINES = ("sync", "gpsimd", "tensor", "vector", "scalar")


class Sched:
    def __init__(self):
        self.ops = {e: [] for e in ENGINES}
        self.ctr = {}

    def val(self, sem):
        return self.ctr.get(sem, 0)

    def emit(self, eng, fn, waits=(), inc=None, inc_n=1):
        post = None
        if inc is not None:
            self.ctr[inc] = self.ctr.get(inc, 0) + inc_n
            post = self.ctr[inc]
        self.ops[eng].append((tuple(waits), fn, inc, inc_n))
        return post


class _Stop(Exception):
    pass


def build(dbg=False, upto=99):
    nc = bass.Bass("TRN2", target_bir_lowering=False)

    def din(name, shape, dt=F32):
        return nc.dram_tensor(name, shape, dt, kind="ExternalInput").ap()

    xT = din("xT", [D, R])
    wqT = din("wqT", [D, D])
    wkT = din("wkT", [D, NKV * HD])
    wvT = din("wvT", [D, NKV * HD])
    woT = din("woT", [D, D])
    w1T = din("w1T", [1024 + 128, HIDP])
    w3T = din("w3T", [1024 + 128, HIDP])
    w2T = din("w2T", [W2K + 128, D])
    a1c = din("a1c", [128, IC * 2])
    a3c = din("a3c", [128, IC * 2])
    a2c = din("a2c", [128, HC * 2])
    cosq = din("cosq", [128, R])
    sinq = din("sinq", [128, R])
    masks = din("masks", [128, 8 * 512], BF16)
    consts = din("consts", [128, 256], BF16)

    out = nc.dram_tensor("out", [D, R], F32, kind="ExternalOutput").ap()

    contrib = nc.dram_tensor("contrib", [RPC, 1024], BF16)
    gathered = nc.dram_tensor("gathered", [NCORES * RPC, 1024], BF16)
    x1d = nc.dram_tensor("x1d", [D, R], F32)

    dbg_t = {}
    if dbg:
        for nm, shp, dt in [
            ("dbg_ln1", [128, IC * 1024], BF16),
            ("dbg_q", [128, IC * 1024], BF16),
            ("dbg_contrib", [RPC, 1024], BF16),
            ("dbg_negB", [128, 8], F32),
            ("dbg_attnT", [128, IC * 1024], BF16),
            ("dbg_x1", [D, R], F32),
            ("dbg_ln2", [128, IC * 1024], BF16),
            ("dbg_gs", [128, FC * 1024], BF16),
        ]:
            dbg_t[nm] = nc.dram_tensor(nm, shp, dt, kind="ExternalOutput").ap()

    sch = Sched()
    with ExitStack() as ctx:
        def sb(name, shape, dt):
            return ctx.enter_context(nc.sbuf_tensor(name, shape, dt))

        def pst(name, shape, dt):
            return ctx.enter_context(nc.psum_tensor(name, shape, dt))

        poolA = sb("poolA", [128, 32768], BF16)
        poolB = sb("poolB", [128, 32768], BF16)
        wbufA = sb("wbufA", [128, 4096], BF16)
        wbufB = sb("wbufB", [128, 4096], BF16)
        masks_s = sb("masks_s", [128, 4096], BF16)
        consts_s = sb("consts_s", [128, 256], BF16)
        scr8 = sb("scr8", [128, 2048], F32)
        fscr = sb("fscr", [128, 1024], F32)
        bscr = sb("bscr", [128, 4096], BF16)
        upads = sb("upads", [128, 3072], BF16)
        kbuf = [sb(f"kbuf{i}", [128, 1024], BF16) for i in range(2)]
        vbuf = [sb(f"vbuf{i}", [128, 8 * 136], BF16) for i in range(2)]
        ptb = [sb(f"ptb{i}", [128, 1024], BF16) for i in range(3)]
        m_s = sb("m_s", [128, 512], F32)
        r_s = sb("r_s", [128, 512], F32)
        mb16 = sb("mb16", [128, 512], BF16)
        a1s = sb("a1s", [128, IC * 2], BF16)
        a3s = sb("a3s", [128, IC * 2], BF16)
        a2s = sb("a2s", [128, HC * 2], BF16)
        qmaxw = sb("qmaxw", [128, 512], F32)
        kmaxw = sb("kmaxw", [128, 512], F32)
        qmax = sb("qmax", [128, 1], F32)
        kmax = sb("kmax", [128, 1], F32)
        kred = sb("kred", [128, 1], F32)
        kmb = sb("kmb", [128, 1], F32)
        negB = sb("negB", [128, 1], F32)
        lnc_t = sb("lnc_t", [128, 1], F32)
        rl = sb("rl", [128, 4], F32)
        stat_s = sb("stat_s", [1, 16], BF16)
        stat2 = sb("stat2", [1, 4], F32)
        epst = sb("epst", [128, 1], F32)
        statg = sb("statg", [1, 64], BF16)
        kst = sb("kst", [128, 1024], BF16)
        statf = sb("statf", [1, 64], F32)

        stp0 = pst("stp0", [128, 1024], F32)
        stp1 = pst("stp1", [128, 1024], F32)
        ob01 = pst("ob01", [128, 1024], F32)
        ob23 = pst("ob23", [128, 1024], F32)
        pb = [stp0[:, 0:512], stp0[:, 512:1024],
              ob01[:, 0:512], ob01[:, 512:1024],
              ob23[:, 0:512], ob23[:, 512:1024]]
        stq = [stp0[:], stp1[:], ob23[:]]
        nrm = stp1[:, 0:512]
        bslot = stp1[:, 512:1024]

        eye = consts_s[:, 0:128]
        ones = consts_s[:, 128:256]
        onesrow = consts_s[0:1, 128:256]

        Ar = poolA[:].rearrange("p (c t) -> p c t", c=IC)
        Br = poolB[:].rearrange("p (c t) -> p c t", c=IC)

        E = sch.emit

        def dma(eng, sem_, o, i, waits=()):
            return E(eng, (lambda e, o=o, i=i: e.dma_start(out=o, in_=i)),
                     waits=waits, inc=sem_, inc_n=16)

        def _sched_all():
            xT_r = xT.rearrange("(c p) t -> p c t", c=IC)
            for q4 in range(4):
                dma("gpsimd", f"XB{q4}", Ar[:, 8 * q4:8 * q4 + 8, :],
                    xT_r[:, 8 * q4:8 * q4 + 8, :])
            dma("gpsimd", "W0", consts_s[:], consts)
            dma("gpsimd", "W0", masks_s[:], masks)
            dma("gpsimd", "W0", a1s[:], a1c)
            dma("gpsimd", "W0", a3s[:], a3c)
            dma("gpsimd", "W0", a2s[:], a2c)
            W0F = sch.val("W0")
            dma("sync", "K0", scr8[:, 0:1024], cosq)
            dma("sync", "K0", scr8[:, 1024:2048], sinq)
            K0F = sch.val("K0")
            cos = scr8[:, 0:1024]
            sin = scr8[:, 1024:2048]

            E("vector", lambda e: e.memset(epst[:], 1e-5))
            E("vector", lambda e: e.memset(lnc_t[:], LNC_BIAS))
            E("vector", lambda e: e.memset(qmaxw[:], 0.0))
            E("vector", lambda e: e.memset(kmaxw[:], 0.0))
            E("vector", lambda e: e.memset(upads[:], 0.0))
            E("vector", lambda e: e.memset(stat_s[:], 0.0))
            E("vector", lambda e: e.memset(vbuf[0][:], 1.0))
            v_memset = E("vector", lambda e: e.memset(vbuf[1][:], 1.0),
                         inc="VE")

            xsq = [bscr[:, 0:512], bscr[:, 512:1024]]
            for tt in range(2):
                for ic in range(IC):
                    w = [(f"XB{ic // 8}", 16)]
                    if ic == 0 and tt == 0:
                        w.append(("W0", W0F))
                    E("tensor", (lambda e, ic=ic, tt=tt: e.matmul(
                        pb[tt][:], ones, Ar[:, ic, 512 * tt:512 * tt + 512],
                        start=(ic == 0), stop=(ic == IC - 1),
                        skip_group_check=True)), waits=w)
            sq_pe = []
            for tt in range(2):
                for ic in range(IC):
                    g = tt * IC + ic
                    par = g % 2
                    w = [(f"XB{ic // 8}", 16)]
                    if g >= 2:
                        w.append(("PE", sq_pe[g - 2]))
                    vv = E("vector", (lambda e, ic=ic, tt=tt, par=par:
                                      e.tensor_tensor(
                                          xsq[par],
                                          Ar[:, ic, 512 * tt:512 * tt + 512],
                                          Ar[:, ic, 512 * tt:512 * tt + 512],
                                          ALU.mult)), waits=w, inc="VE")
                    pv = E("tensor", (lambda e, ic=ic, tt=tt, par=par: e.matmul(
                        pb[2 + tt][:], ones, xsq[par],
                        start=(ic == 0), stop=(ic == IC - 1),
                        skip_group_check=True)), waits=[("VE", vv)], inc="PE")
                    sq_pe.append(pv)
            v_stats_pe = sq_pe[-1]

            def ln_finalize_and_apply(tt, bank_m, bank_sq, src_r, dst_r,
                                      pe_wait):
                E("vector", (lambda e: e.tensor_scalar(
                    m_s[:], pb[bank_m][:], 1.0 / D, None, ALU.mult)),
                    waits=[("PE", pe_wait)])
                E("vector", (lambda e: e.tensor_scalar(
                    fscr[:, 0:512], pb[bank_sq][:], 1.0 / D, None, ALU.mult)))
                E("vector", (lambda e: e.tensor_tensor(
                    r_s[:], m_s[:], m_s[:], ALU.mult)))
                E("vector", (lambda e: e.tensor_tensor(
                    fscr[:, 0:512], fscr[:, 0:512], r_s[:], ALU.subtract)))
                vw = E("vector", lambda e: e.tensor_copy(mb16[:], m_s[:]),
                       inc="VE")
                E("scalar", (lambda e: e.activation(
                    fscr[:, 512:1024], fscr[:, 0:512], AF.Ln,
                    bias=epst[:, 0:1])), waits=[("VE", vw)])
                sw = E("scalar", (lambda e: e.activation(
                    r_s[:], fscr[:, 512:1024], AF.Exp, scale=-0.5)), inc="SE")
                last = 0
                chunk_incs = []
                for ic in range(IC):
                    E("vector", (lambda e, ic=ic: e.tensor_tensor(
                        fscr[:, 0:512], src_r[:, ic, 512 * tt:512 * tt + 512],
                        mb16[:], ALU.subtract)), waits=[("SE", sw)])
                    last = E("vector", (lambda e, ic=ic: e.tensor_tensor(
                        dst_r[:, ic, 512 * tt:512 * tt + 512], fscr[:, 0:512],
                        r_s[:], ALU.mult)), inc="VE")
                    chunk_incs.append(last)
                return last, chunk_incs

            v_ln1 = 0
            for tt in range(2):
                v_ln1, _ = ln_finalize_and_apply(tt, tt, 2 + tt, Ar, Br,
                                                 v_stats_pe)

            if dbg:
                dma("sync", "DBG", dbg_t["dbg_ln1"], poolB[:],
                    waits=[("VE", v_ln1)])
            if upto < 2:
                raise _Stop

            wsem = ["WA", "WB"]
            wkT_r = wkT.rearrange("(c p) d -> p c d", c=IC)
            wvT_r = wvT.rearrange("(c p) d -> p c d", c=IC)
            wb = [wbufA, wbufB]
            wbr = [w[:].rearrange("p (c o) -> p c o", c=IC) for w in wb]
            rope_ve = []
            norm_pe = []
            norm_ve = []
            wpar_last_use = {0: 0, 1: 0}

            def proj_rope_block(tt, par, dst_ap, v_w, wpar, is_q,
                                extra_mm_waits=()):
                gidx = len(rope_ve)
                w = [(wsem[wpar], v_w), ("K0", K0F), ("VE", v_ln1)]
                w += list(extra_mm_waits)
                if gidx >= 2:
                    w.append(("VE", rope_ve[gidx - 2]))
                for ic in range(IC):
                    pk = E("tensor", (lambda e, ic=ic, tt=tt, par=par,
                                      wpar=wpar: e.matmul(
                                          pb[4 + par][:], wbr[wpar][:, ic, :],
                                          Br[:, ic, 512 * tt:512 * tt + 512],
                                          start=(ic == 0), stop=(ic == IC - 1),
                                          skip_group_check=True)),
                           waits=w if ic == 0 else (),
                           inc="PE" if ic == IC - 1 else None)
                sqw = [("PE", pk)]
                if norm_pe:
                    sqw.append(("PE", norm_pe[-1]))
                se = E("scalar", (lambda e, par=par: e.activation(
                    bscr[:, 0:512], pb[4 + par][:], AF.Square)),
                    waits=sqw, inc="SE")
                nmw = [("SE", se)]
                if norm_ve:
                    nmw.append(("VE", norm_ve[-1]))
                pn = E("tensor", (lambda e: e.matmul(
                    nrm, ones, bscr[:, 0:512], start=True, stop=True,
                    skip_group_check=True)), waits=nmw, inc="PE")
                norm_pe.append(pn)
                tgtw = qmaxw if is_q else kmaxw
                nv = E("vector", (lambda e, tgtw=tgtw: e.tensor_tensor(
                    tgtw[:], tgtw[:], nrm, ALU.max)),
                    waits=[("PE", pn)], inc="VE")
                norm_ve.append(nv)
                E("vector", (lambda e, par=par, tt=tt: e.tensor_tensor(
                    fscr[:, 0:512], pb[4 + par][:],
                    cos[:, 512 * tt:512 * tt + 512], ALU.mult)),
                    waits=[("SE", se)])
                E("vector", (lambda e, par=par, tt=tt: e.tensor_tensor(
                    fscr[0:64, 512:1024], pb[4 + par][64:128, :],
                    sin[0:64, 512 * tt:512 * tt + 512], ALU.mult)))
                E("vector", (lambda e, par=par, tt=tt: e.tensor_tensor(
                    fscr[64:128, 512:1024], pb[4 + par][0:64, :],
                    sin[64:128, 512 * tt:512 * tt + 512], ALU.mult)))
                rv = E("vector", (lambda e, dst_ap=dst_ap: e.tensor_tensor(
                    dst_ap, fscr[:, 0:512], fscr[:, 512:1024], ALU.add)),
                    inc="VE")
                rope_ve.append(rv)
                return rv

            kwv = {}
            for kc in range(2):
                kwv[kc] = dma("gpsimd", wsem[kc % 2], wbr[kc % 2],
                              wkT_r[:, :, 128 * kc:128 * kc + 128])
            wst = [poolA[:, 0:16384].rearrange("p (c d) -> p c d", d=512),
                   poolA[:, 16384:32768].rearrange("p (c d) -> p c d", d=512)]
            v_wv = [None, None]
            v_wv[0] = dma("gpsimd", "WV0", wst[0],
                          wvT_r[:, :, 0:512], waits=[("VE", v_ln1)])
            kst_dma = 0
            for kc in range(NKV):
                wpar = kc % 2
                if kc == 4:
                    v_wv[1] = dma("gpsimd", "WV1", wst[1],
                                  wvT_r[:, :, 512:1024],
                                  waits=[("VE", v_ln1)])
                if kc in kwv:
                    v_w = kwv[kc]
                else:
                    v_w = dma("gpsimd", wsem[wpar], wbr[wpar],
                              wkT_r[:, :, 128 * kc:128 * kc + 128],
                              waits=[("PE", wpar_last_use[wpar])])
                for tt in range(2):
                    extra = [("WC", kst_dma)] if kst_dma else []
                    rv = proj_rope_block(tt, tt,
                                         kst[:, 512 * tt:512 * tt + 512],
                                         v_w, wpar, is_q=False,
                                         extra_mm_waits=extra)
                wpar_last_use[wpar] = sch.val("PE")
                kst_dma = dma("gpsimd", "WC",
                              contrib[128 * kc:128 * kc + 128, :],
                              kst[:], waits=[("VE", rv)])

            vkr = E("vector", (lambda e: e.reduce_max(
                kmax[:], kmaxw[:], axis=AXX)), inc="VE")
            se_ks = E("scalar", (lambda e: e.activation(
                stat_s[0:1, 0:1], kmax[0:1, :], AF.Copy)),
                waits=[("VE", vkr)], inc="SE")
            dma("gpsimd", "WC", contrib[2048:2049, 0:16], stat_s[:],
                waits=[("SE", se_ks)])

            wqT_r = wqT.rearrange("(c p) o -> p c o", c=IC)
            qwv = {}
            for oc in range(2):
                qwv[oc] = dma("gpsimd", wsem[oc % 2], wbr[oc % 2],
                              wqT_r[:, :, 128 * oc:128 * oc + 128],
                              waits=[("PE", wpar_last_use[oc % 2])])

            vstage = [ptb[1][:, 0:512], ptb[1][:, 512:1024],
                      ptb[2][:, 0:512], ptb[2][:, 512:1024]]
            pe_v = {}
            vev = {}
            v_contribs = []
            for vh in range(2):
                for ts in range(8):
                    g = vh * 8 + ts
                    par = g % 2
                    w = [(f"WV{vh}", v_wv[vh]), ("VE", v_ln1)]
                    if g >= 4:
                        w.append(("VE", vev[g - 4]))
                    for ic in range(IC):
                        pv_ = E("tensor", (lambda e, ic=ic, ts=ts, par=par,
                                           vh=vh: e.matmul(
                                               pb[par][:],
                                               Br[:, ic,
                                                  128 * ts:128 * ts + 128],
                                               wst[vh][:, ic, :],
                                               start=(ic == 0),
                                               stop=(ic == IC - 1),
                                               skip_group_check=True)),
                                waits=w if ic == 0 else (),
                                inc="PE" if ic == IC - 1 else None)
                    pe_v[(vh, ts)] = pv_
                    evw = [("PE", pv_)]
                    if g >= 4:
                        evw.append(("WC", v_contribs[g - 4]))
                    vev[g] = E("vector", (lambda e, par=par, g=g:
                                          e.tensor_copy(
                                              vstage[g % 4], pb[par][:])),
                               waits=evw, inc="VE")
                    v_contribs.append(dma(
                        "gpsimd", "WC",
                        contrib[1024 + 128 * ts:1024 + 128 * ts + 128,
                                512 * vh:512 * vh + 512],
                        vstage[g % 4], waits=[("VE", vev[g])]))

            if dbg:
                dma("sync", "DBG", dbg_t["dbg_contrib"], contrib[:, :],
                    waits=[("WC", sch.val("WC"))])
            if upto < 4:
                raise _Stop

            E("gpsimd", (lambda e: e.collective_compute(
                "AllGather", ALU.bypass, replica_groups=[list(range(NCORES))],
                ins=[contrib.ap().opt()], outs=[gathered.ap().opt()])),
                waits=[("WC", sch.val("WC"))], inc="CC")

            for oc in range(NH):
                wpar = oc % 2
                if oc in qwv:
                    v_w = qwv[oc]
                else:
                    v_w = dma("gpsimd", wsem[wpar], wbr[wpar],
                              wqT_r[:, :, 128 * oc:128 * oc + 128],
                              waits=[("PE", wpar_last_use[wpar])])
                for tt in range(2):
                    rv = proj_rope_block(tt, tt,
                                         Ar[:, oc, 512 * tt:512 * tt + 512],
                                         v_w, wpar, is_q=True)
                wpar_last_use[wpar] = sch.val("PE")
            v_q = rope_ve[-1]

            if dbg:
                dma("sync", "DBG", dbg_t["dbg_q"], poolA[:],
                    waits=[("VE", v_q)])

            vqr = E("vector", (lambda e: e.reduce_max(
                qmax[:], qmaxw[:], axis=AXX)), inc="VE")
            v_stg = dma("sync", "SG",
                        statg[:].rearrange("p (r h) -> p r h", r=8),
                        gathered.ap().rearrange(
                            "(r q) j -> r q j",
                            r=NCORES)[:, 2048, 0:8].unsqueeze(0),
                        waits=[("CC", 1)])
            E("vector", (lambda e: e.tensor_copy(statf[:], statg[:])),
                waits=[("SG", v_stg)])
            v_kg0 = E("vector", (lambda e: e.reduce_max(
                stat2[:, 0:1], statf[:], axis=AXX)), inc="VE")
            se_kg = E("scalar", (lambda e: e.activation(
                stat_s[0:1, 1:2], stat2[0:1, 0:1], AF.Copy)),
                waits=[("VE", v_kg0)], inc="SE")
            pbc = E("tensor", (lambda e: e.matmul(
                bslot[:, 0:1], onesrow, stat_s[:, 1:2], start=True, stop=True,
                skip_group_check=True)),
                waits=[("SE", se_kg), ("VE", norm_ve[-1])], inc="PE")
            vkmb = E("vector", (lambda e: e.tensor_copy(kmb[:],
                                                         bslot[:, 0:1])),
                     waits=[("PE", pbc)], inc="VE")
            po1 = E("gpsimd", (lambda e: e.tensor_tensor(
                negB[:], qmax[:], kmb[:], ALU.mult)),
                waits=[("VE", max(vqr, vkmb))], inc="PO")
            E("scalar", (lambda e: e.activation(
                kred[:], negB[:], AF.Ln)), waits=[("PO", po1)])
            v_b2 = E("scalar", (lambda e: e.activation(
                negB[:], kred[:], AF.Exp, scale=0.5, bias=lnc_t[:, 0:1])),
                inc="SE")
            v_negB = E("vector", (lambda e: e.tensor_scalar(
                negB[:], negB[:], 60.0, -1.0, ALU.min, ALU.mult)),
                waits=[("SE", v_b2)], inc="VE")
            if dbg:
                dbgst = sb("dbgst", [128, 8], F32)
                E("vector", lambda e: e.tensor_copy(dbgst[:, 0:1], negB[:]),
                  waits=[("VE", v_negB)])
                E("vector", lambda e: e.tensor_copy(dbgst[:, 1:2], qmax[:]))
                E("vector", lambda e: e.tensor_copy(dbgst[:, 2:3], kmax[:]))
                E("vector", lambda e: e.tensor_copy(dbgst[:, 3:4], kmb[:]))
                vdb = E("vector", lambda e: e.tensor_copy(
                    dbgst[0:1, 4:8], statf[:, 0:4]), inc="VE")
                dma("sync", "DBG", dbg_t["dbg_negB"], dbgst[:],
                    waits=[("VE", vdb)])
            if upto < 5:
                raise _Stop

            gat = gathered.ap().rearrange("(r q) j -> r q j", r=NCORES)
            Aq = poolA[:].rearrange("p (g t) -> p g t", g=NH)
            kb_r = [k[:].rearrange("p (r j) -> p r j", r=8) for k in kbuf]
            vb_r = [v[:].rearrange("p (r c) -> p r c", r=8) for v in vbuf]
            kksem = ["KK0", "KK1"]
            vvsem = ["VV0", "VV1"]
            obank = (pb[2], pb[3])

            pairs = []
            for t in range(QT):
                for kvh in range(NKV):
                    u = t * NKV + kvh
                    n = 4 * (t + 1)
                    for pi in range(n):
                        jb, h = pi // 4, pi % 4
                        pairs.append(dict(t=t, kvh=kvh, u=u, jb=jb,
                                          r0=2 * h, r1=2 * h + 1,
                                          first=(pi == 0), last=(pi == n - 1)))
            NPAIR = len(pairs)
            st_pe = [0] * NPAIR
            exp_seL = [0] * NPAIR
            av_peL = [0] * NPAIR
            kv_done = {}
            nload = [0]
            last_st_on_kpar = {0: 0, 1: 0}
            last_av_on_kpar = {0: 0, 1: 0}
            last_ev = [0]

            def kv_load(p):
                pr = pairs[p]
                key = (pr["u"], pr["jb"])
                if key in kv_done:
                    return kv_done[key]
                t, kvh, jb = pr["t"], pr["kvh"], pr["jb"]
                kpar = nload[0] % 2
                nload[0] += 1
                ksrc = gat[:, kvh * 128:kvh * 128 + 128,
                           jb * 128:jb * 128 + 128].transpose((1, 0, 2))
                vsrc = gat[:, 1024 + jb * 128:1024 + jb * 128 + 128,
                           kvh * 128:kvh * 128 + 128].transpose((1, 0, 2))
                v_k = dma("sync", kksem[kpar], kb_r[kpar], ksrc,
                          waits=[("PE", last_st_on_kpar[kpar]), ("CC", 1)])
                vw = [("PE", last_av_on_kpar[kpar])]
                if nload[0] <= 2:
                    vw.append(("VE", v_memset))
                v_v = dma("sync", vvsem[kpar], vb_r[kpar][:, :, 0:128], vsrc,
                          waits=vw)
                kv_done[key] = (kpar, v_k, v_v)
                return kv_done[key]

            def emit_stgroup(p):
                pr = pairs[p]
                sp = p % 3
                kpar, v_k, v_v = kv_load(p)
                pr["kpar"], pr["v_v"] = kpar, v_v
                stw = [(kksem[kpar], v_k), ("VE", v_q)]
                if p < 3:
                    stw.append(("VE", v_negB))
                if p >= 3:
                    stw.append(("SE", exp_seL[p - 3]))
                qrhs = Aq[:, 4 * pr["kvh"]:4 * pr["kvh"] + 4,
                          128 * pr["t"]:128 * pr["t"] + 128]
                diag = (pr["jb"] == pr["t"])
                vst = 0
                for h, r in ((0, pr["r0"]), (1, pr["r1"])):
                    lastmm = (h == 1) and not diag
                    vst = E("tensor", (lambda e, kpar=kpar, r=r, sp=sp, h=h,
                                       qrhs=qrhs, diag=diag: e.matmul(
                                           stq[sp][:, 512 * h:512 * h + 512],
                                           kb_r[kpar][:, r, :], qrhs,
                                           start=True, stop=not diag,
                                           skip_group_check=True)),
                            waits=stw if h == 0 else (),
                            inc="PE" if lastmm else None)
                    if diag:
                        vst = E("tensor", (lambda e, sp=sp, h=h, r=r: e.matmul(
                            stq[sp][:, 512 * h:512 * h + 512], eye,
                            masks_s[:, 512 * r:512 * r + 512],
                            start=False, stop=True, skip_group_check=True)),
                            inc="PE" if h == 1 else None)
                last_st_on_kpar[kpar] = vst
                st_pe[p] = vst

            def emit_exp(p):
                sp = p % 3
                ew = [("PE", st_pe[p]), ("VE", v_negB)]
                if p >= 3:
                    ew.append(("PE", av_peL[p - 3]))
                exp_seL[p] = E("scalar", (lambda e, sp=sp: e.activation(
                    ptb[sp][:], stq[sp][:], AF.Exp,
                    bias=negB[:, 0:1], scale=SCL)), waits=ew, inc="SE")

            def emit_avgroup(p):
                pr = pairs[p]
                sp = p % 3
                ob = obank
                kpar, v_v = pr["kpar"], pr["v_v"]
                avw = [("SE", exp_seL[p]), (vvsem[kpar], v_v)]
                if pr["first"]:
                    avw.append(("VE", last_ev[0]))
                vav = 0
                for h, r in ((0, pr["r0"]), (1, pr["r1"])):
                    for g in range(4):
                        st_flag = pr["first"] and h == 0 and (g % 2 == 0)
                        vav = E("tensor", (lambda e, sp=sp, h=h, r=r, g=g,
                                           ob=ob, kpar=kpar, st_flag=st_flag:
                                           e.matmul(
                                               ob[g // 2][:, 129 * (g % 2):
                                                          129 * (g % 2) + 129],
                                               ptb[sp][:, 512 * h + 128 * g:
                                                       512 * h + 128 * g + 128],
                                               vb_r[kpar][:, r, 0:129],
                                               start=st_flag, stop=False,
                                               skip_group_check=True)),
                                waits=avw if (h == 0 and g == 0) else (),
                                inc="PE" if (h == 1 and g == 3) else None)
                av_peL[p] = vav
                last_av_on_kpar[kpar] = vav
                if pr["last"]:
                    emit_evac(pr, vav)

            def emit_evac(pr, v_av_all):
                ob = obank
                t, kvh = pr["t"], pr["kvh"]
                E("scalar", (lambda e, ob=ob: e.activation(
                    rl[:, 0:2], ob[0][:, 128:259:129], AF.Ln)),
                    waits=[("PE", v_av_all)])
                E("scalar", (lambda e, ob=ob: e.activation(
                    rl[:, 2:4], ob[1][:, 128:259:129], AF.Ln)))
                v_l = E("scalar", (lambda e: e.activation(
                    rl[:], rl[:], AF.Exp, scale=-1.0)), inc="SE")
                vsc = 0
                for g in range(4):
                    gg = 4 * kvh + g
                    vsc = E("vector", (lambda e, ob=ob, g=g, gg=gg, t=t:
                                       e.tensor_scalar(
                                           Aq[:, gg, 128 * t:128 * t + 128],
                                           ob[g // 2][:, 129 * (g % 2):
                                                      129 * (g % 2) + 128],
                                           rl[:, g:g + 1], None, ALU.mult)),
                            waits=[("SE", v_l)] if g == 0 else (), inc="VE")
                last_ev[0] = vsc

            for pp in range(min(3, NPAIR)):
                emit_stgroup(pp)
            for p in range(NPAIR):
                emit_exp(p)
                if p + 3 < NPAIR:
                    emit_stgroup(p + 3)
                emit_avgroup(p)
            v_attn = last_ev[0]

            trbit = stp1[:].bitcast(BF16)
            tslots = [trbit[:, 0:128], trbit[:, 1024:1152]]
            tr_ve = []
            for t in range(QT):
                for g in range(NH):
                    idx = len(tr_ve)
                    tw = []
                    if idx == 0:
                        tw.append(("VE", v_attn))
                    if idx >= 2:
                        tw.append(("VE", tr_ve[idx - 2]))
                    ptv = E("tensor", (lambda e, g=g, t=t, idx=idx:
                                       e.transpose(
                                           tslots[idx % 2],
                                           Aq[:, g, 128 * t:128 * t + 128],
                                           eye)), waits=tw, inc="PE")
                    tv = E("vector", (lambda e, g=g, t=t, idx=idx:
                                      e.tensor_copy(
                                          Br[:, g, 128 * t:128 * t + 128],
                                          tslots[idx % 2])),
                           waits=[("PE", ptv)], inc="VE")
                    tr_ve.append(tv)
            v_attn = tr_ve[-1]

            if dbg:
                dma("sync", "DBG", dbg_t["dbg_attnT"], poolB[:],
                    waits=[("VE", v_attn)])
            if upto < 6:
                raise _Stop

            woT_r = woT.rearrange("(c p) o -> p c o", c=IC)
            xT_rr = xT.rearrange("(c p) t -> c p t", c=IC)
            xrel = [scr8[:, 0:512], scr8[:, 512:1024]]
            xrsem = ["XR0", "XR1"]
            x1f = fscr[:, 0:512]
            x1f_ve = []
            x1_st = []
            wpar_lu = {0: sch.val("PE"), 1: sch.val("PE")}
            v_ln2pe = 0
            for oc in range(IC):
                wpar = oc % 2
                v_w = dma("gpsimd", wsem[wpar], wbr[wpar],
                          woT_r[:, :, 128 * oc:128 * oc + 128],
                          waits=[("PE", wpar_lu[wpar])])
                for tt in range(2):
                    g = oc * 2 + tt
                    par = tt
                    xpar = g % 2
                    xw = []
                    if g >= 2:
                        xw.append(("VE", x1f_ve[g - 2]))
                    v_x = dma("sync", xrsem[xpar], xrel[xpar],
                              xT_rr[oc, :, 512 * tt:512 * tt + 512], waits=xw)
                    mw = [(wsem[wpar], v_w), ("VE", v_attn)]
                    if g >= 2:
                        mw.append(("VE", x1f_ve[g - 2]))
                    for ic in range(IC):
                        v_pe = E("tensor", (lambda e, ic=ic, tt=tt, par=par,
                                            wpar=wpar: e.matmul(
                                                pb[par][:],
                                                wbr[wpar][:, ic, :],
                                                Br[:, ic,
                                                   512 * tt:512 * tt + 512],
                                                start=(ic == 0),
                                                stop=(ic == IC - 1),
                                                skip_group_check=True)),
                                 waits=mw if ic == 0 else (),
                                 inc="PE" if ic == IC - 1 else None)
                    fw = [("PE", v_pe), (xrsem[xpar], v_x)]
                    if x1_st:
                        fw.append(("ST1", x1_st[-1]))
                    E("vector", (lambda e, par=par, xpar=xpar:
                                 e.tensor_tensor(
                                     x1f, pb[par][:], xrel[xpar], ALU.add)),
                        waits=fw)
                    vx1 = E("vector", lambda e: e.tensor_copy(
                        fscr[:, 512:1024], x1f), inc="VE")
                    vst = dma("sync", "ST1",
                              x1d[128 * oc:128 * oc + 128,
                                  512 * tt:512 * tt + 512],
                              fscr[:, 512:1024], waits=[("VE", vx1)])
                    x1_st.append(vst)
                    E("vector", (lambda e, oc=oc, tt=tt: e.tensor_copy(
                        Ar[:, oc, 512 * tt:512 * tt + 512], x1f)))
                    vsq = E("vector", (lambda e, oc=oc, tt=tt, xpar=xpar:
                                       e.tensor_tensor(
                                           xsq[xpar],
                                           Ar[:, oc,
                                              512 * tt:512 * tt + 512],
                                           Ar[:, oc,
                                              512 * tt:512 * tt + 512],
                                           ALU.mult)), inc="VE")
                    x1f_ve.append(vsq)
                    E("tensor", (lambda e, tt=tt, oc=oc: e.matmul(
                        pb[2 + tt][:], ones,
                        Ar[:, oc, 512 * tt:512 * tt + 512],
                        start=(oc == 0), stop=(oc == IC - 1),
                        skip_group_check=True)), waits=[("VE", vsq)])
                    v_ln2pe = E("tensor", (lambda e, tt=tt, oc=oc, xpar=xpar:
                                           e.matmul(
                                               pb[4 + tt][:], ones,
                                               xsq[xpar],
                                               start=(oc == 0),
                                               stop=(oc == IC - 1),
                                               skip_group_check=True)),
                                inc="PE")
                wpar_lu[wpar] = sch.val("PE")
            if dbg:
                dma("sync", "DBG", dbg_t["dbg_x1"], x1d[:, :],
                    waits=[("ST1", x1_st[-1])])
            if upto < 7:
                raise _Stop

            v_ln2 = 0
            ln2_chunks = {}
            for tt in range(2):
                v_ln2, ln2_chunks[tt] = ln_finalize_and_apply(
                    tt, 2 + tt, 4 + tt, Ar, Br, v_ln2pe)
            if dbg:
                dma("sync", "DBG", dbg_t["dbg_ln2"], poolB[:],
                    waits=[("VE", v_ln2)])
            if upto < 8:
                raise _Stop

            u1pad = upads[:, 0:1024]
            u3pad = upads[:, 1024:2048]
            u2pad = upads[:, 2048:3072]
            a1r = a1s[:].rearrange("p (c r) -> p c r", c=IC)
            a3r = a3s[:].rearrange("p (c r) -> p c r", c=IC)
            v_use = 0
            for tt in range(2):
                for (ar, upad) in ((a1r, u1pad), (a3r, u3pad)):
                    for ic in range(IC):
                        w = [("VE", ln2_chunks[tt][ic]), ("W0", W0F)]
                        if v_use:
                            w.append(("SE", v_use))
                        pu = E("tensor", (lambda e, ic=ic, tt=tt, ar=ar:
                                          e.matmul(
                                              nrm[0:2, :], ar[:, ic, :],
                                              Br[:, ic,
                                                 512 * tt:512 * tt + 512],
                                              start=(ic == 0),
                                              stop=(ic == IC - 1),
                                              skip_group_check=True)),
                               waits=w,
                               inc="PE" if ic == IC - 1 else None)
                    v_use = E("scalar", (lambda e, tt=tt, upad=upad:
                                         e.activation(
                                             upad[0:2,
                                                  512 * tt:512 * tt + 512],
                                             nrm[0:2, :], AF.Copy,
                                             scale=0.5)),
                              waits=[("PE", pu)], inc="SE")
            hs = poolA[:, 0:8192].rearrange("p (c t) -> p c t", c=8)
            v_hs = 0
            for j in range(8):
                for tt in range(2):
                    E("vector", (lambda e, j=j, tt=tt: e.tensor_tensor(
                        fscr[:, 0:512], Br[:, j, 512 * tt:512 * tt + 512],
                        Br[:, j + 8, 512 * tt:512 * tt + 512], ALU.add)),
                        waits=[("VE", v_ln2)])
                    E("vector", (lambda e, j=j, tt=tt: e.tensor_tensor(
                        fscr[:, 512:1024],
                        Br[:, j + 16, 512 * tt:512 * tt + 512],
                        Br[:, j + 24, 512 * tt:512 * tt + 512], ALU.add)))
                    v_hs = E("vector", (lambda e, j=j, tt=tt:
                                        e.tensor_tensor(
                                            hs[:, j,
                                               512 * tt:512 * tt + 512],
                                            fscr[:, 0:512],
                                            fscr[:, 512:1024], ALU.add)),
                             inc="VE")

            w1T_r = w1T.rearrange("(c p) h -> p c h", c=9)
            w3T_r = w3T.rearrange("(c p) h -> p c h", c=9)
            gsv = poolB[:, 0:FC * 1024].rearrange("p (c t) -> p c t", c=FC)
            gcs = [[bscr[:, 512 * (2 * f + tt):512 * (2 * f + tt) + 512]
                    for tt in range(2)] for f in range(4)]
            sg = ptb[0][:, 0:512]
            a2r = a2s[:].rearrange("p (c r) -> p c r", c=HC)
            w1b = [w[:, 0:1152].rearrange("p (c h) -> p c h", c=9)
                   for w in wb]
            w3b = [w[:, 1152:2304].rearrange("p (c h) -> p c h", c=9)
                   for w in wb]
            silu_se = []
            gc_ve = []
            u2_pe = {}
            wpar_last = dict(wpar_lu)
            u2_started = {0: False, 1: False}
            v_gs = 0
            for p in range(FC):
                for f in range(4):
                    hc = 22 * f + p
                    wpar = (4 * p + f) % 2
                    dma("gpsimd", wsem[wpar], w1b[wpar],
                        w1T_r[:, :, 128 * hc:128 * hc + 128],
                        waits=[("PE", wpar_last[wpar])])
                    v_w3 = dma("gpsimd", wsem[wpar], w3b[wpar],
                               w3T_r[:, :, 128 * hc:128 * hc + 128])
                    for tt in range(2):
                        i_gu = len(silu_se)
                        gbank = [pb[0], nrm][i_gu % 2]
                        ubank = [pb[1], bslot][i_gu % 2]
                        sgb = [sg, ptb[0][:, 512:1024]][i_gu % 2]
                        mw = [(wsem[wpar], v_w3), ("VE", v_hs)]
                        if v_use:
                            mw.append(("SE", v_use))
                        if i_gu >= 2:
                            mw.append(("SE", silu_se[i_gu - 2]))
                        for ic in range(9):
                            rhs = (hs[:, ic, 512 * tt:512 * tt + 512]
                                   if ic < 8
                                   else u1pad[:, 512 * tt:512 * tt + 512])
                            E("tensor", (lambda e, ic=ic, wpar=wpar, rhs=rhs,
                                         gbank=gbank: e.matmul(
                                             gbank, w1b[wpar][:, ic, :],
                                             rhs, start=(ic == 0),
                                             stop=(ic == 8),
                                             skip_group_check=True)),
                                waits=mw if ic == 0 else ())
                        uw = [("VE", gc_ve[i_gu - 2])] if i_gu >= 2 else []
                        for ic in range(9):
                            rhs = (hs[:, ic, 512 * tt:512 * tt + 512]
                                   if ic < 8
                                   else u3pad[:, 512 * tt:512 * tt + 512])
                            v_gu = E("tensor", (lambda e, ic=ic, wpar=wpar,
                                                rhs=rhs, ubank=ubank:
                                                e.matmul(
                                                    ubank,
                                                    w3b[wpar][:, ic, :],
                                                    rhs, start=(ic == 0),
                                                    stop=(ic == 8),
                                                    skip_group_check=True)),
                                     waits=uw if ic == 0 else (),
                                     inc="PE" if ic == 8 else None)
                        sse = E("scalar", (lambda e, gbank=gbank, sgb=sgb:
                                           e.activation(
                                               sgb, gbank, AF.Silu)),
                            waits=[("PE", v_gu)], inc="SE")
                        silu_se.append(sse)
                        gw = [("SE", sse)]
                        if (f, tt) in u2_pe:
                            gw.append(("PE", u2_pe[(f, tt)]))
                        vgc = E("vector", (lambda e, f=f, tt=tt, sgb=sgb,
                                           ubank=ubank: e.tensor_tensor(
                                               gcs[f][tt], sgb, ubank,
                                               ALU.mult)),
                                waits=gw, inc="VE")
                        gc_ve.append(vgc)
                        u2w = [("VE", vgc), ("W0", W0F)]
                        if not u2_started[tt]:
                            u2w.append(("VE", v_ln2))
                        pu2 = E("tensor", (lambda e, hc=hc, f=f, tt=tt,
                                           s=not u2_started[tt]: e.matmul(
                                               pb[2 + tt][0:2, :],
                                               a2r[:, hc, :], gcs[f][tt],
                                               start=s,
                                               stop=(hc == HC - 1),
                                               skip_group_check=True)),
                                 waits=u2w, inc="PE")
                        u2_started[tt] = True
                        u2_pe[(f, tt)] = pu2
                    wpar_last[wpar] = sch.val("PE")
                for tt in range(2):
                    E("vector", (lambda e, tt=tt: e.tensor_tensor(
                        fscr[:, 0:512], gcs[0][tt], gcs[1][tt], ALU.add)),
                        waits=[("PE", u2_pe[(3, tt)])])
                    E("vector", (lambda e, tt=tt: e.tensor_tensor(
                        fscr[:, 512:1024], gcs[2][tt], gcs[3][tt], ALU.add)))
                    v_gs = E("vector", (lambda e, p=p, tt=tt:
                                        e.tensor_tensor(
                                            gsv[:, p,
                                                512 * tt:512 * tt + 512],
                                            fscr[:, 0:512],
                                            fscr[:, 512:1024], ALU.add)),
                             inc="VE")
            E("scalar", (lambda e: e.activation(
                u2pad[0:2, 0:512], pb[2][0:2, :], AF.Copy, scale=0.5)),
                waits=[("PE", u2_pe[(3, 0)])])
            v_u2b = E("scalar", (lambda e: e.activation(
                u2pad[0:2, 512:1024], pb[3][0:2, :], AF.Copy, scale=0.5)),
                waits=[("PE", u2_pe[(3, 1)])], inc="SE")
            if dbg:
                dma("sync", "DBG", dbg_t["dbg_gs"], poolB[:, 0:FC * 1024],
                    waits=[("VE", v_gs)])

            w2T_r = w2T.rearrange("(c p) o -> p c o", c=23)
            w2b = [w[:, 0:2944].rearrange("p (c o) -> p c o", c=23)
                   for w in wb]

            x2f = fscr[:, 512:1024]
            y_ve = []
            y_st = []
            st1_full = x1_st[-1]
            wpar_last = {0: sch.val("PE"), 1: sch.val("PE")}
            for oc in range(IC):
                wpar = oc % 2
                v_w = dma("gpsimd", wsem[wpar], w2b[wpar],
                          w2T_r[:, :, 128 * oc:128 * oc + 128],
                          waits=[("PE", wpar_last[wpar])])
                for tt in range(2):
                    g = oc * 2 + tt
                    par = tt
                    xpar = g % 2
                    xw = [("ST1", st1_full)]
                    if g >= 2:
                        xw.append(("VE", y_ve[g - 2]))
                    v_x = dma("sync", xrsem[xpar], xrel[xpar],
                              x1d[128 * oc:128 * oc + 128,
                                  512 * tt:512 * tt + 512], waits=xw)
                    mw = [(wsem[wpar], v_w), ("VE", v_gs), ("SE", v_u2b)]
                    if g >= 2:
                        mw.append(("VE", y_ve[g - 2]))
                    for ic in range(23):
                        rhs = (gsv[:, ic, 512 * tt:512 * tt + 512]
                               if ic < 22
                               else u2pad[:, 512 * tt:512 * tt + 512])
                        v_pe = E("tensor", (lambda e, ic=ic, par=par,
                                            wpar=wpar, rhs=rhs: e.matmul(
                                                pb[par][:],
                                                w2b[wpar][:, ic, :], rhs,
                                                start=(ic == 0),
                                                stop=(ic == 22),
                                                skip_group_check=True)),
                                 waits=mw if ic == 0 else (),
                                 inc="PE" if ic == 22 else None)
                    fw = [("PE", v_pe), (xrsem[xpar], v_x)]
                    if y_st:
                        fw.append(("ST2", y_st[-1]))
                    vy = E("vector", (lambda e, par=par, xpar=xpar:
                                      e.tensor_tensor(
                                          x2f, pb[par][:], xrel[xpar],
                                          ALU.add)), waits=fw, inc="VE")
                    y_ve.append(vy)
                    vst = dma("sync", "ST2",
                              out[128 * oc:128 * oc + 128,
                                  512 * tt:512 * tt + 512],
                              x2f, waits=[("VE", vy)])
                    y_st.append(vst)
                wpar_last[wpar] = sch.val("PE")

        try:
            _sched_all()
        except _Stop:
            pass

        all_dma_sems = [s for s in sch.ctr
                        if s not in ("PE", "VE", "SE", "PO", "CC")]
        E("gpsimd", lambda e: None,
          waits=[(s, sch.val(s)) for s in all_dma_sems])

        semnames = sorted({s for ops in sch.ops.values()
                           for (ws, _, inc, _) in ops
                           for s in ([w[0] for w in ws] +
                                     ([inc] if inc else []))})
        sems = {s: ctx.enter_context(nc.semaphore(s)) for s in semnames}
        block = ctx.enter_context(nc.Block())

        def mk(name):
            def f(eng):
                seen = {}
                for waits, fn, inc, inc_n in sch.ops[name]:
                    for (s, v) in waits:
                        if v > 0 and seen.get(s, 0) < v:
                            eng.wait_ge(sems[s], v)
                            seen[s] = v
                    instr = fn(eng)
                    if inc is not None:
                        instr.then_inc(sems[inc], inc_n)
            return f

        block.sync(mk("sync"))
        block.gpsimd(mk("gpsimd"))
        block.tensor(mk("tensor"))
        block.vector(mk("vector"))
        block.scalar(mk("scalar"))

    return nc



def _stage(inputs):
    x = np.asarray(inputs["L0_attn_hidden_states"])[0]
    Wq = np.asarray(inputs["L0_attn_Wq"])
    Wk = np.asarray(inputs["L0_attn_Wk"])
    Wv = np.asarray(inputs["L0_attn_Wv"])
    Wo = np.asarray(inputs["L0_attn_Wo"])
    w1q = np.asarray(inputs["L0_ffn_w1q"])
    a1 = np.asarray(inputs["L0_ffn_a1"])
    b1 = np.asarray(inputs["L0_ffn_b1"])
    w3q = np.asarray(inputs["L0_ffn_w3q"])
    a3 = np.asarray(inputs["L0_ffn_a3"])
    b3 = np.asarray(inputs["L0_ffn_b3"])
    w2q = np.asarray(inputs["L0_ffn_w2q"])
    a2 = np.asarray(inputs["L0_ffn_a2"])
    b2 = np.asarray(inputs["L0_ffn_b2"])

    C = np.ascontiguousarray
    f32 = np.float32

    wqT = C(Wq.T.astype(f32))
    wkT = C(Wk.T.astype(f32))
    wvT = C(Wv.T.astype(f32))
    woT = C(Wo.T.astype(f32))

    def pad_gate(wT, bT):
        outw = np.zeros((1152, HIDP), f32)
        for f in range(4):
            outw[0:1024, 2816 * f:2816 * f + 2752] = \
                wT[:, 2752 * f:2752 * f + 2752]
            outw[1024:1026, 2816 * f:2816 * f + 2752] = \
                bT[:, 2752 * f:2752 * f + 2752]
        return outw

    w1T = pad_gate(C(w1q.T.astype(f32)), C(b1.T.astype(f32)))
    w3T = pad_gate(C(w3q.T.astype(f32)), C(b3.T.astype(f32)))

    w2T = np.zeros((W2K + 128, D), f32)
    w2T[0:2752, :] = w2q.T.astype(f32)
    w2T[W2K:W2K + 2, :] = b2.T.astype(f32)

    def a_small(a, nch):
        acx = a.T.astype(f32).reshape(nch, 128, a.shape[0])
        return C(acx.transpose(1, 0, 2).reshape(128, nch * a.shape[0]))

    a1cv = a_small(a1, IC)
    a3cv = a_small(a3, IC)
    a2p = np.zeros((2, HIDP), f32)
    for f in range(4):
        a2p[:, 2816 * f:2816 * f + 2752] = a2[:, 2752 * f:2752 * f + 2752]
    a2cv = a_small(a2p, HC)

    inv = (1.0 / (10000.0 ** (np.arange(0, HD, 2, dtype=np.float64) / HD)))
    bf = ml_dtypes.bfloat16

    eye_ones = np.zeros((128, 256), f32)
    eye_ones[:, 0:128] = np.eye(128)
    eye_ones[:, 128:256] = 1.0
    consts_v = eye_ones.astype(bf)

    in_maps = []
    for c in range(NCORES):
        xc = C(x[c::8, :].T.astype(f32))
        pos = (c + 8 * np.arange(R)).astype(np.float64)
        fr = np.outer(pos, inv)
        emb = np.concatenate([fr, fr], axis=1)
        cosv = C(np.cos(emb).T.astype(f32))
        sinv = np.sin(emb).T.astype(f32)
        sins = sinv.copy()
        sins[0:64, :] = -sinv[0:64, :]
        sins = C(sins)

        jj = np.arange(128)[:, None]
        mmi = np.arange(128)[None, :]
        allow = (jj < mmi)
        mrow = []
        for rr in range(8):
            al = allow | ((jj == mmi) & (rr <= c))
            mk = np.where(al, 0.0, -1e9).astype(f32)
            mrow.append(np.tile(mk, (1, 4)))
        maskv = np.concatenate(mrow, axis=1).astype(bf)

        in_maps.append({
            "xT": xc, "wqT": wqT, "wkT": wkT, "wvT": wvT, "woT": woT,
            "w1T": w1T, "w3T": w3T, "w2T": w2T,
            "a1c": a1cv, "a3c": a3cv, "a2c": a2cv,
            "cosq": cosv, "sinq": sins,
            "masks": maskv, "consts": consts_v,
        })
    return in_maps


def kernel(**inputs):
    in_maps = _stage(inputs)
    nc = build(dbg=False)
    res = run_bass_kernel_spmd(nc, in_maps, core_ids=list(range(NCORES)))
    full = np.empty((1, S, D), dtype=np.float32)
    for c in range(NCORES):
        full[0, c::8, :] = np.asarray(res.results[c]["out"],
                                      dtype=np.float32).T
    return full


# version 08
# speedup vs baseline: 1.0928x; 1.0002x over previous
import math
import numpy as np
import ml_dtypes
import concourse.bass as bass
import concourse.mybir as mybir
from concourse.bass_utils import run_bass_kernel_spmd
from contextlib import ExitStack

F32 = mybir.dt.float32
BF16 = mybir.dt.bfloat16
AF = mybir.ActivationFunctionType
ALU = mybir.AluOpType
AXX = mybir.AxisListType.X

NCORES = 8
S = 8192
D = 4096
NH = 32
NKV = 8
HD = 128
R = 1024
IC = 32
QT = 8
HIDP = 11264
HC = 88
FC = 22
W2K = 2816
RPC = 2049
SCL = 1.0 / math.sqrt(HD)
LNC_BIAS = -0.5 * math.log(HD)
ENGINES = ("sync", "gpsimd", "tensor", "vector", "scalar")


class Sched:
    def __init__(self):
        self.ops = {e: [] for e in ENGINES}
        self.ctr = {}

    def val(self, sem):
        return self.ctr.get(sem, 0)

    def emit(self, eng, fn, waits=(), inc=None, inc_n=1):
        post = None
        if inc is not None:
            self.ctr[inc] = self.ctr.get(inc, 0) + inc_n
            post = self.ctr[inc]
        self.ops[eng].append((tuple(waits), fn, inc, inc_n))
        return post


class _Stop(Exception):
    pass


def build(dbg=False, upto=99):
    nc = bass.Bass("TRN2", target_bir_lowering=False)

    def din(name, shape, dt=F32):
        return nc.dram_tensor(name, shape, dt, kind="ExternalInput").ap()

    xT = din("xT", [D, R])
    wqT = din("wqT", [D, D])
    wkT = din("wkT", [D, NKV * HD])
    wvT = din("wvT", [D, NKV * HD])
    woT = din("woT", [D, D])
    w1T = din("w1T", [1024 + 128, HIDP])
    w3T = din("w3T", [1024 + 128, HIDP])
    w2T = din("w2T", [W2K + 128, D])
    a1c = din("a1c", [128, IC * 2])
    a3c = din("a3c", [128, IC * 2])
    a2c = din("a2c", [128, HC * 2])
    cosq = din("cosq", [128, R])
    sinq = din("sinq", [128, R])
    masks = din("masks", [128, 8 * 512], BF16)
    consts = din("consts", [128, 256], BF16)

    out = nc.dram_tensor("out", [D, R], F32, kind="ExternalOutput").ap()

    contrib = nc.dram_tensor("contrib", [RPC, 1024], BF16)
    gathered = nc.dram_tensor("gathered", [NCORES * RPC, 1024], BF16)
    x1d = nc.dram_tensor("x1d", [D, R], F32)

    dbg_t = {}
    if dbg:
        for nm, shp, dt in [
            ("dbg_ln1", [128, IC * 1024], BF16),
            ("dbg_q", [128, IC * 1024], BF16),
            ("dbg_contrib", [RPC, 1024], BF16),
            ("dbg_negB", [128, 8], F32),
            ("dbg_attnT", [128, IC * 1024], BF16),
            ("dbg_x1", [D, R], F32),
            ("dbg_ln2", [128, IC * 1024], BF16),
            ("dbg_gs", [128, FC * 1024], BF16),
        ]:
            dbg_t[nm] = nc.dram_tensor(nm, shp, dt, kind="ExternalOutput").ap()

    sch = Sched()
    with ExitStack() as ctx:
        def sb(name, shape, dt):
            return ctx.enter_context(nc.sbuf_tensor(name, shape, dt))

        def pst(name, shape, dt):
            return ctx.enter_context(nc.psum_tensor(name, shape, dt))

        poolA = sb("poolA", [128, 32768], BF16)
        poolB = sb("poolB", [128, 32768], BF16)
        wbufU = sb("wbufU", [128, 8192], BF16)
        wbufA = wbufU[:, 0:4096]
        wbufB = wbufU[:, 4096:8192]
        masks_s = sb("masks_s", [128, 4096], BF16)
        consts_s = sb("consts_s", [128, 256], BF16)
        scr8 = sb("scr8", [128, 2048], F32)
        fscr = sb("fscr", [128, 1024], F32)
        bscr = sb("bscr", [128, 4096], BF16)
        upads = sb("upads", [128, 3072], BF16)
        kbuf = [sb(f"kbuf{i}", [128, 1024], BF16) for i in range(2)]
        vbuf = [sb(f"vbuf{i}", [128, 8 * 136], BF16) for i in range(2)]
        ptb = [sb(f"ptb{i}", [128, 1024], BF16) for i in range(3)]
        m_s = sb("m_s", [128, 512], F32)
        r_s = sb("r_s", [128, 512], F32)
        mb16 = sb("mb16", [128, 512], BF16)
        a1s = sb("a1s", [128, IC * 2], BF16)
        a3s = sb("a3s", [128, IC * 2], BF16)
        a2s = sb("a2s", [128, HC * 2], BF16)
        qmaxw = sb("qmaxw", [128, 512], F32)
        kmaxw = sb("kmaxw", [128, 512], F32)
        qmax = sb("qmax", [128, 1], F32)
        kmax = sb("kmax", [128, 1], F32)
        kred = sb("kred", [128, 1], F32)
        kmb = sb("kmb", [128, 1], F32)
        negB = sb("negB", [128, 1], F32)
        lnc_t = sb("lnc_t", [128, 1], F32)
        rl = sb("rl", [128, 4], F32)
        stat_s = sb("stat_s", [1, 16], BF16)
        stat2 = sb("stat2", [1, 4], F32)
        epst = sb("epst", [128, 1], F32)
        statg = sb("statg", [1, 64], BF16)
        kst = sb("kst", [128, 1024], BF16)
        statf = sb("statf", [1, 64], F32)

        stp0 = pst("stp0", [128, 1024], F32)
        stp1 = pst("stp1", [128, 1024], F32)
        ob01 = pst("ob01", [128, 1024], F32)
        ob23 = pst("ob23", [128, 1024], F32)
        pb = [stp0[:, 0:512], stp0[:, 512:1024],
              ob01[:, 0:512], ob01[:, 512:1024],
              ob23[:, 0:512], ob23[:, 512:1024]]
        stq = [stp0[:], stp1[:], ob23[:]]
        nrm = stp1[:, 0:512]
        bslot = stp1[:, 512:1024]

        eye = consts_s[:, 0:128]
        ones = consts_s[:, 128:256]
        onesrow = consts_s[0:1, 128:256]

        Ar = poolA[:].rearrange("p (c t) -> p c t", c=IC)
        Br = poolB[:].rearrange("p (c t) -> p c t", c=IC)

        E = sch.emit

        def dma(eng, sem_, o, i, waits=()):
            return E(eng, (lambda e, o=o, i=i: e.dma_start(out=o, in_=i)),
                     waits=waits, inc=sem_, inc_n=16)

        def _sched_all():
            xT_r = xT.rearrange("(c p) t -> p c t", c=IC)
            for q4 in range(4):
                dma("gpsimd", f"XB{q4}", Ar[:, 8 * q4:8 * q4 + 8, :],
                    xT_r[:, 8 * q4:8 * q4 + 8, :])
            dma("gpsimd", "W0", consts_s[:], consts)
            dma("gpsimd", "W0", masks_s[:], masks)
            dma("gpsimd", "W0", a1s[:], a1c)
            dma("gpsimd", "W0", a3s[:], a3c)
            dma("gpsimd", "W0", a2s[:], a2c)
            W0F = sch.val("W0")
            dma("sync", "K0", scr8[:, 0:1024], cosq)
            dma("sync", "K0", scr8[:, 1024:2048], sinq)
            K0F = sch.val("K0")
            cos = scr8[:, 0:1024]
            sin = scr8[:, 1024:2048]

            E("vector", lambda e: e.memset(epst[:], 1e-5))
            E("vector", lambda e: e.memset(lnc_t[:], LNC_BIAS))
            E("vector", lambda e: e.memset(qmaxw[:], 0.0))
            E("vector", lambda e: e.memset(kmaxw[:], 0.0))
            E("vector", lambda e: e.memset(upads[:], 0.0))
            E("vector", lambda e: e.memset(stat_s[:], 0.0))
            E("vector", lambda e: e.memset(vbuf[0][:], 1.0))
            v_memset = E("vector", lambda e: e.memset(vbuf[1][:], 1.0),
                         inc="VE")

            xsq = [bscr[:, 0:512], bscr[:, 512:1024]]
            for tt in range(2):
                for ic in range(IC):
                    w = [(f"XB{ic // 8}", 16)]
                    if ic == 0 and tt == 0:
                        w.append(("W0", W0F))
                    E("tensor", (lambda e, ic=ic, tt=tt: e.matmul(
                        pb[tt][:], ones, Ar[:, ic, 512 * tt:512 * tt + 512],
                        start=(ic == 0), stop=(ic == IC - 1),
                        skip_group_check=True)), waits=w)
            sq_pe = []
            for tt in range(2):
                for ic in range(IC):
                    g = tt * IC + ic
                    par = g % 2
                    w = [(f"XB{ic // 8}", 16)]
                    if g >= 2:
                        w.append(("PE", sq_pe[g - 2]))
                    vv = E("vector", (lambda e, ic=ic, tt=tt, par=par:
                                      e.tensor_tensor(
                                          xsq[par],
                                          Ar[:, ic, 512 * tt:512 * tt + 512],
                                          Ar[:, ic, 512 * tt:512 * tt + 512],
                                          ALU.mult)), waits=w, inc="VE")
                    pv = E("tensor", (lambda e, ic=ic, tt=tt, par=par: e.matmul(
                        pb[2 + tt][:], ones, xsq[par],
                        start=(ic == 0), stop=(ic == IC - 1),
                        skip_group_check=True)), waits=[("VE", vv)], inc="PE")
                    sq_pe.append(pv)
            v_stats_pe = sq_pe[-1]

            def ln_finalize_and_apply(tt, bank_m, bank_sq, src_r, dst_r,
                                      pe_wait):
                E("vector", (lambda e: e.tensor_scalar(
                    m_s[:], pb[bank_m][:], 1.0 / D, None, ALU.mult)),
                    waits=[("PE", pe_wait)])
                E("vector", (lambda e: e.tensor_scalar(
                    fscr[:, 0:512], pb[bank_sq][:], 1.0 / D, None, ALU.mult)))
                E("vector", (lambda e: e.tensor_tensor(
                    r_s[:], m_s[:], m_s[:], ALU.mult)))
                E("vector", (lambda e: e.tensor_tensor(
                    fscr[:, 0:512], fscr[:, 0:512], r_s[:], ALU.subtract)))
                vw = E("vector", lambda e: e.tensor_copy(mb16[:], m_s[:]),
                       inc="VE")
                E("scalar", (lambda e: e.activation(
                    fscr[:, 512:1024], fscr[:, 0:512], AF.Ln,
                    bias=epst[:, 0:1])), waits=[("VE", vw)])
                sw = E("scalar", (lambda e: e.activation(
                    r_s[:], fscr[:, 512:1024], AF.Exp, scale=-0.5)), inc="SE")
                last = 0
                chunk_incs = []
                for ic in range(IC):
                    E("vector", (lambda e, ic=ic: e.tensor_tensor(
                        fscr[:, 0:512], src_r[:, ic, 512 * tt:512 * tt + 512],
                        mb16[:], ALU.subtract)), waits=[("SE", sw)])
                    last = E("vector", (lambda e, ic=ic: e.tensor_tensor(
                        dst_r[:, ic, 512 * tt:512 * tt + 512], fscr[:, 0:512],
                        r_s[:], ALU.mult)), inc="VE")
                    chunk_incs.append(last)
                return last, chunk_incs

            v_ln1 = 0
            for tt in range(2):
                v_ln1, _ = ln_finalize_and_apply(tt, tt, 2 + tt, Ar, Br,
                                                 v_stats_pe)

            if dbg:
                dma("sync", "DBG", dbg_t["dbg_ln1"], poolB[:],
                    waits=[("VE", v_ln1)])
            if upto < 2:
                raise _Stop

            wsem = ["WA", "WB"]
            wkT_r = wkT.rearrange("(c p) d -> p c d", c=IC)
            wvT_r = wvT.rearrange("(c p) d -> p c d", c=IC)
            wb = [wbufA, wbufB]
            wbr = [w.rearrange("p (c o) -> p c o", c=IC) for w in wb]
            rope_ve = []
            norm_pe = []
            norm_ve = []
            wpar_last_use = {0: 0, 1: 0}

            def proj_rope_block(tt, par, dst_ap, v_w, wpar, is_q,
                                extra_mm_waits=()):
                gidx = len(rope_ve)
                w = [(wsem[wpar], v_w), ("K0", K0F), ("VE", v_ln1)]
                w += list(extra_mm_waits)
                if gidx >= 2:
                    w.append(("VE", rope_ve[gidx - 2]))
                for ic in range(IC):
                    pk = E("tensor", (lambda e, ic=ic, tt=tt, par=par,
                                      wpar=wpar: e.matmul(
                                          pb[4 + par][:], wbr[wpar][:, ic, :],
                                          Br[:, ic, 512 * tt:512 * tt + 512],
                                          start=(ic == 0), stop=(ic == IC - 1),
                                          skip_group_check=True)),
                           waits=w if ic == 0 else (),
                           inc="PE" if ic == IC - 1 else None)
                sqw = [("PE", pk)]
                if norm_pe:
                    sqw.append(("PE", norm_pe[-1]))
                se = E("scalar", (lambda e, par=par: e.activation(
                    bscr[:, 0:512], pb[4 + par][:], AF.Square)),
                    waits=sqw, inc="SE")
                nmw = [("SE", se)]
                if norm_ve:
                    nmw.append(("VE", norm_ve[-1]))
                pn = E("tensor", (lambda e: e.matmul(
                    nrm, ones, bscr[:, 0:512], start=True, stop=True,
                    skip_group_check=True)), waits=nmw, inc="PE")
                norm_pe.append(pn)
                tgtw = qmaxw if is_q else kmaxw
                nv = E("vector", (lambda e, tgtw=tgtw: e.tensor_tensor(
                    tgtw[:], tgtw[:], nrm, ALU.max)),
                    waits=[("PE", pn)], inc="VE")
                norm_ve.append(nv)
                E("vector", (lambda e, par=par, tt=tt: e.tensor_tensor(
                    fscr[:, 0:512], pb[4 + par][:],
                    cos[:, 512 * tt:512 * tt + 512], ALU.mult)),
                    waits=[("SE", se)])
                E("vector", (lambda e, par=par, tt=tt: e.tensor_tensor(
                    fscr[0:64, 512:1024], pb[4 + par][64:128, :],
                    sin[0:64, 512 * tt:512 * tt + 512], ALU.mult)))
                E("vector", (lambda e, par=par, tt=tt: e.tensor_tensor(
                    fscr[64:128, 512:1024], pb[4 + par][0:64, :],
                    sin[64:128, 512 * tt:512 * tt + 512], ALU.mult)))
                rv = E("vector", (lambda e, dst_ap=dst_ap: e.tensor_tensor(
                    dst_ap, fscr[:, 0:512], fscr[:, 512:1024], ALU.add)),
                    inc="VE")
                rope_ve.append(rv)
                return rv

            kwv = {}
            for kc in range(2):
                kwv[kc] = dma("gpsimd", wsem[kc % 2], wbr[kc % 2],
                              wkT_r[:, :, 128 * kc:128 * kc + 128])
            wst = [poolA[:, 0:16384].rearrange("p (c d) -> p c d", d=512),
                   poolA[:, 16384:32768].rearrange("p (c d) -> p c d", d=512)]
            v_wv = [None, None]
            v_wv[0] = dma("gpsimd", "WV0", wst[0],
                          wvT_r[:, :, 0:512], waits=[("VE", v_ln1)])
            kst_dma = 0
            for kc in range(NKV):
                wpar = kc % 2
                if kc == 4:
                    v_wv[1] = dma("gpsimd", "WV1", wst[1],
                                  wvT_r[:, :, 512:1024],
                                  waits=[("VE", v_ln1)])
                if kc in kwv:
                    v_w = kwv[kc]
                else:
                    v_w = dma("gpsimd", wsem[wpar], wbr[wpar],
                              wkT_r[:, :, 128 * kc:128 * kc + 128],
                              waits=[("PE", wpar_last_use[wpar])])
                for tt in range(2):
                    extra = [("WC", kst_dma)] if kst_dma else []
                    rv = proj_rope_block(tt, tt,
                                         kst[:, 512 * tt:512 * tt + 512],
                                         v_w, wpar, is_q=False,
                                         extra_mm_waits=extra)
                wpar_last_use[wpar] = sch.val("PE")
                kst_dma = dma("gpsimd", "WC",
                              contrib[128 * kc:128 * kc + 128, :],
                              kst[:], waits=[("VE", rv)])

            vkr = E("vector", (lambda e: e.reduce_max(
                kmax[:], kmaxw[:], axis=AXX)), inc="VE")
            se_ks = E("scalar", (lambda e: e.activation(
                stat_s[0:1, 0:1], kmax[0:1, :], AF.Copy)),
                waits=[("VE", vkr)], inc="SE")
            dma("gpsimd", "WC", contrib[2048:2049, 0:16], stat_s[:],
                waits=[("SE", se_ks)])

            wqT_r = wqT.rearrange("(c p) o -> p c o", c=IC)
            qwv = {}
            for oc in range(2):
                qwv[oc] = dma("gpsimd", wsem[oc % 2], wbr[oc % 2],
                              wqT_r[:, :, 128 * oc:128 * oc + 128],
                              waits=[("PE", wpar_last_use[oc % 2])])

            vstage = [ptb[1][:, 0:512], ptb[1][:, 512:1024],
                      ptb[2][:, 0:512], ptb[2][:, 512:1024]]
            pe_v = {}
            vev = {}
            v_contribs = []
            for vh in range(2):
                for ts in range(8):
                    g = vh * 8 + ts
                    par = g % 2
                    w = [(f"WV{vh}", v_wv[vh]), ("VE", v_ln1)]
                    if g >= 4:
                        w.append(("VE", vev[g - 4]))
                    for ic in range(IC):
                        pv_ = E("tensor", (lambda e, ic=ic, ts=ts, par=par,
                                           vh=vh: e.matmul(
                                               pb[par][:],
                                               Br[:, ic,
                                                  128 * ts:128 * ts + 128],
                                               wst[vh][:, ic, :],
                                               start=(ic == 0),
                                               stop=(ic == IC - 1),
                                               skip_group_check=True)),
                                waits=w if ic == 0 else (),
                                inc="PE" if ic == IC - 1 else None)
                    pe_v[(vh, ts)] = pv_
                    evw = [("PE", pv_)]
                    if g >= 4:
                        evw.append(("WC", v_contribs[g - 4]))
                    vev[g] = E("vector", (lambda e, par=par, g=g:
                                          e.tensor_copy(
                                              vstage[g % 4], pb[par][:])),
                               waits=evw, inc="VE")
                    v_contribs.append(dma(
                        "gpsimd", "WC",
                        contrib[1024 + 128 * ts:1024 + 128 * ts + 128,
                                512 * vh:512 * vh + 512],
                        vstage[g % 4], waits=[("VE", vev[g])]))

            if dbg:
                dma("sync", "DBG", dbg_t["dbg_contrib"], contrib[:, :],
                    waits=[("WC", sch.val("WC"))])
            if upto < 4:
                raise _Stop

            E("gpsimd", (lambda e: e.collective_compute(
                "AllGather", ALU.bypass, replica_groups=[list(range(NCORES))],
                ins=[contrib.ap().opt()], outs=[gathered.ap().opt()])),
                waits=[("WC", sch.val("WC"))], inc="CC")

            for oc in range(NH):
                wpar = oc % 2
                if oc in qwv:
                    v_w = qwv[oc]
                else:
                    v_w = dma("gpsimd", wsem[wpar], wbr[wpar],
                              wqT_r[:, :, 128 * oc:128 * oc + 128],
                              waits=[("PE", wpar_last_use[wpar])])
                for tt in range(2):
                    rv = proj_rope_block(tt, tt,
                                         Ar[:, oc, 512 * tt:512 * tt + 512],
                                         v_w, wpar, is_q=True)
                wpar_last_use[wpar] = sch.val("PE")
            v_q = rope_ve[-1]

            if dbg:
                dma("sync", "DBG", dbg_t["dbg_q"], poolA[:],
                    waits=[("VE", v_q)])

            vqr = E("vector", (lambda e: e.reduce_max(
                qmax[:], qmaxw[:], axis=AXX)), inc="VE")
            v_stg = dma("sync", "SG",
                        statg[:].rearrange("p (r h) -> p r h", r=8),
                        gathered.ap().rearrange(
                            "(r q) j -> r q j",
                            r=NCORES)[:, 2048, 0:8].unsqueeze(0),
                        waits=[("CC", 1)])
            E("vector", (lambda e: e.tensor_copy(statf[:], statg[:])),
                waits=[("SG", v_stg)])
            v_kg0 = E("vector", (lambda e: e.reduce_max(
                stat2[:, 0:1], statf[:], axis=AXX)), inc="VE")
            se_kg = E("scalar", (lambda e: e.activation(
                stat_s[0:1, 1:2], stat2[0:1, 0:1], AF.Copy)),
                waits=[("VE", v_kg0)], inc="SE")
            pbc = E("tensor", (lambda e: e.matmul(
                bslot[:, 0:1], onesrow, stat_s[:, 1:2], start=True, stop=True,
                skip_group_check=True)),
                waits=[("SE", se_kg), ("VE", norm_ve[-1])], inc="PE")
            vkmb = E("vector", (lambda e: e.tensor_copy(kmb[:],
                                                         bslot[:, 0:1])),
                     waits=[("PE", pbc)], inc="VE")
            po1 = E("gpsimd", (lambda e: e.tensor_tensor(
                negB[:], qmax[:], kmb[:], ALU.mult)),
                waits=[("VE", max(vqr, vkmb))], inc="PO")
            E("scalar", (lambda e: e.activation(
                kred[:], negB[:], AF.Ln)), waits=[("PO", po1)])
            v_b2 = E("scalar", (lambda e: e.activation(
                negB[:], kred[:], AF.Exp, scale=0.5, bias=lnc_t[:, 0:1])),
                inc="SE")
            v_negB = E("vector", (lambda e: e.tensor_scalar(
                negB[:], negB[:], 60.0, -1.0, ALU.min, ALU.mult)),
                waits=[("SE", v_b2)], inc="VE")
            if dbg:
                dbgst = sb("dbgst", [128, 8], F32)
                E("vector", lambda e: e.tensor_copy(dbgst[:, 0:1], negB[:]),
                  waits=[("VE", v_negB)])
                E("vector", lambda e: e.tensor_copy(dbgst[:, 1:2], qmax[:]))
                E("vector", lambda e: e.tensor_copy(dbgst[:, 2:3], kmax[:]))
                E("vector", lambda e: e.tensor_copy(dbgst[:, 3:4], kmb[:]))
                vdb = E("vector", lambda e: e.tensor_copy(
                    dbgst[0:1, 4:8], statf[:, 0:4]), inc="VE")
                dma("sync", "DBG", dbg_t["dbg_negB"], dbgst[:],
                    waits=[("VE", vdb)])
            if upto < 5:
                raise _Stop

            gat = gathered.ap().rearrange("(r q) j -> r q j", r=NCORES)
            Aq = poolA[:].rearrange("p (g t) -> p g t", g=NH)
            kb_r = [k[:].rearrange("p (r j) -> p r j", r=8) for k in kbuf]
            vb_r = [v[:].rearrange("p (r c) -> p r c", r=8) for v in vbuf]
            kksem = ["KK0", "KK1"]
            vvsem = ["VV0", "VV1"]
            obank = (pb[2], pb[3])

            pairs = []
            for t in range(QT):
                for kvh in range(NKV):
                    u = t * NKV + kvh
                    n = 4 * (t + 1)
                    for pi in range(n):
                        jb, h = pi // 4, pi % 4
                        pairs.append(dict(t=t, kvh=kvh, u=u, jb=jb,
                                          r0=2 * h, r1=2 * h + 1,
                                          first=(pi == 0), last=(pi == n - 1)))
            NPAIR = len(pairs)
            st_pe = [0] * NPAIR
            exp_seL = [0] * NPAIR
            av_peL = [0] * NPAIR
            kv_done = {}
            nload = [0]
            last_st_on_kpar = {0: 0, 1: 0}
            last_av_on_kpar = {0: 0, 1: 0}
            last_ev = [0]

            def kv_load(p):
                pr = pairs[p]
                key = (pr["u"], pr["jb"])
                if key in kv_done:
                    return kv_done[key]
                t, kvh, jb = pr["t"], pr["kvh"], pr["jb"]
                kpar = nload[0] % 2
                nload[0] += 1
                ksrc = gat[:, kvh * 128:kvh * 128 + 128,
                           jb * 128:jb * 128 + 128].transpose((1, 0, 2))
                vsrc = gat[:, 1024 + jb * 128:1024 + jb * 128 + 128,
                           kvh * 128:kvh * 128 + 128].transpose((1, 0, 2))
                v_k = dma("sync", kksem[kpar], kb_r[kpar], ksrc,
                          waits=[("PE", last_st_on_kpar[kpar]), ("CC", 1)])
                vw = [("PE", last_av_on_kpar[kpar])]
                if nload[0] <= 2:
                    vw.append(("VE", v_memset))
                v_v = dma("sync", vvsem[kpar], vb_r[kpar][:, :, 0:128], vsrc,
                          waits=vw)
                kv_done[key] = (kpar, v_k, v_v)
                return kv_done[key]

            def emit_stgroup(p):
                pr = pairs[p]
                sp = p % 3
                kpar, v_k, v_v = kv_load(p)
                pr["kpar"], pr["v_v"] = kpar, v_v
                stw = [(kksem[kpar], v_k), ("VE", v_q)]
                if p < 3:
                    stw.append(("VE", v_negB))
                if p >= 3:
                    stw.append(("SE", exp_seL[p - 3]))
                qrhs = Aq[:, 4 * pr["kvh"]:4 * pr["kvh"] + 4,
                          128 * pr["t"]:128 * pr["t"] + 128]
                diag = (pr["jb"] == pr["t"])
                vst = 0
                for h, r in ((0, pr["r0"]), (1, pr["r1"])):
                    lastmm = (h == 1) and not diag
                    vst = E("tensor", (lambda e, kpar=kpar, r=r, sp=sp, h=h,
                                       qrhs=qrhs, diag=diag: e.matmul(
                                           stq[sp][:, 512 * h:512 * h + 512],
                                           kb_r[kpar][:, r, :], qrhs,
                                           start=True, stop=not diag,
                                           skip_group_check=True)),
                            waits=stw if h == 0 else (),
                            inc="PE" if lastmm else None)
                    if diag:
                        vst = E("tensor", (lambda e, sp=sp, h=h, r=r: e.matmul(
                            stq[sp][:, 512 * h:512 * h + 512], eye,
                            masks_s[:, 512 * r:512 * r + 512],
                            start=False, stop=True, skip_group_check=True)),
                            inc="PE" if h == 1 else None)
                last_st_on_kpar[kpar] = vst
                st_pe[p] = vst

            def emit_exp(p):
                sp = p % 3
                ew = [("PE", st_pe[p]), ("VE", v_negB)]
                if p >= 3:
                    ew.append(("PE", av_peL[p - 3]))
                exp_seL[p] = E("scalar", (lambda e, sp=sp: e.activation(
                    ptb[sp][:], stq[sp][:], AF.Exp,
                    bias=negB[:, 0:1], scale=SCL)), waits=ew, inc="SE")

            def emit_avgroup(p):
                pr = pairs[p]
                sp = p % 3
                ob = obank
                kpar, v_v = pr["kpar"], pr["v_v"]
                avw = [("SE", exp_seL[p]), (vvsem[kpar], v_v)]
                if pr["first"]:
                    avw.append(("VE", last_ev[0]))
                vav = 0
                for h, r in ((0, pr["r0"]), (1, pr["r1"])):
                    for g in range(4):
                        st_flag = pr["first"] and h == 0 and (g % 2 == 0)
                        vav = E("tensor", (lambda e, sp=sp, h=h, r=r, g=g,
                                           ob=ob, kpar=kpar, st_flag=st_flag:
                                           e.matmul(
                                               ob[g // 2][:, 129 * (g % 2):
                                                          129 * (g % 2) + 129],
                                               ptb[sp][:, 512 * h + 128 * g:
                                                       512 * h + 128 * g + 128],
                                               vb_r[kpar][:, r, 0:129],
                                               start=st_flag, stop=False,
                                               skip_group_check=True)),
                                waits=avw if (h == 0 and g == 0) else (),
                                inc="PE" if (h == 1 and g == 3) else None)
                av_peL[p] = vav
                last_av_on_kpar[kpar] = vav
                if pr["last"]:
                    emit_evac(pr, vav)

            def emit_evac(pr, v_av_all):
                ob = obank
                t, kvh = pr["t"], pr["kvh"]
                E("scalar", (lambda e, ob=ob: e.activation(
                    rl[:, 0:2], ob[0][:, 128:259:129], AF.Ln)),
                    waits=[("PE", v_av_all)])
                E("scalar", (lambda e, ob=ob: e.activation(
                    rl[:, 2:4], ob[1][:, 128:259:129], AF.Ln)))
                v_l = E("scalar", (lambda e: e.activation(
                    rl[:], rl[:], AF.Exp, scale=-1.0)), inc="SE")
                vsc = 0
                for g in range(4):
                    gg = 4 * kvh + g
                    vsc = E("vector", (lambda e, ob=ob, g=g, gg=gg, t=t:
                                       e.tensor_scalar(
                                           Aq[:, gg, 128 * t:128 * t + 128],
                                           ob[g // 2][:, 129 * (g % 2):
                                                      129 * (g % 2) + 128],
                                           rl[:, g:g + 1], None, ALU.mult)),
                            waits=[("SE", v_l)] if g == 0 else (), inc="VE")
                last_ev[0] = vsc

            for pp in range(min(3, NPAIR)):
                emit_stgroup(pp)
            for p in range(NPAIR):
                emit_exp(p)
                if p + 3 < NPAIR:
                    emit_stgroup(p + 3)
                emit_avgroup(p)
            v_attn = last_ev[0]

            trbit = stp1[:].bitcast(BF16)
            tslots = [trbit[:, 0:128], trbit[:, 1024:1152]]
            tr_ve = []
            for t in range(QT):
                for g in range(NH):
                    idx = len(tr_ve)
                    tw = []
                    if idx == 0:
                        tw.append(("VE", v_attn))
                    if idx >= 2:
                        tw.append(("VE", tr_ve[idx - 2]))
                    ptv = E("tensor", (lambda e, g=g, t=t, idx=idx:
                                       e.transpose(
                                           tslots[idx % 2],
                                           Aq[:, g, 128 * t:128 * t + 128],
                                           eye)), waits=tw, inc="PE")
                    tv = E("vector", (lambda e, g=g, t=t, idx=idx:
                                      e.tensor_copy(
                                          Br[:, g, 128 * t:128 * t + 128],
                                          tslots[idx % 2])),
                           waits=[("PE", ptv)], inc="VE")
                    tr_ve.append(tv)
            v_attn = tr_ve[-1]

            if dbg:
                dma("sync", "DBG", dbg_t["dbg_attnT"], poolB[:],
                    waits=[("VE", v_attn)])
            if upto < 6:
                raise _Stop

            woT_r = woT.rearrange("(c p) o -> p c o", c=IC)
            xT_rr = xT.rearrange("(c p) t -> c p t", c=IC)
            xrel = [scr8[:, 0:512], scr8[:, 512:1024]]
            xrsem = ["XR0", "XR1"]
            x1f = fscr[:, 0:512]
            x1f_ve = []
            x1_st = []
            wpar_lu = {0: sch.val("PE"), 1: sch.val("PE")}
            v_ln2pe = 0
            for oc in range(IC):
                wpar = oc % 2
                v_w = dma("gpsimd", wsem[wpar], wbr[wpar],
                          woT_r[:, :, 128 * oc:128 * oc + 128],
                          waits=[("PE", wpar_lu[wpar])])
                for tt in range(2):
                    g = oc * 2 + tt
                    par = tt
                    xpar = g % 2
                    xw = []
                    if g >= 2:
                        xw.append(("VE", x1f_ve[g - 2]))
                    v_x = dma("sync", xrsem[xpar], xrel[xpar],
                              xT_rr[oc, :, 512 * tt:512 * tt + 512], waits=xw)
                    mw = [(wsem[wpar], v_w), ("VE", v_attn)]
                    if g >= 2:
                        mw.append(("VE", x1f_ve[g - 2]))
                    for ic in range(IC):
                        v_pe = E("tensor", (lambda e, ic=ic, tt=tt, par=par,
                                            wpar=wpar: e.matmul(
                                                pb[par][:],
                                                wbr[wpar][:, ic, :],
                                                Br[:, ic,
                                                   512 * tt:512 * tt + 512],
                                                start=(ic == 0),
                                                stop=(ic == IC - 1),
                                                skip_group_check=True)),
                                 waits=mw if ic == 0 else (),
                                 inc="PE" if ic == IC - 1 else None)
                    fw = [("PE", v_pe), (xrsem[xpar], v_x)]
                    if x1_st:
                        fw.append(("ST1", x1_st[-1]))
                    E("vector", (lambda e, par=par, xpar=xpar:
                                 e.tensor_tensor(
                                     x1f, pb[par][:], xrel[xpar], ALU.add)),
                        waits=fw)
                    vx1 = E("vector", lambda e: e.tensor_copy(
                        fscr[:, 512:1024], x1f), inc="VE")
                    vst = dma("sync", "ST1",
                              x1d[128 * oc:128 * oc + 128,
                                  512 * tt:512 * tt + 512],
                              fscr[:, 512:1024], waits=[("VE", vx1)])
                    x1_st.append(vst)
                    E("vector", (lambda e, oc=oc, tt=tt: e.tensor_copy(
                        Ar[:, oc, 512 * tt:512 * tt + 512], x1f)))
                    vsq = E("vector", (lambda e, oc=oc, tt=tt, xpar=xpar:
                                       e.tensor_tensor(
                                           xsq[xpar],
                                           Ar[:, oc,
                                              512 * tt:512 * tt + 512],
                                           Ar[:, oc,
                                              512 * tt:512 * tt + 512],
                                           ALU.mult)), inc="VE")
                    x1f_ve.append(vsq)
                    E("tensor", (lambda e, tt=tt, oc=oc: e.matmul(
                        pb[2 + tt][:], ones,
                        Ar[:, oc, 512 * tt:512 * tt + 512],
                        start=(oc == 0), stop=(oc == IC - 1),
                        skip_group_check=True)), waits=[("VE", vsq)])
                    v_ln2pe = E("tensor", (lambda e, tt=tt, oc=oc, xpar=xpar:
                                           e.matmul(
                                               pb[4 + tt][:], ones,
                                               xsq[xpar],
                                               start=(oc == 0),
                                               stop=(oc == IC - 1),
                                               skip_group_check=True)),
                                inc="PE")
                wpar_lu[wpar] = sch.val("PE")
            if dbg:
                dma("sync", "DBG", dbg_t["dbg_x1"], x1d[:, :],
                    waits=[("ST1", x1_st[-1])])
            if upto < 7:
                raise _Stop

            v_ln2 = 0
            ln2_chunks = {}
            for tt in range(2):
                v_ln2, ln2_chunks[tt] = ln_finalize_and_apply(
                    tt, 2 + tt, 4 + tt, Ar, Br, v_ln2pe)
            if dbg:
                dma("sync", "DBG", dbg_t["dbg_ln2"], poolB[:],
                    waits=[("VE", v_ln2)])
            if upto < 8:
                raise _Stop

            u1pad = upads[:, 0:1024]
            u3pad = upads[:, 1024:2048]
            u2pad = upads[:, 2048:3072]
            a1r = a1s[:].rearrange("p (c r) -> p c r", c=IC)
            a3r = a3s[:].rearrange("p (c r) -> p c r", c=IC)
            v_use = 0
            for tt in range(2):
                for (ar, upad) in ((a1r, u1pad), (a3r, u3pad)):
                    for ic in range(IC):
                        w = [("VE", ln2_chunks[tt][ic]), ("W0", W0F)]
                        if v_use:
                            w.append(("SE", v_use))
                        pu = E("tensor", (lambda e, ic=ic, tt=tt, ar=ar:
                                          e.matmul(
                                              nrm[0:2, :], ar[:, ic, :],
                                              Br[:, ic,
                                                 512 * tt:512 * tt + 512],
                                              start=(ic == 0),
                                              stop=(ic == IC - 1),
                                              skip_group_check=True)),
                               waits=w,
                               inc="PE" if ic == IC - 1 else None)
                    v_use = E("scalar", (lambda e, tt=tt, upad=upad:
                                         e.activation(
                                             upad[0:2,
                                                  512 * tt:512 * tt + 512],
                                             nrm[0:2, :], AF.Copy,
                                             scale=0.5)),
                              waits=[("PE", pu)], inc="SE")
            hs = poolA[:, 0:8192].rearrange("p (c t) -> p c t", c=8)
            v_hs = 0
            for j in range(8):
                for tt in range(2):
                    E("vector", (lambda e, j=j, tt=tt: e.tensor_tensor(
                        fscr[:, 0:512], Br[:, j, 512 * tt:512 * tt + 512],
                        Br[:, j + 8, 512 * tt:512 * tt + 512], ALU.add)),
                        waits=[("VE", v_ln2)])
                    E("vector", (lambda e, j=j, tt=tt: e.tensor_tensor(
                        fscr[:, 512:1024],
                        Br[:, j + 16, 512 * tt:512 * tt + 512],
                        Br[:, j + 24, 512 * tt:512 * tt + 512], ALU.add)))
                    v_hs = E("vector", (lambda e, j=j, tt=tt:
                                        e.tensor_tensor(
                                            hs[:, j,
                                               512 * tt:512 * tt + 512],
                                            fscr[:, 0:512],
                                            fscr[:, 512:1024], ALU.add)),
                             inc="VE")

            w1T_r = w1T.rearrange("(c p) h -> p c h", c=9)
            w3T_r = w3T.rearrange("(c p) h -> p c h", c=9)
            gsv = poolB[:, 0:FC * 1024].rearrange("p (c t) -> p c t", c=FC)
            gcs = [[bscr[:, 512 * (2 * f + tt):512 * (2 * f + tt) + 512]
                    for tt in range(2)] for f in range(4)]
            sg = ptb[0][:, 0:512]
            a2r = a2s[:].rearrange("p (c r) -> p c r", c=HC)
            w1b = [wbufU[:, off:off + 1152].rearrange("p (c h) -> p c h", c=9)
                   for off in (0, 2304, 4608)]
            w3b = [wbufU[:, off + 1152:off + 2304]
                   .rearrange("p (c h) -> p c h", c=9)
                   for off in (0, 2304, 4608)]
            silu_se = []
            gc_ve = []
            u2_pe = {}
            pmax = max(wpar_lu[0], wpar_lu[1])
            wpar_last = {0: pmax, 1: pmax, 2: pmax}
            wsem3 = ["WA", "WB", "WF"]
            u2_started = {0: False, 1: False}
            v_gs = 0
            for p in range(FC):
                for f in range(4):
                    hc = 22 * f + p
                    wpar = (4 * p + f) % 3
                    dma("gpsimd", wsem3[wpar], w1b[wpar],
                        w1T_r[:, :, 128 * hc:128 * hc + 128],
                        waits=[("PE", wpar_last[wpar])])
                    v_w3 = dma("gpsimd", wsem3[wpar], w3b[wpar],
                               w3T_r[:, :, 128 * hc:128 * hc + 128])
                    for tt in range(2):
                        i_gu = len(silu_se)
                        gbank = [pb[0], nrm][i_gu % 2]
                        ubank = [pb[1], bslot][i_gu % 2]
                        sgb = [sg, ptb[0][:, 512:1024]][i_gu % 2]
                        mw = [(wsem3[wpar], v_w3), ("VE", v_hs)]
                        if v_use:
                            mw.append(("SE", v_use))
                        if i_gu >= 2:
                            mw.append(("SE", silu_se[i_gu - 2]))
                        for ic in range(9):
                            rhs = (hs[:, ic, 512 * tt:512 * tt + 512]
                                   if ic < 8
                                   else u1pad[:, 512 * tt:512 * tt + 512])
                            E("tensor", (lambda e, ic=ic, wpar=wpar, rhs=rhs,
                                         gbank=gbank: e.matmul(
                                             gbank, w1b[wpar][:, ic, :],
                                             rhs, start=(ic == 0),
                                             stop=(ic == 8),
                                             skip_group_check=True)),
                                waits=mw if ic == 0 else ())
                        uw = [("VE", gc_ve[i_gu - 2])] if i_gu >= 2 else []
                        for ic in range(9):
                            rhs = (hs[:, ic, 512 * tt:512 * tt + 512]
                                   if ic < 8
                                   else u3pad[:, 512 * tt:512 * tt + 512])
                            v_gu = E("tensor", (lambda e, ic=ic, wpar=wpar,
                                                rhs=rhs, ubank=ubank:
                                                e.matmul(
                                                    ubank,
                                                    w3b[wpar][:, ic, :],
                                                    rhs, start=(ic == 0),
                                                    stop=(ic == 8),
                                                    skip_group_check=True)),
                                     waits=uw if ic == 0 else (),
                                     inc="PE" if ic == 8 else None)
                        sse = E("scalar", (lambda e, gbank=gbank, sgb=sgb:
                                           e.activation(
                                               sgb, gbank, AF.Silu)),
                            waits=[("PE", v_gu)], inc="SE")
                        silu_se.append(sse)
                        gw = [("SE", sse)]
                        if (f, tt) in u2_pe:
                            gw.append(("PE", u2_pe[(f, tt)]))
                        vgc = E("vector", (lambda e, f=f, tt=tt, sgb=sgb,
                                           ubank=ubank: e.tensor_tensor(
                                               gcs[f][tt], sgb, ubank,
                                               ALU.mult)),
                                waits=gw, inc="VE")
                        gc_ve.append(vgc)
                        u2w = [("VE", vgc), ("W0", W0F)]
                        if not u2_started[tt]:
                            u2w.append(("VE", v_ln2))
                        pu2 = E("tensor", (lambda e, hc=hc, f=f, tt=tt,
                                           s=not u2_started[tt]: e.matmul(
                                               pb[2 + tt][0:2, :],
                                               a2r[:, hc, :], gcs[f][tt],
                                               start=s,
                                               stop=(hc == HC - 1),
                                               skip_group_check=True)),
                                 waits=u2w, inc="PE")
                        u2_started[tt] = True
                        u2_pe[(f, tt)] = pu2
                    wpar_last[wpar] = sch.val("PE")
                for tt in range(2):
                    E("vector", (lambda e, tt=tt: e.tensor_tensor(
                        fscr[:, 0:512], gcs[0][tt], gcs[1][tt], ALU.add)),
                        waits=[("PE", u2_pe[(3, tt)])])
                    E("vector", (lambda e, tt=tt: e.tensor_tensor(
                        fscr[:, 512:1024], gcs[2][tt], gcs[3][tt], ALU.add)))
                    v_gs = E("vector", (lambda e, p=p, tt=tt:
                                        e.tensor_tensor(
                                            gsv[:, p,
                                                512 * tt:512 * tt + 512],
                                            fscr[:, 0:512],
                                            fscr[:, 512:1024], ALU.add)),
                             inc="VE")
            E("scalar", (lambda e: e.activation(
                u2pad[0:2, 0:512], pb[2][0:2, :], AF.Copy, scale=0.5)),
                waits=[("PE", u2_pe[(3, 0)])])
            v_u2b = E("scalar", (lambda e: e.activation(
                u2pad[0:2, 512:1024], pb[3][0:2, :], AF.Copy, scale=0.5)),
                waits=[("PE", u2_pe[(3, 1)])], inc="SE")
            if dbg:
                dma("sync", "DBG", dbg_t["dbg_gs"], poolB[:, 0:FC * 1024],
                    waits=[("VE", v_gs)])

            w2T_r = w2T.rearrange("(c p) o -> p c o", c=23)

            x2f = fscr[:, 512:1024]
            y_ve = []
            y_st = []
            st1_full = x1_st[-1]
            wpar_last = {0: sch.val("PE"), 1: sch.val("PE")}
            w2b = [wbufU[:, 0:2944].rearrange("p (c o) -> p c o", c=23),
                   wbufU[:, 2944:5888].rearrange("p (c o) -> p c o", c=23)]
            for oc in range(IC):
                wpar = oc % 2
                v_w = dma("gpsimd", wsem[wpar], w2b[wpar],
                          w2T_r[:, :, 128 * oc:128 * oc + 128],
                          waits=[("PE", wpar_last[wpar])])
                for tt in range(2):
                    g = oc * 2 + tt
                    par = tt
                    xpar = g % 2
                    xw = [("ST1", st1_full)]
                    if g >= 2:
                        xw.append(("VE", y_ve[g - 2]))
                    v_x = dma("sync", xrsem[xpar], xrel[xpar],
                              x1d[128 * oc:128 * oc + 128,
                                  512 * tt:512 * tt + 512], waits=xw)
                    mw = [(wsem[wpar], v_w), ("VE", v_gs), ("SE", v_u2b)]
                    if g >= 2:
                        mw.append(("VE", y_ve[g - 2]))
                    for ic in range(23):
                        rhs = (gsv[:, ic, 512 * tt:512 * tt + 512]
                               if ic < 22
                               else u2pad[:, 512 * tt:512 * tt + 512])
                        v_pe = E("tensor", (lambda e, ic=ic, par=par,
                                            wpar=wpar, rhs=rhs: e.matmul(
                                                pb[par][:],
                                                w2b[wpar][:, ic, :], rhs,
                                                start=(ic == 0),
                                                stop=(ic == 22),
                                                skip_group_check=True)),
                                 waits=mw if ic == 0 else (),
                                 inc="PE" if ic == 22 else None)
                    fw = [("PE", v_pe), (xrsem[xpar], v_x)]
                    if y_st:
                        fw.append(("ST2", y_st[-1]))
                    vy = E("vector", (lambda e, par=par, xpar=xpar:
                                      e.tensor_tensor(
                                          x2f, pb[par][:], xrel[xpar],
                                          ALU.add)), waits=fw, inc="VE")
                    y_ve.append(vy)
                    vst = dma("sync", "ST2",
                              out[128 * oc:128 * oc + 128,
                                  512 * tt:512 * tt + 512],
                              x2f, waits=[("VE", vy)])
                    y_st.append(vst)
                wpar_last[wpar] = sch.val("PE")

        try:
            _sched_all()
        except _Stop:
            pass

        all_dma_sems = [s for s in sch.ctr
                        if s not in ("PE", "VE", "SE", "PO", "CC")]
        E("gpsimd", lambda e: None,
          waits=[(s, sch.val(s)) for s in all_dma_sems])

        semnames = sorted({s for ops in sch.ops.values()
                           for (ws, _, inc, _) in ops
                           for s in ([w[0] for w in ws] +
                                     ([inc] if inc else []))})
        sems = {s: ctx.enter_context(nc.semaphore(s)) for s in semnames}
        block = ctx.enter_context(nc.Block())

        def mk(name):
            def f(eng):
                seen = {}
                for waits, fn, inc, inc_n in sch.ops[name]:
                    for (s, v) in waits:
                        if v > 0 and seen.get(s, 0) < v:
                            eng.wait_ge(sems[s], v)
                            seen[s] = v
                    instr = fn(eng)
                    if inc is not None:
                        instr.then_inc(sems[inc], inc_n)
            return f

        block.sync(mk("sync"))
        block.gpsimd(mk("gpsimd"))
        block.tensor(mk("tensor"))
        block.vector(mk("vector"))
        block.scalar(mk("scalar"))

    return nc



def _stage(inputs):
    x = np.asarray(inputs["L0_attn_hidden_states"])[0]
    Wq = np.asarray(inputs["L0_attn_Wq"])
    Wk = np.asarray(inputs["L0_attn_Wk"])
    Wv = np.asarray(inputs["L0_attn_Wv"])
    Wo = np.asarray(inputs["L0_attn_Wo"])
    w1q = np.asarray(inputs["L0_ffn_w1q"])
    a1 = np.asarray(inputs["L0_ffn_a1"])
    b1 = np.asarray(inputs["L0_ffn_b1"])
    w3q = np.asarray(inputs["L0_ffn_w3q"])
    a3 = np.asarray(inputs["L0_ffn_a3"])
    b3 = np.asarray(inputs["L0_ffn_b3"])
    w2q = np.asarray(inputs["L0_ffn_w2q"])
    a2 = np.asarray(inputs["L0_ffn_a2"])
    b2 = np.asarray(inputs["L0_ffn_b2"])

    C = np.ascontiguousarray
    f32 = np.float32

    wqT = C(Wq.T.astype(f32))
    wkT = C(Wk.T.astype(f32))
    wvT = C(Wv.T.astype(f32))
    woT = C(Wo.T.astype(f32))

    def pad_gate(wT, bT):
        outw = np.zeros((1152, HIDP), f32)
        for f in range(4):
            outw[0:1024, 2816 * f:2816 * f + 2752] = \
                wT[:, 2752 * f:2752 * f + 2752]
            outw[1024:1026, 2816 * f:2816 * f + 2752] = \
                bT[:, 2752 * f:2752 * f + 2752]
        return outw

    w1T = pad_gate(C(w1q.T.astype(f32)), C(b1.T.astype(f32)))
    w3T = pad_gate(C(w3q.T.astype(f32)), C(b3.T.astype(f32)))

    w2T = np.zeros((W2K + 128, D), f32)
    w2T[0:2752, :] = w2q.T.astype(f32)
    w2T[W2K:W2K + 2, :] = b2.T.astype(f32)

    def a_small(a, nch):
        acx = a.T.astype(f32).reshape(nch, 128, a.shape[0])
        return C(acx.transpose(1, 0, 2).reshape(128, nch * a.shape[0]))

    a1cv = a_small(a1, IC)
    a3cv = a_small(a3, IC)
    a2p = np.zeros((2, HIDP), f32)
    for f in range(4):
        a2p[:, 2816 * f:2816 * f + 2752] = a2[:, 2752 * f:2752 * f + 2752]
    a2cv = a_small(a2p, HC)

    inv = (1.0 / (10000.0 ** (np.arange(0, HD, 2, dtype=np.float64) / HD)))
    bf = ml_dtypes.bfloat16

    eye_ones = np.zeros((128, 256), f32)
    eye_ones[:, 0:128] = np.eye(128)
    eye_ones[:, 128:256] = 1.0
    consts_v = eye_ones.astype(bf)

    in_maps = []
    for c in range(NCORES):
        xc = C(x[c::8, :].T.astype(f32))
        pos = (c + 8 * np.arange(R)).astype(np.float64)
        fr = np.outer(pos, inv)
        emb = np.concatenate([fr, fr], axis=1)
        cosv = C(np.cos(emb).T.astype(f32))
        sinv = np.sin(emb).T.astype(f32)
        sins = sinv.copy()
        sins[0:64, :] = -sinv[0:64, :]
        sins = C(sins)

        jj = np.arange(128)[:, None]
        mmi = np.arange(128)[None, :]
        allow = (jj < mmi)
        mrow = []
        for rr in range(8):
            al = allow | ((jj == mmi) & (rr <= c))
            mk = np.where(al, 0.0, -1e9).astype(f32)
            mrow.append(np.tile(mk, (1, 4)))
        maskv = np.concatenate(mrow, axis=1).astype(bf)

        in_maps.append({
            "xT": xc, "wqT": wqT, "wkT": wkT, "wvT": wvT, "woT": woT,
            "w1T": w1T, "w3T": w3T, "w2T": w2T,
            "a1c": a1cv, "a3c": a3cv, "a2c": a2cv,
            "cosq": cosv, "sinq": sins,
            "masks": maskv, "consts": consts_v,
        })
    return in_maps


def kernel(**inputs):
    in_maps = _stage(inputs)
    nc = build(dbg=False)
    res = run_bass_kernel_spmd(nc, in_maps, core_ids=list(range(NCORES)))
    full = np.empty((1, S, D), dtype=np.float32)
    for c in range(NCORES):
        full[0, c::8, :] = np.asarray(res.results[c]["out"],
                                      dtype=np.float32).T
    return full


# version 09
# speedup vs baseline: 1.1012x; 1.0077x over previous
import math
import numpy as np
import ml_dtypes
import concourse.bass as bass
import concourse.mybir as mybir
from concourse.bass_utils import run_bass_kernel_spmd
from contextlib import ExitStack

F32 = mybir.dt.float32
BF16 = mybir.dt.bfloat16
AF = mybir.ActivationFunctionType
ALU = mybir.AluOpType
AXX = mybir.AxisListType.X

NCORES = 8
S = 8192
D = 4096
NH = 32
NKV = 8
HD = 128
R = 1024
IC = 32
QT = 8
HIDP = 11264
HC = 88
FC = 22
W2K = 2816
RPC = 2049
SCL = 1.0 / math.sqrt(HD)
LNC_BIAS = -0.5 * math.log(HD)
ENGINES = ("sync", "gpsimd", "tensor", "vector", "scalar")


class Sched:
    def __init__(self):
        self.ops = {e: [] for e in ENGINES}
        self.ctr = {}

    def val(self, sem):
        return self.ctr.get(sem, 0)

    def emit(self, eng, fn, waits=(), inc=None, inc_n=1):
        post = None
        if inc is not None:
            self.ctr[inc] = self.ctr.get(inc, 0) + inc_n
            post = self.ctr[inc]
        self.ops[eng].append((tuple(waits), fn, inc, inc_n))
        return post


class _Stop(Exception):
    pass


def build(dbg=False, upto=99):
    nc = bass.Bass("TRN2", target_bir_lowering=False)

    def din(name, shape, dt=F32):
        return nc.dram_tensor(name, shape, dt, kind="ExternalInput").ap()

    xT = din("xT", [D, R])
    wqT = din("wqT", [D, D])
    wkT = din("wkT", [D, NKV * HD])
    wvT = din("wvT", [D, NKV * HD])
    woT = din("woT", [D, D])
    w1T = din("w1T", [1024 + 128, HIDP])
    w3T = din("w3T", [1024 + 128, HIDP])
    w2T = din("w2T", [W2K + 128, D])
    a1c = din("a1c", [128, IC * 2])
    a3c = din("a3c", [128, IC * 2])
    a2c = din("a2c", [128, HC * 2])
    cosq = din("cosq", [128, R])
    sinq = din("sinq", [128, R])
    masks = din("masks", [128, 8 * 512], BF16)
    consts = din("consts", [128, 256], BF16)

    out = nc.dram_tensor("out", [D, R], F32, kind="ExternalOutput").ap()

    contrib = nc.dram_tensor("contrib", [RPC, 1024], BF16)
    gathered = nc.dram_tensor("gathered", [NCORES * RPC, 1024], BF16)
    x1d = nc.dram_tensor("x1d", [D, R], F32)

    dbg_t = {}
    if dbg:
        for nm, shp, dt in [
            ("dbg_ln1", [128, IC * 1024], BF16),
            ("dbg_q", [128, IC * 1024], BF16),
            ("dbg_contrib", [RPC, 1024], BF16),
            ("dbg_negB", [128, 8], F32),
            ("dbg_attnT", [128, IC * 1024], BF16),
            ("dbg_x1", [D, R], F32),
            ("dbg_ln2", [128, IC * 1024], BF16),
            ("dbg_gs", [128, FC * 1024], BF16),
        ]:
            dbg_t[nm] = nc.dram_tensor(nm, shp, dt, kind="ExternalOutput").ap()

    sch = Sched()
    with ExitStack() as ctx:
        def sb(name, shape, dt):
            return ctx.enter_context(nc.sbuf_tensor(name, shape, dt))

        def pst(name, shape, dt):
            return ctx.enter_context(nc.psum_tensor(name, shape, dt))

        poolA = sb("poolA", [128, 32768], BF16)
        poolB = sb("poolB", [128, 32768], BF16)
        wbufU = sb("wbufU", [128, 8192], BF16)
        wbufA = wbufU[:, 0:4096]
        wbufB = wbufU[:, 4096:8192]
        masks_s = sb("masks_s", [128, 4096], BF16)
        consts_s = sb("consts_s", [128, 256], BF16)
        scr8 = sb("scr8", [128, 2048], F32)
        fscr = sb("fscr", [128, 1024], F32)
        bscr = sb("bscr", [128, 4096], BF16)
        upads = sb("upads", [128, 3072], BF16)
        kbuf = [sb(f"kbuf{i}", [128, 1024], BF16) for i in range(2)]
        vbuf = [sb(f"vbuf{i}", [128, 8 * 136], BF16) for i in range(2)]
        ptb = [sb(f"ptb{i}", [128, 1024], BF16) for i in range(3)]
        m_s = sb("m_s", [128, 512], F32)
        r_s = sb("r_s", [128, 512], F32)
        mb16 = sb("mb16", [128, 512], BF16)
        a1s = sb("a1s", [128, IC * 2], BF16)
        a3s = sb("a3s", [128, IC * 2], BF16)
        a2s = sb("a2s", [128, HC * 2], BF16)
        qmaxw = sb("qmaxw", [128, 512], F32)
        kmaxw = sb("kmaxw", [128, 512], F32)
        qmax = sb("qmax", [128, 1], F32)
        kmax = sb("kmax", [128, 1], F32)
        kred = sb("kred", [128, 1], F32)
        kmb = sb("kmb", [128, 1], F32)
        negB = sb("negB", [128, 1], F32)
        lnc_t = sb("lnc_t", [128, 1], F32)
        rl = sb("rl", [128, 4], F32)
        stat_s = sb("stat_s", [1, 16], BF16)
        stat2 = sb("stat2", [1, 4], F32)
        epst = sb("epst", [128, 1], F32)
        statg = sb("statg", [1, 64], BF16)
        kst = sb("kst", [128, 1024], BF16)
        statf = sb("statf", [1, 64], F32)

        stp0 = pst("stp0", [128, 1024], F32)
        stp1 = pst("stp1", [128, 1024], F32)
        ob01 = pst("ob01", [128, 1024], F32)
        ob23 = pst("ob23", [128, 1024], F32)
        pb = [stp0[:, 0:512], stp0[:, 512:1024],
              ob01[:, 0:512], ob01[:, 512:1024],
              ob23[:, 0:512], ob23[:, 512:1024]]
        stq = [stp0[:], stp1[:], ob23[:]]
        nrm = stp1[:, 0:512]
        bslot = stp1[:, 512:1024]

        eye = consts_s[:, 0:128]
        ones = consts_s[:, 128:256]
        onesrow = consts_s[0:1, 128:256]

        Ar = poolA[:].rearrange("p (c t) -> p c t", c=IC)
        Br = poolB[:].rearrange("p (c t) -> p c t", c=IC)

        E = sch.emit

        def dma(eng, sem_, o, i, waits=()):
            return E(eng, (lambda e, o=o, i=i: e.dma_start(out=o, in_=i)),
                     waits=waits, inc=sem_, inc_n=16)

        def _sched_all():
            xT_r = xT.rearrange("(c p) t -> p c t", c=IC)
            for q4 in range(4):
                dma("gpsimd", f"XB{q4}", Ar[:, 8 * q4:8 * q4 + 8, :],
                    xT_r[:, 8 * q4:8 * q4 + 8, :])
            dma("gpsimd", "W0", consts_s[:], consts)
            dma("gpsimd", "W0", masks_s[:], masks)
            dma("gpsimd", "W0", a1s[:], a1c)
            dma("gpsimd", "W0", a3s[:], a3c)
            dma("gpsimd", "W0", a2s[:], a2c)
            W0F = sch.val("W0")
            dma("sync", "K0", scr8[:, 0:1024], cosq)
            dma("sync", "K0", scr8[:, 1024:2048], sinq)
            K0F = sch.val("K0")
            cos = scr8[:, 0:1024]
            sin = scr8[:, 1024:2048]

            E("vector", lambda e: e.memset(epst[:], 1e-5))
            E("vector", lambda e: e.memset(lnc_t[:], LNC_BIAS))
            E("vector", lambda e: e.memset(qmaxw[:], 0.0))
            E("vector", lambda e: e.memset(kmaxw[:], 0.0))
            E("vector", lambda e: e.memset(upads[:], 0.0))
            E("vector", lambda e: e.memset(stat_s[:], 0.0))
            E("vector", lambda e: e.memset(vbuf[0][:], 1.0))
            v_memset = E("vector", lambda e: e.memset(vbuf[1][:], 1.0),
                         inc="VE")

            xsq = [bscr[:, 0:512], bscr[:, 512:1024]]
            for tt in range(2):
                for ic in range(IC):
                    w = [(f"XB{ic // 8}", 16)]
                    if ic == 0 and tt == 0:
                        w.append(("W0", W0F))
                    E("tensor", (lambda e, ic=ic, tt=tt: e.matmul(
                        pb[tt][:], ones, Ar[:, ic, 512 * tt:512 * tt + 512],
                        start=(ic == 0), stop=(ic == IC - 1),
                        skip_group_check=True)), waits=w)
            sq_pe = []
            for tt in range(2):
                for ic in range(IC):
                    g = tt * IC + ic
                    par = g % 2
                    w = [(f"XB{ic // 8}", 16)]
                    if g >= 2:
                        w.append(("PE", sq_pe[g - 2]))
                    vv = E("vector", (lambda e, ic=ic, tt=tt, par=par:
                                      e.tensor_tensor(
                                          xsq[par],
                                          Ar[:, ic, 512 * tt:512 * tt + 512],
                                          Ar[:, ic, 512 * tt:512 * tt + 512],
                                          ALU.mult)), waits=w, inc="VE")
                    pv = E("tensor", (lambda e, ic=ic, tt=tt, par=par: e.matmul(
                        pb[2 + tt][:], ones, xsq[par],
                        start=(ic == 0), stop=(ic == IC - 1),
                        skip_group_check=True)), waits=[("VE", vv)], inc="PE")
                    sq_pe.append(pv)
            v_stats_pe = sq_pe[-1]

            def ln_finalize_and_apply(tt, bank_m, bank_sq, src_r, dst_r,
                                      pe_wait):
                E("vector", (lambda e: e.tensor_scalar(
                    m_s[:], pb[bank_m][:], 1.0 / D, None, ALU.mult)),
                    waits=[("PE", pe_wait)])
                E("vector", (lambda e: e.tensor_scalar(
                    fscr[:, 0:512], pb[bank_sq][:], 1.0 / D, None, ALU.mult)))
                E("vector", (lambda e: e.tensor_tensor(
                    r_s[:], m_s[:], m_s[:], ALU.mult)))
                E("vector", (lambda e: e.tensor_tensor(
                    fscr[:, 0:512], fscr[:, 0:512], r_s[:], ALU.subtract)))
                vw = E("vector", lambda e: e.tensor_copy(mb16[:], m_s[:]),
                       inc="VE")
                E("scalar", (lambda e: e.activation(
                    fscr[:, 512:1024], fscr[:, 0:512], AF.Ln,
                    bias=epst[:, 0:1])), waits=[("VE", vw)])
                sw = E("scalar", (lambda e: e.activation(
                    r_s[:], fscr[:, 512:1024], AF.Exp, scale=-0.5)), inc="SE")
                last = 0
                chunk_incs = []
                for ic in range(IC):
                    E("vector", (lambda e, ic=ic: e.tensor_tensor(
                        fscr[:, 0:512], src_r[:, ic, 512 * tt:512 * tt + 512],
                        mb16[:], ALU.subtract)), waits=[("SE", sw)])
                    last = E("vector", (lambda e, ic=ic: e.tensor_tensor(
                        dst_r[:, ic, 512 * tt:512 * tt + 512], fscr[:, 0:512],
                        r_s[:], ALU.mult)), inc="VE")
                    chunk_incs.append(last)
                return last, chunk_incs

            v_ln1 = 0
            for tt in range(2):
                v_ln1, _ = ln_finalize_and_apply(tt, tt, 2 + tt, Ar, Br,
                                                 v_stats_pe)

            if dbg:
                dma("sync", "DBG", dbg_t["dbg_ln1"], poolB[:],
                    waits=[("VE", v_ln1)])
            if upto < 2:
                raise _Stop

            wsem = ["WA", "WB"]
            wkT_r = wkT.rearrange("(c p) d -> p c d", c=IC)
            wvT_r = wvT.rearrange("(c p) d -> p c d", c=IC)
            wb = [wbufA, wbufB]
            wbr = [w.rearrange("p (c o) -> p c o", c=IC) for w in wb]
            rope_ve = []
            norm_pe = []
            norm_ve = []
            wpar_last_use = {0: 0, 1: 0}

            def proj_rope_block(tt, par, dst_ap, v_w, wpar, is_q,
                                extra_mm_waits=()):
                gidx = len(rope_ve)
                w = [(wsem[wpar], v_w), ("K0", K0F), ("VE", v_ln1)]
                w += list(extra_mm_waits)
                if gidx >= 2:
                    w.append(("VE", rope_ve[gidx - 2]))
                for ic in range(IC):
                    pk = E("tensor", (lambda e, ic=ic, tt=tt, par=par,
                                      wpar=wpar: e.matmul(
                                          pb[4 + par][:], wbr[wpar][:, ic, :],
                                          Br[:, ic, 512 * tt:512 * tt + 512],
                                          start=(ic == 0), stop=(ic == IC - 1),
                                          skip_group_check=True)),
                           waits=w if ic == 0 else (),
                           inc="PE" if ic == IC - 1 else None)
                sqw = [("PE", pk)]
                if norm_pe:
                    sqw.append(("PE", norm_pe[-1]))
                se = E("scalar", (lambda e, par=par: e.activation(
                    bscr[:, 0:512], pb[4 + par][:], AF.Square)),
                    waits=sqw, inc="SE")
                nmw = [("SE", se)]
                if norm_ve:
                    nmw.append(("VE", norm_ve[-1]))
                pn = E("tensor", (lambda e: e.matmul(
                    nrm, ones, bscr[:, 0:512], start=True, stop=True,
                    skip_group_check=True)), waits=nmw, inc="PE")
                norm_pe.append(pn)
                tgtw = qmaxw if is_q else kmaxw
                nv = E("vector", (lambda e, tgtw=tgtw: e.tensor_tensor(
                    tgtw[:], tgtw[:], nrm, ALU.max)),
                    waits=[("PE", pn)], inc="VE")
                norm_ve.append(nv)
                E("vector", (lambda e, par=par, tt=tt: e.tensor_tensor(
                    fscr[:, 0:512], pb[4 + par][:],
                    cos[:, 512 * tt:512 * tt + 512], ALU.mult)),
                    waits=[("SE", se)])
                E("vector", (lambda e, par=par, tt=tt: e.tensor_tensor(
                    fscr[0:64, 512:1024], pb[4 + par][64:128, :],
                    sin[0:64, 512 * tt:512 * tt + 512], ALU.mult)))
                E("vector", (lambda e, par=par, tt=tt: e.tensor_tensor(
                    fscr[64:128, 512:1024], pb[4 + par][0:64, :],
                    sin[64:128, 512 * tt:512 * tt + 512], ALU.mult)))
                rv = E("vector", (lambda e, dst_ap=dst_ap: e.tensor_tensor(
                    dst_ap, fscr[:, 0:512], fscr[:, 512:1024], ALU.add)),
                    inc="VE")
                rope_ve.append(rv)
                return rv

            kwv = {}
            for kc in range(2):
                kwv[kc] = dma("gpsimd", wsem[kc % 2], wbr[kc % 2],
                              wkT_r[:, :, 128 * kc:128 * kc + 128])
            wst = [poolA[:, 0:16384].rearrange("p (c d) -> p c d", d=512),
                   poolA[:, 16384:32768].rearrange("p (c d) -> p c d", d=512)]
            v_wv = [None, None]
            v_wv[0] = dma("gpsimd", "WV0", wst[0],
                          wvT_r[:, :, 0:512], waits=[("VE", v_ln1)])
            kst_dma = 0
            for kc in range(NKV):
                wpar = kc % 2
                if kc == 4:
                    v_wv[1] = dma("gpsimd", "WV1", wst[1],
                                  wvT_r[:, :, 512:1024],
                                  waits=[("VE", v_ln1)])
                if kc in kwv:
                    v_w = kwv[kc]
                else:
                    v_w = dma("gpsimd", wsem[wpar], wbr[wpar],
                              wkT_r[:, :, 128 * kc:128 * kc + 128],
                              waits=[("PE", wpar_last_use[wpar])])
                for tt in range(2):
                    extra = [("WC", kst_dma)] if kst_dma else []
                    rv = proj_rope_block(tt, tt,
                                         kst[:, 512 * tt:512 * tt + 512],
                                         v_w, wpar, is_q=False,
                                         extra_mm_waits=extra)
                wpar_last_use[wpar] = sch.val("PE")
                kst_dma = dma("gpsimd", "WC",
                              contrib[128 * kc:128 * kc + 128, :],
                              kst[:], waits=[("VE", rv)])

            vkr = E("vector", (lambda e: e.reduce_max(
                kmax[:], kmaxw[:], axis=AXX)), inc="VE")
            se_ks = E("scalar", (lambda e: e.activation(
                stat_s[0:1, 0:1], kmax[0:1, :], AF.Copy)),
                waits=[("VE", vkr)], inc="SE")
            dma("gpsimd", "WC", contrib[2048:2049, 0:16], stat_s[:],
                waits=[("SE", se_ks)])

            wqT_r = wqT.rearrange("(c p) o -> p c o", c=IC)
            qwv = {}
            for oc in range(2):
                qwv[oc] = dma("gpsimd", wsem[oc % 2], wbr[oc % 2],
                              wqT_r[:, :, 128 * oc:128 * oc + 128],
                              waits=[("PE", wpar_last_use[oc % 2])])

            vstage = [ptb[1][:, 0:512], ptb[1][:, 512:1024],
                      ptb[2][:, 0:512], ptb[2][:, 512:1024]]
            pe_v = {}
            vev = {}
            v_contribs = []
            for vh in range(2):
                for ts in range(8):
                    g = vh * 8 + ts
                    par = g % 2
                    w = [(f"WV{vh}", v_wv[vh]), ("VE", v_ln1)]
                    if g >= 4:
                        w.append(("VE", vev[g - 4]))
                    for ic in range(IC):
                        pv_ = E("tensor", (lambda e, ic=ic, ts=ts, par=par,
                                           vh=vh: e.matmul(
                                               pb[par][:],
                                               Br[:, ic,
                                                  128 * ts:128 * ts + 128],
                                               wst[vh][:, ic, :],
                                               start=(ic == 0),
                                               stop=(ic == IC - 1),
                                               skip_group_check=True)),
                                waits=w if ic == 0 else (),
                                inc="PE" if ic == IC - 1 else None)
                    pe_v[(vh, ts)] = pv_
                    evw = [("PE", pv_)]
                    if g >= 4:
                        evw.append(("WC", v_contribs[g - 4]))
                    vev[g] = E("vector", (lambda e, par=par, g=g:
                                          e.tensor_copy(
                                              vstage[g % 4], pb[par][:])),
                               waits=evw, inc="VE")
                    v_contribs.append(dma(
                        "gpsimd", "WC",
                        contrib[1024 + 128 * ts:1024 + 128 * ts + 128,
                                512 * vh:512 * vh + 512],
                        vstage[g % 4], waits=[("VE", vev[g])]))

            if dbg:
                dma("sync", "DBG", dbg_t["dbg_contrib"], contrib[:, :],
                    waits=[("WC", sch.val("WC"))])
            if upto < 4:
                raise _Stop

            E("gpsimd", (lambda e: e.collective_compute(
                "AllGather", ALU.bypass, replica_groups=[list(range(NCORES))],
                ins=[contrib.ap().opt()], outs=[gathered.ap().opt()])),
                waits=[("WC", sch.val("WC"))], inc="CC")

            for oc in range(NH):
                wpar = oc % 2
                if oc in qwv:
                    v_w = qwv[oc]
                else:
                    v_w = dma("gpsimd", wsem[wpar], wbr[wpar],
                              wqT_r[:, :, 128 * oc:128 * oc + 128],
                              waits=[("PE", wpar_last_use[wpar])])
                for tt in range(2):
                    rv = proj_rope_block(tt, tt,
                                         Ar[:, oc, 512 * tt:512 * tt + 512],
                                         v_w, wpar, is_q=True)
                wpar_last_use[wpar] = sch.val("PE")
            v_q = rope_ve[-1]

            if dbg:
                dma("sync", "DBG", dbg_t["dbg_q"], poolA[:],
                    waits=[("VE", v_q)])

            vqr = E("vector", (lambda e: e.reduce_max(
                qmax[:], qmaxw[:], axis=AXX)), inc="VE")
            v_stg = dma("sync", "SG",
                        statg[:].rearrange("p (r h) -> p r h", r=8),
                        gathered.ap().rearrange(
                            "(r q) j -> r q j",
                            r=NCORES)[:, 2048, 0:8].unsqueeze(0),
                        waits=[("CC", 1)])
            E("vector", (lambda e: e.tensor_copy(statf[:], statg[:])),
                waits=[("SG", v_stg)])
            v_kg0 = E("vector", (lambda e: e.reduce_max(
                stat2[:, 0:1], statf[:], axis=AXX)), inc="VE")
            se_kg = E("scalar", (lambda e: e.activation(
                stat_s[0:1, 1:2], stat2[0:1, 0:1], AF.Copy)),
                waits=[("VE", v_kg0)], inc="SE")
            pbc = E("tensor", (lambda e: e.matmul(
                bslot[:, 0:1], onesrow, stat_s[:, 1:2], start=True, stop=True,
                skip_group_check=True)),
                waits=[("SE", se_kg), ("VE", norm_ve[-1])], inc="PE")
            vkmb = E("vector", (lambda e: e.tensor_copy(kmb[:],
                                                         bslot[:, 0:1])),
                     waits=[("PE", pbc)], inc="VE")
            po1 = E("gpsimd", (lambda e: e.tensor_tensor(
                negB[:], qmax[:], kmb[:], ALU.mult)),
                waits=[("VE", max(vqr, vkmb))], inc="PO")
            E("scalar", (lambda e: e.activation(
                kred[:], negB[:], AF.Ln)), waits=[("PO", po1)])
            v_b2 = E("scalar", (lambda e: e.activation(
                negB[:], kred[:], AF.Exp, scale=0.5, bias=lnc_t[:, 0:1])),
                inc="SE")
            v_negB = E("vector", (lambda e: e.tensor_scalar(
                negB[:], negB[:], 60.0, -1.0, ALU.min, ALU.mult)),
                waits=[("SE", v_b2)], inc="VE")
            if dbg:
                dbgst = sb("dbgst", [128, 8], F32)
                E("vector", lambda e: e.tensor_copy(dbgst[:, 0:1], negB[:]),
                  waits=[("VE", v_negB)])
                E("vector", lambda e: e.tensor_copy(dbgst[:, 1:2], qmax[:]))
                E("vector", lambda e: e.tensor_copy(dbgst[:, 2:3], kmax[:]))
                E("vector", lambda e: e.tensor_copy(dbgst[:, 3:4], kmb[:]))
                vdb = E("vector", lambda e: e.tensor_copy(
                    dbgst[0:1, 4:8], statf[:, 0:4]), inc="VE")
                dma("sync", "DBG", dbg_t["dbg_negB"], dbgst[:],
                    waits=[("VE", vdb)])
            if upto < 5:
                raise _Stop

            gat = gathered.ap().rearrange("(r q) j -> r q j", r=NCORES)
            Aq = poolA[:].rearrange("p (g t) -> p g t", g=NH)
            kb_r = [k[:].rearrange("p (r j) -> p r j", r=8) for k in kbuf]
            vb_r = [v[:].rearrange("p (r c) -> p r c", r=8) for v in vbuf]
            kksem = ["KK0", "KK1"]
            vvsem = ["VV0", "VV1"]
            obank = (pb[2], pb[3])

            pairs = []
            for t in range(QT):
                for kvh in range(NKV):
                    u = t * NKV + kvh
                    n = 4 * (t + 1)
                    for pi in range(n):
                        jb, h = pi // 4, pi % 4
                        pairs.append(dict(t=t, kvh=kvh, u=u, jb=jb,
                                          r0=2 * h, r1=2 * h + 1,
                                          first=(pi == 0), last=(pi == n - 1)))
            NPAIR = len(pairs)
            st_pe = [0] * NPAIR
            exp_seL = [0] * NPAIR
            av_peL = [0] * NPAIR
            kv_done = {}
            nload = [0]
            last_st_on_kpar = {0: 0, 1: 0}
            last_av_on_kpar = {0: 0, 1: 0}
            last_ev = [0]

            def kv_load(p):
                pr = pairs[p]
                key = (pr["u"], pr["jb"])
                if key in kv_done:
                    return kv_done[key]
                t, kvh, jb = pr["t"], pr["kvh"], pr["jb"]
                kpar = nload[0] % 2
                nload[0] += 1
                ksrc = gat[:, kvh * 128:kvh * 128 + 128,
                           jb * 128:jb * 128 + 128].transpose((1, 0, 2))
                vsrc = gat[:, 1024 + jb * 128:1024 + jb * 128 + 128,
                           kvh * 128:kvh * 128 + 128].transpose((1, 0, 2))
                v_k = dma("sync", kksem[kpar], kb_r[kpar], ksrc,
                          waits=[("PE", last_st_on_kpar[kpar]), ("CC", 1)])
                vw = [("PE", last_av_on_kpar[kpar])]
                if nload[0] <= 2:
                    vw.append(("VE", v_memset))
                v_v = dma("sync", vvsem[kpar], vb_r[kpar][:, :, 0:128], vsrc,
                          waits=vw)
                kv_done[key] = (kpar, v_k, v_v)
                return kv_done[key]

            def emit_stgroup(p):
                pr = pairs[p]
                sp = p % 3
                kpar, v_k, v_v = kv_load(p)
                pr["kpar"], pr["v_v"] = kpar, v_v
                stw = [(kksem[kpar], v_k), ("VE", v_q)]
                if p < 3:
                    stw.append(("VE", v_negB))
                if p >= 3:
                    stw.append(("SE", exp_seL[p - 3]))
                qrhs = Aq[:, 4 * pr["kvh"]:4 * pr["kvh"] + 4,
                          128 * pr["t"]:128 * pr["t"] + 128]
                diag = (pr["jb"] == pr["t"])
                vst = 0
                for h, r in ((0, pr["r0"]), (1, pr["r1"])):
                    lastmm = (h == 1) and not diag
                    vst = E("tensor", (lambda e, kpar=kpar, r=r, sp=sp, h=h,
                                       qrhs=qrhs, diag=diag: e.matmul(
                                           stq[sp][:, 512 * h:512 * h + 512],
                                           kb_r[kpar][:, r, :], qrhs,
                                           start=True, stop=not diag,
                                           skip_group_check=True)),
                            waits=stw if h == 0 else (),
                            inc="PE" if lastmm else None)
                    if diag:
                        vst = E("tensor", (lambda e, sp=sp, h=h, r=r: e.matmul(
                            stq[sp][:, 512 * h:512 * h + 512], eye,
                            masks_s[:, 512 * r:512 * r + 512],
                            start=False, stop=True, skip_group_check=True)),
                            inc="PE" if h == 1 else None)
                last_st_on_kpar[kpar] = vst
                st_pe[p] = vst

            def emit_exp(p):
                sp = p % 3
                ew = [("PE", st_pe[p]), ("VE", v_negB)]
                if p >= 3:
                    ew.append(("PE", av_peL[p - 3]))
                exp_seL[p] = E("scalar", (lambda e, sp=sp: e.activation(
                    ptb[sp][:], stq[sp][:], AF.Exp,
                    bias=negB[:, 0:1], scale=SCL)), waits=ew, inc="SE")

            def emit_avgroup(p):
                pr = pairs[p]
                sp = p % 3
                ob = obank
                kpar, v_v = pr["kpar"], pr["v_v"]
                avw = [("SE", exp_seL[p]), (vvsem[kpar], v_v)]
                if pr["first"]:
                    avw.append(("VE", last_ev[0]))
                vav = 0
                for h, r in ((0, pr["r0"]), (1, pr["r1"])):
                    for g in range(4):
                        st_flag = pr["first"] and h == 0 and (g % 2 == 0)
                        vav = E("tensor", (lambda e, sp=sp, h=h, r=r, g=g,
                                           ob=ob, kpar=kpar, st_flag=st_flag:
                                           e.matmul(
                                               ob[g // 2][:, 129 * (g % 2):
                                                          129 * (g % 2) + 129],
                                               ptb[sp][:, 512 * h + 128 * g:
                                                       512 * h + 128 * g + 128],
                                               vb_r[kpar][:, r, 0:129],
                                               start=st_flag, stop=False,
                                               skip_group_check=True)),
                                waits=avw if (h == 0 and g == 0) else (),
                                inc="PE" if (h == 1 and g == 3) else None)
                av_peL[p] = vav
                last_av_on_kpar[kpar] = vav
                if pr["last"]:
                    emit_evac(pr, vav)

            def emit_evac(pr, v_av_all):
                ob = obank
                t, kvh = pr["t"], pr["kvh"]
                E("scalar", (lambda e, ob=ob: e.activation(
                    rl[:, 0:2], ob[0][:, 128:259:129], AF.Ln)),
                    waits=[("PE", v_av_all)])
                E("scalar", (lambda e, ob=ob: e.activation(
                    rl[:, 2:4], ob[1][:, 128:259:129], AF.Ln)))
                v_l = E("scalar", (lambda e: e.activation(
                    rl[:], rl[:], AF.Exp, scale=-1.0)), inc="SE")
                vsc = 0
                for g in range(4):
                    gg = 4 * kvh + g
                    vsc = E("vector", (lambda e, ob=ob, g=g, gg=gg, t=t:
                                       e.tensor_scalar(
                                           Aq[:, gg, 128 * t:128 * t + 128],
                                           ob[g // 2][:, 129 * (g % 2):
                                                      129 * (g % 2) + 128],
                                           rl[:, g:g + 1], None, ALU.mult)),
                            waits=[("SE", v_l)] if g == 0 else (), inc="VE")
                last_ev[0] = vsc

            for pp in range(min(3, NPAIR)):
                emit_stgroup(pp)
            for p in range(NPAIR):
                emit_exp(p)
                if p + 3 < NPAIR:
                    emit_stgroup(p + 3)
                emit_avgroup(p)
            v_attn = last_ev[0]

            trbit = stp1[:].bitcast(BF16)
            tslots = [trbit[:, 0:128], trbit[:, 1024:1152]]
            tr_ve = []
            chv = {}
            for g in range(NH):
                for t in range(QT):
                    idx = len(tr_ve)
                    tw = []
                    if idx == 0:
                        tw.append(("VE", v_attn))
                    if idx >= 2:
                        tw.append(("VE", tr_ve[idx - 2]))
                    ptv = E("tensor", (lambda e, g=g, t=t, idx=idx:
                                       e.transpose(
                                           tslots[idx % 2],
                                           Aq[:, g, 128 * t:128 * t + 128],
                                           eye)), waits=tw, inc="PE")
                    tv = E("vector", (lambda e, g=g, t=t, idx=idx:
                                      e.tensor_copy(
                                          Br[:, g, 128 * t:128 * t + 128],
                                          tslots[idx % 2])),
                           waits=[("PE", ptv)], inc="VE")
                    tr_ve.append(tv)
                chv[g] = tr_ve[-1]
            v_attn = tr_ve[-1]

            if dbg:
                dma("sync", "DBG", dbg_t["dbg_attnT"], poolB[:],
                    waits=[("VE", v_attn)])
            if upto < 6:
                raise _Stop

            woT_r = woT.rearrange("(c p) o -> p c o", c=IC)
            xT_rr = xT.rearrange("(c p) t -> c p t", c=IC)
            xrel = [scr8[:, 0:512], scr8[:, 512:1024]]
            xrsem = ["XR0", "XR1"]
            x1f = fscr[:, 0:512]
            x1f_ve = []
            x1_st = []
            wpar_lu = dict(wpar_last_use)
            v_ln2pe = 0
            for oc in range(IC):
                wpar = oc % 2
                v_w = dma("gpsimd", wsem[wpar], wbr[wpar],
                          woT_r[:, :, 128 * oc:128 * oc + 128],
                          waits=[("PE", wpar_lu[wpar])])
                for tt in range(2):
                    g = oc * 2 + tt
                    par = tt
                    xpar = g % 2
                    xw = []
                    if g >= 2:
                        xw.append(("VE", x1f_ve[g - 2]))
                    v_x = dma("sync", xrsem[xpar], xrel[xpar],
                              xT_rr[oc, :, 512 * tt:512 * tt + 512], waits=xw)
                    mw = [(wsem[wpar], v_w)]
                    if g >= 2:
                        mw.append(("VE", x1f_ve[g - 2]))
                    for ic in range(IC):
                        icw = list(mw) if ic == 0 else []
                        icw.append(("VE", chv[ic]))
                        v_pe = E("tensor", (lambda e, ic=ic, tt=tt, par=par,
                                            wpar=wpar: e.matmul(
                                                pb[par][:],
                                                wbr[wpar][:, ic, :],
                                                Br[:, ic,
                                                   512 * tt:512 * tt + 512],
                                                start=(ic == 0),
                                                stop=(ic == IC - 1),
                                                skip_group_check=True)),
                                 waits=icw,
                                 inc="PE" if ic == IC - 1 else None)
                    fw = [("PE", v_pe), (xrsem[xpar], v_x)]
                    if x1_st:
                        fw.append(("ST1", x1_st[-1]))
                    E("vector", (lambda e, par=par, xpar=xpar:
                                 e.tensor_tensor(
                                     x1f, pb[par][:], xrel[xpar], ALU.add)),
                        waits=fw)
                    vx1 = E("vector", lambda e: e.tensor_copy(
                        fscr[:, 512:1024], x1f), inc="VE")
                    vst = dma("sync", "ST1",
                              x1d[128 * oc:128 * oc + 128,
                                  512 * tt:512 * tt + 512],
                              fscr[:, 512:1024], waits=[("VE", vx1)])
                    x1_st.append(vst)
                    E("vector", (lambda e, oc=oc, tt=tt: e.tensor_copy(
                        Ar[:, oc, 512 * tt:512 * tt + 512], x1f)))
                    vsq = E("vector", (lambda e, oc=oc, tt=tt, xpar=xpar:
                                       e.tensor_tensor(
                                           xsq[xpar],
                                           Ar[:, oc,
                                              512 * tt:512 * tt + 512],
                                           Ar[:, oc,
                                              512 * tt:512 * tt + 512],
                                           ALU.mult)), inc="VE")
                    x1f_ve.append(vsq)
                    E("tensor", (lambda e, tt=tt, oc=oc: e.matmul(
                        pb[2 + tt][:], ones,
                        Ar[:, oc, 512 * tt:512 * tt + 512],
                        start=(oc == 0), stop=(oc == IC - 1),
                        skip_group_check=True)), waits=[("VE", vsq)])
                    v_ln2pe = E("tensor", (lambda e, tt=tt, oc=oc, xpar=xpar:
                                           e.matmul(
                                               pb[4 + tt][:], ones,
                                               xsq[xpar],
                                               start=(oc == 0),
                                               stop=(oc == IC - 1),
                                               skip_group_check=True)),
                                inc="PE")
                wpar_lu[wpar] = sch.val("PE")
            if dbg:
                dma("sync", "DBG", dbg_t["dbg_x1"], x1d[:, :],
                    waits=[("ST1", x1_st[-1])])
            if upto < 7:
                raise _Stop

            v_ln2 = 0
            ln2_chunks = {}
            for tt in range(2):
                v_ln2, ln2_chunks[tt] = ln_finalize_and_apply(
                    tt, 2 + tt, 4 + tt, Ar, Br, v_ln2pe)
            if dbg:
                dma("sync", "DBG", dbg_t["dbg_ln2"], poolB[:],
                    waits=[("VE", v_ln2)])
            if upto < 8:
                raise _Stop

            u1pad = upads[:, 0:1024]
            u3pad = upads[:, 1024:2048]
            u2pad = upads[:, 2048:3072]
            a1r = a1s[:].rearrange("p (c r) -> p c r", c=IC)
            a3r = a3s[:].rearrange("p (c r) -> p c r", c=IC)
            v_use = 0
            for tt in range(2):
                for (ar, upad) in ((a1r, u1pad), (a3r, u3pad)):
                    for ic in range(IC):
                        w = [("VE", ln2_chunks[tt][ic]), ("W0", W0F)]
                        if v_use:
                            w.append(("SE", v_use))
                        pu = E("tensor", (lambda e, ic=ic, tt=tt, ar=ar:
                                          e.matmul(
                                              nrm[0:2, :], ar[:, ic, :],
                                              Br[:, ic,
                                                 512 * tt:512 * tt + 512],
                                              start=(ic == 0),
                                              stop=(ic == IC - 1),
                                              skip_group_check=True)),
                               waits=w,
                               inc="PE" if ic == IC - 1 else None)
                    v_use = E("scalar", (lambda e, tt=tt, upad=upad:
                                         e.activation(
                                             upad[0:2,
                                                  512 * tt:512 * tt + 512],
                                             nrm[0:2, :], AF.Copy,
                                             scale=0.5)),
                              waits=[("PE", pu)], inc="SE")
            hs = poolA[:, 0:8192].rearrange("p (c t) -> p c t", c=8)
            v_hs = 0
            for j in range(8):
                for tt in range(2):
                    E("vector", (lambda e, j=j, tt=tt: e.tensor_tensor(
                        fscr[:, 0:512], Br[:, j, 512 * tt:512 * tt + 512],
                        Br[:, j + 8, 512 * tt:512 * tt + 512], ALU.add)),
                        waits=[("VE", v_ln2)])
                    E("vector", (lambda e, j=j, tt=tt: e.tensor_tensor(
                        fscr[:, 512:1024],
                        Br[:, j + 16, 512 * tt:512 * tt + 512],
                        Br[:, j + 24, 512 * tt:512 * tt + 512], ALU.add)))
                    v_hs = E("vector", (lambda e, j=j, tt=tt:
                                        e.tensor_tensor(
                                            hs[:, j,
                                               512 * tt:512 * tt + 512],
                                            fscr[:, 0:512],
                                            fscr[:, 512:1024], ALU.add)),
                             inc="VE")

            w1T_r = w1T.rearrange("(c p) h -> p c h", c=9)
            w3T_r = w3T.rearrange("(c p) h -> p c h", c=9)
            gsv = poolB[:, 0:FC * 1024].rearrange("p (c t) -> p c t", c=FC)
            gcs = [[bscr[:, 512 * (2 * f + tt):512 * (2 * f + tt) + 512]
                    for tt in range(2)] for f in range(4)]
            sg = ptb[0][:, 0:512]
            a2r = a2s[:].rearrange("p (c r) -> p c r", c=HC)
            w1b = [wbufU[:, off:off + 1152].rearrange("p (c h) -> p c h", c=9)
                   for off in (0, 2304, 4608)]
            w3b = [wbufU[:, off + 1152:off + 2304]
                   .rearrange("p (c h) -> p c h", c=9)
                   for off in (0, 2304, 4608)]
            silu_se = []
            gc_ve = []
            u2_pe = {}
            pmax = max(wpar_lu[0], wpar_lu[1])
            wpar_last = {0: pmax, 1: pmax, 2: pmax}
            wsem3 = ["WA", "WB", "WF"]
            u2_started = {0: False, 1: False}
            v_gs = 0
            for p in range(FC):
                for f in range(4):
                    hc = 22 * f + p
                    wpar = (4 * p + f) % 3
                    dma("gpsimd", wsem3[wpar], w1b[wpar],
                        w1T_r[:, :, 128 * hc:128 * hc + 128],
                        waits=[("PE", wpar_last[wpar])])
                    v_w3 = dma("gpsimd", wsem3[wpar], w3b[wpar],
                               w3T_r[:, :, 128 * hc:128 * hc + 128])
                    for tt in range(2):
                        i_gu = len(silu_se)
                        gbank = [pb[0], nrm][i_gu % 2]
                        ubank = [pb[1], bslot][i_gu % 2]
                        sgb = [sg, ptb[0][:, 512:1024]][i_gu % 2]
                        mw = [(wsem3[wpar], v_w3), ("VE", v_hs)]
                        if v_use:
                            mw.append(("SE", v_use))
                        if i_gu >= 2:
                            mw.append(("SE", silu_se[i_gu - 2]))
                        for ic in range(9):
                            rhs = (hs[:, ic, 512 * tt:512 * tt + 512]
                                   if ic < 8
                                   else u1pad[:, 512 * tt:512 * tt + 512])
                            E("tensor", (lambda e, ic=ic, wpar=wpar, rhs=rhs,
                                         gbank=gbank: e.matmul(
                                             gbank, w1b[wpar][:, ic, :],
                                             rhs, start=(ic == 0),
                                             stop=(ic == 8),
                                             skip_group_check=True)),
                                waits=mw if ic == 0 else ())
                        uw = [("VE", gc_ve[i_gu - 2])] if i_gu >= 2 else []
                        for ic in range(9):
                            rhs = (hs[:, ic, 512 * tt:512 * tt + 512]
                                   if ic < 8
                                   else u3pad[:, 512 * tt:512 * tt + 512])
                            v_gu = E("tensor", (lambda e, ic=ic, wpar=wpar,
                                                rhs=rhs, ubank=ubank:
                                                e.matmul(
                                                    ubank,
                                                    w3b[wpar][:, ic, :],
                                                    rhs, start=(ic == 0),
                                                    stop=(ic == 8),
                                                    skip_group_check=True)),
                                     waits=uw if ic == 0 else (),
                                     inc="PE" if ic == 8 else None)
                        sse = E("scalar", (lambda e, gbank=gbank, sgb=sgb:
                                           e.activation(
                                               sgb, gbank, AF.Silu)),
                            waits=[("PE", v_gu)], inc="SE")
                        silu_se.append(sse)
                        gw = [("SE", sse)]
                        if (f, tt) in u2_pe:
                            gw.append(("PE", u2_pe[(f, tt)]))
                        vgc = E("vector", (lambda e, f=f, tt=tt, sgb=sgb,
                                           ubank=ubank: e.tensor_tensor(
                                               gcs[f][tt], sgb, ubank,
                                               ALU.mult)),
                                waits=gw, inc="VE")
                        gc_ve.append(vgc)
                        u2w = [("VE", vgc), ("W0", W0F)]
                        if not u2_started[tt]:
                            u2w.append(("VE", v_ln2))
                        pu2 = E("tensor", (lambda e, hc=hc, f=f, tt=tt,
                                           s=not u2_started[tt]: e.matmul(
                                               pb[2 + tt][0:2, :],
                                               a2r[:, hc, :], gcs[f][tt],
                                               start=s,
                                               stop=(hc == HC - 1),
                                               skip_group_check=True)),
                                 waits=u2w, inc="PE")
                        u2_started[tt] = True
                        u2_pe[(f, tt)] = pu2
                    wpar_last[wpar] = sch.val("PE")
                for tt in range(2):
                    E("vector", (lambda e, tt=tt: e.tensor_tensor(
                        fscr[:, 0:512], gcs[0][tt], gcs[1][tt], ALU.add)),
                        waits=[("PE", u2_pe[(3, tt)])])
                    E("vector", (lambda e, tt=tt: e.tensor_tensor(
                        fscr[:, 512:1024], gcs[2][tt], gcs[3][tt], ALU.add)))
                    v_gs = E("vector", (lambda e, p=p, tt=tt:
                                        e.tensor_tensor(
                                            gsv[:, p,
                                                512 * tt:512 * tt + 512],
                                            fscr[:, 0:512],
                                            fscr[:, 512:1024], ALU.add)),
                             inc="VE")
            E("scalar", (lambda e: e.activation(
                u2pad[0:2, 0:512], pb[2][0:2, :], AF.Copy, scale=0.5)),
                waits=[("PE", u2_pe[(3, 0)])])
            v_u2b = E("scalar", (lambda e: e.activation(
                u2pad[0:2, 512:1024], pb[3][0:2, :], AF.Copy, scale=0.5)),
                waits=[("PE", u2_pe[(3, 1)])], inc="SE")
            if dbg:
                dma("sync", "DBG", dbg_t["dbg_gs"], poolB[:, 0:FC * 1024],
                    waits=[("VE", v_gs)])

            w2T_r = w2T.rearrange("(c p) o -> p c o", c=23)

            x2f = fscr[:, 512:1024]
            y_ve = []
            y_st = []
            st1_full = x1_st[-1]
            wpar_last = {0: sch.val("PE"), 1: sch.val("PE")}
            w2b = [wbufU[:, 0:2944].rearrange("p (c o) -> p c o", c=23),
                   wbufU[:, 2944:5888].rearrange("p (c o) -> p c o", c=23)]
            for oc in range(IC):
                wpar = oc % 2
                v_w = dma("gpsimd", wsem[wpar], w2b[wpar],
                          w2T_r[:, :, 128 * oc:128 * oc + 128],
                          waits=[("PE", wpar_last[wpar])])
                for tt in range(2):
                    g = oc * 2 + tt
                    par = tt
                    xpar = g % 2
                    xw = [("ST1", st1_full)]
                    if g >= 2:
                        xw.append(("VE", y_ve[g - 2]))
                    v_x = dma("sync", xrsem[xpar], xrel[xpar],
                              x1d[128 * oc:128 * oc + 128,
                                  512 * tt:512 * tt + 512], waits=xw)
                    mw = [(wsem[wpar], v_w), ("VE", v_gs), ("SE", v_u2b)]
                    if g >= 2:
                        mw.append(("VE", y_ve[g - 2]))
                    for ic in range(23):
                        rhs = (gsv[:, ic, 512 * tt:512 * tt + 512]
                               if ic < 22
                               else u2pad[:, 512 * tt:512 * tt + 512])
                        v_pe = E("tensor", (lambda e, ic=ic, par=par,
                                            wpar=wpar, rhs=rhs: e.matmul(
                                                pb[par][:],
                                                w2b[wpar][:, ic, :], rhs,
                                                start=(ic == 0),
                                                stop=(ic == 22),
                                                skip_group_check=True)),
                                 waits=mw if ic == 0 else (),
                                 inc="PE" if ic == 22 else None)
                    fw = [("PE", v_pe), (xrsem[xpar], v_x)]
                    if y_st:
                        fw.append(("ST2", y_st[-1]))
                    vy = E("vector", (lambda e, par=par, xpar=xpar:
                                      e.tensor_tensor(
                                          x2f, pb[par][:], xrel[xpar],
                                          ALU.add)), waits=fw, inc="VE")
                    y_ve.append(vy)
                    vst = dma("sync", "ST2",
                              out[128 * oc:128 * oc + 128,
                                  512 * tt:512 * tt + 512],
                              x2f, waits=[("VE", vy)])
                    y_st.append(vst)
                wpar_last[wpar] = sch.val("PE")

        try:
            _sched_all()
        except _Stop:
            pass

        all_dma_sems = [s for s in sch.ctr
                        if s not in ("PE", "VE", "SE", "PO", "CC")]
        E("gpsimd", lambda e: None,
          waits=[(s, sch.val(s)) for s in all_dma_sems])

        semnames = sorted({s for ops in sch.ops.values()
                           for (ws, _, inc, _) in ops
                           for s in ([w[0] for w in ws] +
                                     ([inc] if inc else []))})
        sems = {s: ctx.enter_context(nc.semaphore(s)) for s in semnames}
        block = ctx.enter_context(nc.Block())

        def mk(name):
            def f(eng):
                seen = {}
                for waits, fn, inc, inc_n in sch.ops[name]:
                    for (s, v) in waits:
                        if v > 0 and seen.get(s, 0) < v:
                            eng.wait_ge(sems[s], v)
                            seen[s] = v
                    instr = fn(eng)
                    if inc is not None:
                        instr.then_inc(sems[inc], inc_n)
            return f

        block.sync(mk("sync"))
        block.gpsimd(mk("gpsimd"))
        block.tensor(mk("tensor"))
        block.vector(mk("vector"))
        block.scalar(mk("scalar"))

    return nc



def _stage(inputs):
    x = np.asarray(inputs["L0_attn_hidden_states"])[0]
    Wq = np.asarray(inputs["L0_attn_Wq"])
    Wk = np.asarray(inputs["L0_attn_Wk"])
    Wv = np.asarray(inputs["L0_attn_Wv"])
    Wo = np.asarray(inputs["L0_attn_Wo"])
    w1q = np.asarray(inputs["L0_ffn_w1q"])
    a1 = np.asarray(inputs["L0_ffn_a1"])
    b1 = np.asarray(inputs["L0_ffn_b1"])
    w3q = np.asarray(inputs["L0_ffn_w3q"])
    a3 = np.asarray(inputs["L0_ffn_a3"])
    b3 = np.asarray(inputs["L0_ffn_b3"])
    w2q = np.asarray(inputs["L0_ffn_w2q"])
    a2 = np.asarray(inputs["L0_ffn_a2"])
    b2 = np.asarray(inputs["L0_ffn_b2"])

    C = np.ascontiguousarray
    f32 = np.float32

    wqT = C(Wq.T.astype(f32))
    wkT = C(Wk.T.astype(f32))
    wvT = C(Wv.T.astype(f32))
    woT = C(Wo.T.astype(f32))

    def pad_gate(wT, bT):
        outw = np.zeros((1152, HIDP), f32)
        for f in range(4):
            outw[0:1024, 2816 * f:2816 * f + 2752] = \
                wT[:, 2752 * f:2752 * f + 2752]
            outw[1024:1026, 2816 * f:2816 * f + 2752] = \
                bT[:, 2752 * f:2752 * f + 2752]
        return outw

    w1T = pad_gate(C(w1q.T.astype(f32)), C(b1.T.astype(f32)))
    w3T = pad_gate(C(w3q.T.astype(f32)), C(b3.T.astype(f32)))

    w2T = np.zeros((W2K + 128, D), f32)
    w2T[0:2752, :] = w2q.T.astype(f32)
    w2T[W2K:W2K + 2, :] = b2.T.astype(f32)

    def a_small(a, nch):
        acx = a.T.astype(f32).reshape(nch, 128, a.shape[0])
        return C(acx.transpose(1, 0, 2).reshape(128, nch * a.shape[0]))

    a1cv = a_small(a1, IC)
    a3cv = a_small(a3, IC)
    a2p = np.zeros((2, HIDP), f32)
    for f in range(4):
        a2p[:, 2816 * f:2816 * f + 2752] = a2[:, 2752 * f:2752 * f + 2752]
    a2cv = a_small(a2p, HC)

    inv = (1.0 / (10000.0 ** (np.arange(0, HD, 2, dtype=np.float64) / HD)))
    bf = ml_dtypes.bfloat16

    eye_ones = np.zeros((128, 256), f32)
    eye_ones[:, 0:128] = np.eye(128)
    eye_ones[:, 128:256] = 1.0
    consts_v = eye_ones.astype(bf)

    in_maps = []
    for c in range(NCORES):
        xc = C(x[c::8, :].T.astype(f32))
        pos = (c + 8 * np.arange(R)).astype(np.float64)
        fr = np.outer(pos, inv)
        emb = np.concatenate([fr, fr], axis=1)
        cosv = C(np.cos(emb).T.astype(f32))
        sinv = np.sin(emb).T.astype(f32)
        sins = sinv.copy()
        sins[0:64, :] = -sinv[0:64, :]
        sins = C(sins)

        jj = np.arange(128)[:, None]
        mmi = np.arange(128)[None, :]
        allow = (jj < mmi)
        mrow = []
        for rr in range(8):
            al = allow | ((jj == mmi) & (rr <= c))
            mk = np.where(al, 0.0, -1e9).astype(f32)
            mrow.append(np.tile(mk, (1, 4)))
        maskv = np.concatenate(mrow, axis=1).astype(bf)

        in_maps.append({
            "xT": xc, "wqT": wqT, "wkT": wkT, "wvT": wvT, "woT": woT,
            "w1T": w1T, "w3T": w3T, "w2T": w2T,
            "a1c": a1cv, "a3c": a3cv, "a2c": a2cv,
            "cosq": cosv, "sinq": sins,
            "masks": maskv, "consts": consts_v,
        })
    return in_maps


def kernel(**inputs):
    in_maps = _stage(inputs)
    nc = build(dbg=False)
    res = run_bass_kernel_spmd(nc, in_maps, core_ids=list(range(NCORES)))
    full = np.empty((1, S, D), dtype=np.float32)
    for c in range(NCORES):
        full[0, c::8, :] = np.asarray(res.results[c]["out"],
                                      dtype=np.float32).T
    return full
